# version 01
# baseline (speedup 1.0000x reference)
import numpy as np
import concourse.bass as bass
import concourse.mybir as mybir
from concourse.bass_utils import run_bass_kernel_spmd

F32 = mybir.dt.float32
BF16 = mybir.dt.bfloat16
FP8 = mybir.dt.float8e4
I32 = mybir.dt.int32
ALU = mybir.AluOpType
AF = mybir.ActivationFunctionType
AX = mybir.AxisListType.X

TOK = 2048
S = 4096
D = 2048
II = 8192
NCORE = 8
ENGS = ("tensor", "vector", "scalar", "sync", "gpsimd")

DEBUG_TAPS = []


class PB:

    def __init__(self, nc):
        self.nc = nc
        self.ev = {e: [] for e in ENGS}
        self.sem = {}
        self.cnt = {}
        self.wm = {e: {} for e in ENGS}

    def add_sem(self, name):
        self.sem[name] = self.nc.alloc_semaphore(name)
        self.cnt[name] = 0

    def emit(self, eng, fn, waits=(), inc=None, by=1):
        w2 = []
        for s, v in waits:
            if not v or v <= 0:
                continue
            if v > self.wm[eng].get(s, 0):
                w2.append((s, v))
                self.wm[eng][s] = v
        self.ev[eng].append((fn, w2, inc, by))
        if inc is not None:
            self.cnt[inc] += by
            return self.cnt[inc]
        return None

    def dma(self, eng, sem, out, in_, waits=(), **kw):
        return self.emit(eng, lambda e, o=out, i=in_, kw=kw: e.dma_start(o, i, **kw),
                         waits=waits, inc=sem, by=16)

    def dma_tap(self, out, in_, waits=()):
        w = list(waits) + ([("sTAP", self.cnt["sTAP"])] if self.cnt["sTAP"] else [])
        return self.dma("gpsimd", "sTAP", out, in_, waits=w)

    def fence(self):
        return [(s, c) for s, c in self.cnt.items() if c > 0]

    def build(self, blk):
        for eng in ENGS:
            evs = self.ev[eng]
            if not evs:
                continue
            sec = getattr(blk, eng)

            def body(e, evs=evs):
                for fn, waits, inc, by in evs:
                    for s, v in waits:
                        e.wait_ge(self.sem[s], v)
                    ins = fn(e)
                    if inc is not None:
                        ins.then_inc(self.sem[inc], by)
            sec(body)


class Arena:
    def __init__(self, t, nbytes):
        self.t = t
        self.cap = nbytes
        self.off = 0
        self.marks = []

    def alloc(self, n_elem, dt):
        esz = {F32: 4, BF16: 2, FP8: 1}[dt]
        self.off = (self.off + 63) & ~63
        start = self.off
        self.off += n_elem * esz
        assert self.off <= self.cap, f"arena overflow {self.off} > {self.cap}"
        if dt == F32:
            return self.t[:, start // 4: start // 4 + n_elem]
        return self.t.bitcast(dt)[:, start // esz: start // esz + n_elem]

    def push(self):
        self.marks.append(self.off)

    def pop(self):
        self.off = self.marks.pop()


def build_program():
    nc = bass.Bass("TRN2", target_bir_lowering=False)
    p = PB(nc)
    for s_ in ("sT", "sV", "sS", "sID", "sXTA", "sWO", "sWKV", "sW0", "sW1",
               "sGU0g", "sGU0u", "sGU1g", "sGU1u", "sD0", "sD1", "sXB",
               "sWH", "sWL", "sFIN", "sBO", "sRR", "sRS", "sOUT", "sTAP"):
        p.add_sem(s_)

    dr = lambda n, sh, dt, k: nc.dram_tensor(n, sh, dt, kind=k).ap()
    x_own = dr("x_own", [TOK, D], F32, "ExternalInput")
    x_oth = dr("x_oth", [TOK, D], F32, "ExternalInput")
    wk_h = dr("wk", [D, 128], F32, "ExternalInput")
    wv_h = dr("wv", [D, 128], F32, "ExternalInput")
    bk_h = dr("bk", [128], F32, "ExternalInput")
    bv_h = dr("bv", [128], F32, "ExternalInput")
    wo_h = dr("wo", [D, D], F32, "ExternalInput")
    bo_h = dr("bo", [D], F32, "ExternalInput")
    gn_h = dr("g_nat", [II, D], F32, "ExternalInput")
    un_h = dr("u_nat", [II, D], F32, "ExternalInput")
    dn_h = dr("d_nat", [D, II], F32, "ExternalInput")
    gt_h = dr("g_t", [D, II], F32, "ExternalInput")
    ut_h = dr("u_t", [D, II], F32, "ExternalInput")
    dt_h = dr("d_t", [II, D], F32, "ExternalInput")
    id_h = dr("ident", [128, 128], F32, "ExternalInput")
    out_h = dr("out", [TOK, D], F32, "ExternalOutput")
    sc_g = dr("sc_g", [16, 2, 128, II], BF16, "Internal")
    sc_u = dr("sc_u", [16, 2, 128, II], BF16, "Internal")
    sc_d = dr("sc_d", [64, 2, 128, D], BF16, "Internal")
    taps = {nm: dr("tap_" + nm, sh, dt, "ExternalOutput") for nm, sh, dt in DEBUG_TAPS}

    ident = nc.alloc_sbuf_tensor("ident_s", [128, 128], F32).ap()
    bo_rep = nc.alloc_sbuf_tensor("bo_rep", [128, D], F32).ap()
    woe_h = nc.alloc_sbuf_tensor("woe_h", [128, D], BF16).ap()
    woe_l = nc.alloc_sbuf_tensor("woe_l", [128, D], BF16).ap()
    oT_h = nc.alloc_sbuf_tensor("oT_h", [128, TOK], BF16).ap()
    oT_l = nc.alloc_sbuf_tensor("oT_l", [128, TOK], BF16).ap()
    rinv = nc.alloc_sbuf_tensor("rinv", [128, 16], F32).ap()
    ones_c = nc.alloc_sbuf_tensor("ones_c", [128, 1], BF16).ap()
    c128 = nc.alloc_sbuf_tensor("c128", [128, 1], F32).ap()
    trig = nc.alloc_sbuf_tensor("trig", [128, 2], F32).ap()
    smallv = nc.alloc_sbuf_tensor("smallv", [128, 16], F32).ap()
    AR_B = 162176
    ar = Arena(nc.alloc_sbuf_tensor("arena", [128, AR_B // 4], F32).ap(), AR_B)

    PS = nc.alloc_psum_tensor("PS", [128, 4096], F32).ap()
    bank = lambda i: PS[:, i * 512:(i + 1) * 512]
    bank_free = [[("sT", 0)] for _ in range(8)]

    MMKW = dict(skip_group_check=True)
    st = {}

    p.emit("vector", lambda e: e.memset(ones_c, 1.0))
    p.emit("vector", lambda e: e.memset(c128, 1.0 / 128.0))
    cV = p.emit("vector", lambda e: e.memset(trig, 2.0), inc="sV")
    p.emit("vector", lambda e: e.tensor_copy(trig[:, 1:2].bitcast(I32), trig[:, 0:1].bitcast(I32)),
           waits=[("sV", cV)], inc="sV")
    cID = p.dma("sync", "sID", ident, id_h)
    cG_bo = p.dma("gpsimd", "sBO", bo_rep, bo_h.unsqueeze(0).partition_broadcast(128))
    id_rdy = [("sID", cID)]

    ar.push()
    wo_all = ar.alloc(8 * D, F32)
    wo_v = wo_all.rearrange("p (c m) -> p c m", m=D)
    woe_f = ar.alloc(D, F32)
    wo_r = wo_h.rearrange("(c p) m -> p c m", p=128)
    half_done = []
    wo_gate = []
    for hf in range(2):
        cY = p.dma("sync", "sWO", wo_v, wo_r[:, hf * 8:(hf + 1) * 8, :], waits=wo_gate)
        cV = p.emit("vector", lambda e: e.tensor_add(
            wo_v[:, 0:4, :].rearrange("p a b -> p (a b)"), wo_v[:, 0:4, :].rearrange("p a b -> p (a b)"),
            wo_v[:, 4:8, :].rearrange("p a b -> p (a b)")), waits=[("sWO", cY)], inc="sV")
        cV = p.emit("vector", lambda e: e.tensor_add(
            wo_v[:, 0:2, :].rearrange("p a b -> p (a b)"), wo_v[:, 0:2, :].rearrange("p a b -> p (a b)"),
            wo_v[:, 2:4, :].rearrange("p a b -> p (a b)")), waits=[("sV", cV)], inc="sV")
        if hf == 0:
            cV = p.emit("vector", lambda e: e.tensor_add(woe_f, wo_v[:, 0, :], wo_v[:, 1, :]),
                        waits=[("sV", cV)], inc="sV")
        else:
            cV = p.emit("vector", lambda e: e.tensor_add(wo_v[:, 0, :], wo_v[:, 0, :], wo_v[:, 1, :]),
                        waits=[("sV", cV)], inc="sV")
            cV = p.emit("vector", lambda e: e.tensor_add(woe_f, woe_f, wo_v[:, 0, :]),
                        waits=[("sV", cV)], inc="sV")
        wo_gate = [("sV", cV)]
    cS = p.emit("scalar", lambda e: e.activation(woe_h, woe_f, AF.Identity),
                waits=[("sV", cV)], inc="sS")
    cV = p.emit("vector", lambda e: e.tensor_tensor(woe_l, woe_f, woe_h, op=ALU.subtract),
                waits=[("sS", cS)], inc="sV")
    st["woe"] = [("sV", cV), ("sS", cS)]
    ar.pop()
    ar.push()

    kTvT = ar.alloc(2 * S, F32)
    kT_f, vT_f = kTvT[:, 0:S], kTvT[:, S:2 * S]
    kv_b = ar.alloc(4 * S, BF16)
    kT_hi, kT_lo = kv_b[:, 0:S], kv_b[:, S:2 * S]
    vT_hi, vT_lo = kv_b[:, 2 * S:3 * S], kv_b[:, 3 * S:4 * S]
    ar.push()
    wkv_f = ar.alloc(2 * D, F32)
    wk_f = wkv_f[:, 0:D].rearrange("p (t d) -> p t d", d=128)
    wv_f = wkv_f[:, D:2 * D].rearrange("p (t d) -> p t d", d=128)
    wkv_b = ar.alloc(4 * D, BF16)
    wk_hi = wkv_b[:, 0 * D:1 * D].rearrange("p (t d) -> p t d", d=128)
    wk_lo = wkv_b[:, 1 * D:2 * D].rearrange("p (t d) -> p t d", d=128)
    wv_hi = wkv_b[:, 2 * D:3 * D].rearrange("p (t d) -> p t d", d=128)
    wv_lo = wkv_b[:, 3 * D:4 * D].rearrange("p (t d) -> p t d", d=128)
    bkv = ar.alloc(2, F32)
    cY1 = p.dma("sync", "sWKV", wk_f, wk_h.rearrange("(t p) d -> p t d", p=128),
                waits=st["woe"])
    cY2 = p.dma("sync", "sWKV", wv_f, wv_h.rearrange("(t p) d -> p t d", p=128),
                waits=[("sWKV", cY1)])
    cY3 = p.dma("sync", "sWKV", bkv[:, 0:1], bk_h.unsqueeze(1), waits=[("sWKV", cY2)])
    cY4 = p.dma("sync", "sWKV", bkv[:, 1:2], bv_h.unsqueeze(1), waits=[("sWKV", cY3)])
    cS1 = p.emit("scalar", lambda e: e.activation(
        wk_hi.rearrange("p a b -> p (a b)"), wkv_f[:, 0:D], AF.Identity), waits=[("sWKV", cY1)], inc="sS")
    cS2 = p.emit("scalar", lambda e: e.activation(
        wv_hi.rearrange("p a b -> p (a b)"), wkv_f[:, D:2 * D], AF.Identity), waits=[("sWKV", cY2)], inc="sS")
    cV1 = p.emit("vector", lambda e: e.tensor_tensor(
        wk_lo.rearrange("p a b -> p (a b)"), wkv_f[:, 0:D],
        wk_hi.rearrange("p a b -> p (a b)"), op=ALU.subtract), waits=[("sS", cS1)], inc="sV")
    cV2 = p.emit("vector", lambda e: e.tensor_tensor(
        wv_lo.rearrange("p a b -> p (a b)"), wkv_f[:, D:2 * D],
        wv_hi.rearrange("p a b -> p (a b)"), op=ALU.subtract), waits=[("sS", cS2)], inc="sV")
    st["wkv"] = [("sV", cV2), ("sWKV", cY4)]

    xt2 = [ar.alloc(D, F32)]
    xt2.append(xt2[0])
    sq_b = ar.alloc(D, F32)
    xn_b = ar.alloc(D, F32)
    xnT_hi = ar.alloc(16 * 512, BF16).rearrange("p (t q) -> p t q", q=512)
    xnT_lo = ar.alloc(16 * 512, BF16).rearrange("p (t q) -> p t q", q=512)
    xown_r = x_own.rearrange("(n p) m -> n p m", p=128)
    xoth_r = x_oth.rearrange("(n p) m -> n p m", p=128)

    def emit_ln(xt_ap, wa, sqb, xnb):
        rs = smallv
        c1 = p.emit("vector", lambda e: e.reduce_sum(rs[:, 0:1], xt_ap, axis=AX), waits=wa, inc="sV")
        c2 = p.emit("scalar", lambda e: e.activation(sqb, xt_ap, AF.Square, accum_out=rs[:, 2:3]),
                    waits=wa, inc="sS")
        c3 = p.emit("vector", lambda e: e.tensor_scalar(rs[:, 1:2], rs[:, 0:1], 1.0 / D, None, op0=ALU.mult),
                    waits=[("sV", c1)], inc="sV")
        c4 = p.emit("vector", lambda e: e.tensor_tensor(rs[:, 3:4], rs[:, 1:2], rs[:, 1:2], op=ALU.mult),
                    waits=[("sV", c3)], inc="sV")
        c5 = p.emit("vector", lambda e: e.tensor_scalar(rs[:, 4:5], rs[:, 2:3], 1.0 / D, 1e-5,
                                                        op0=ALU.mult, op1=ALU.add),
                    waits=[("sS", c2)], inc="sV")
        c6 = p.emit("vector", lambda e: e.tensor_tensor(rs[:, 5:6], rs[:, 4:5], rs[:, 3:4], op=ALU.subtract),
                    waits=[("sV", c5), ("sV", c4)], inc="sV")
        c7 = p.emit("scalar", lambda e: e.activation(rs[:, 6:7], rs[:, 5:6], AF.Sqrt),
                    waits=[("sV", c6)], inc="sS")
        c8 = p.emit("vector", lambda e: e.reciprocal(rs[:, 7:8], rs[:, 6:7]), waits=[("sS", c7)], inc="sV")
        c9 = p.emit("vector", lambda e: e.tensor_tensor(rs[:, 8:9], rs[:, 7:8], rs[:, 7:8], op=ALU.mult),
                    waits=[("sV", c8)], inc="sV")
        c10 = p.emit("vector", lambda e: e.tensor_tensor(rs[:, 9:10], rs[:, 8:9], rs[:, 5:6], op=ALU.mult),
                     waits=[("sV", c9)], inc="sV")
        c11 = p.emit("vector", lambda e: e.tensor_scalar(rs[:, 10:11], rs[:, 9:10], -0.5, 1.5,
                                                         op0=ALU.mult, op1=ALU.add),
                     waits=[("sV", c10)], inc="sV")
        c12 = p.emit("vector", lambda e: e.tensor_tensor(rs[:, 11:12], rs[:, 7:8], rs[:, 10:11], op=ALU.mult),
                     waits=[("sV", c11)], inc="sV")
        c15 = p.emit("vector", lambda e: e.tensor_tensor(rs[:, 13:14], rs[:, 1:2], rs[:, 11:12], op=ALU.mult),
                     waits=[("sV", c12)], inc="sV")
        c16 = p.emit("vector", lambda e: e.tensor_scalar(rs[:, 14:15], rs[:, 13:14], -1.0, None, op0=ALU.mult),
                     waits=[("sV", c15)], inc="sV")
        c17 = p.emit("scalar", lambda e: e.activation(xnb, xt_ap, AF.Identity,
                                                      bias=rs[:, 14:15], scale=rs[:, 11:12]),
                     waits=wa + [("sV", c16)], inc="sS")
        return c17


    xt_free = [[], []]
    xnT_free = [("sT", 0)]
    for w in range(8):
        for i in range(4):
            gtile = w * 4 + i
            src = xown_r[gtile] if gtile < 16 else xoth_r[gtile - 16]
            par = 0
            cy = p.dma("sync", "sXTA", xt2[par], src, waits=xt_free[par])
            cs_xn = emit_ln(xt2[par], [("sXTA", cy)], sq_b, xn_b)
            xt_free[par] = [("sS", cs_xn)]
            for r in range(4):
                bnk = 6 + (r % 2)
                ct = None
                for j in range(4):
                    mt = r * 4 + j
                    ct = p.emit("tensor",
                                lambda e, mt=mt, bnk=bnk, j=j: e.transpose(
                                    bank(bnk)[:, j * 128:(j + 1) * 128],
                                    xn_b[:, mt * 128:(mt + 1) * 128], ident),
                                waits=[("sS", cs_xn)] + bank_free[bnk] + xnT_free + id_rdy, inc="sT")
                dst_h = xnT_hi[:, r * 4:(r + 1) * 4, i * 128:(i + 1) * 128]
                dst_l = xnT_lo[:, r * 4:(r + 1) * 4, i * 128:(i + 1) * 128]
                cs = p.emit("scalar", lambda e, d=dst_h, bnk=bnk: e.activation(
                    d, bank(bnk).rearrange("p (a b) -> p a b", b=128), AF.Identity),
                    waits=[("sT", ct)], inc="sS")
                cv = p.emit("vector", lambda e, dl=dst_l, dh=dst_h, bnk=bnk: e.tensor_tensor(
                    dl, bank(bnk).rearrange("p (a b) -> p a b", b=128), dh, op=ALU.subtract),
                    waits=[("sS", cs)], inc="sV")
                bank_free[bnk] = [("sV", cv)]
        split_rdy = [("sV", p.cnt["sV"]), ("sS", p.cnt["sS"])]
        for ch, (whi, wlo, pb) in enumerate([(wk_hi, wk_lo, 4), (wv_hi, wv_lo, 5)]):
            ops = [(whi, xnT_hi), (whi, xnT_lo), (wlo, xnT_hi)]
            cm = None
            n_ops = 16 * 3
            k = 0
            for mt in range(16):
                for (la, hb) in ops:
                    first, last = (k == 0), (k == n_ops - 1)
                    extra = (st["wkv"] + split_rdy + bank_free[pb]) if first else []
                    cm = p.emit("tensor",
                                lambda e, la=la, hb=hb, mt=mt, pb=pb, first=first, last=last: e.matmul(
                                    bank(pb), la[:, mt, :], hb[:, mt, :],
                                    start=first, stop=last, **MMKW),
                                waits=extra, inc="sT")
                    k += 1
            dst = kT_f if ch == 0 else vT_f
            cs = p.emit("scalar",
                        lambda e, dst=dst, w=w, pb=pb, ch=ch: e.activation(
                            dst[:, w * 512:(w + 1) * 512], bank(pb), AF.Identity, bias=bkv[:, ch:ch + 1]),
                        waits=[("sT", cm)], inc="sS")
            bank_free[pb] = [("sS", cs)]
        xnT_free = [("sT", p.cnt["sT"])]
    kv_done = [("sS", p.cnt["sS"])]
    cS1 = p.emit("scalar", lambda e: e.activation(kT_hi, kT_f, AF.Identity), waits=kv_done, inc="sS")
    cV1 = p.emit("vector", lambda e: e.tensor_tensor(kT_lo, kT_f, kT_hi, op=ALU.subtract),
                 waits=[("sS", cS1)], inc="sV")
    cS2 = p.emit("scalar", lambda e: e.activation(vT_hi, vT_f, AF.Identity), waits=kv_done, inc="sS")
    cV2 = p.emit("vector", lambda e: e.tensor_tensor(vT_lo, vT_f, vT_hi, op=ALU.subtract),
                 waits=[("sS", cS2)], inc="sV")
    kv_split = [("sV", cV2)]
    ar.pop()
    ar.push()
    vtok_b = ar.alloc(2 * S, BF16)
    vtok_hi = vtok_b[:, 0:S].rearrange("p (t d) -> p t d", d=128)
    vtok_lo = vtok_b[:, S:2 * S].rearrange("p (t d) -> p t d", d=128)
    for r in range(8):
        bnk = 6 + (r % 2)
        ct = None
        for j in range(4):
            it = r * 4 + j
            ct = p.emit("tensor",
                        lambda e, it=it, bnk=bnk, j=j: e.transpose(
                            bank(bnk)[:, j * 128:(j + 1) * 128],
                            vT_f[:, it * 128:(it + 1) * 128], ident),
                        waits=kv_done + bank_free[bnk], inc="sT")
        dh = vtok_hi[:, r * 4:(r + 1) * 4, :]
        dl = vtok_lo[:, r * 4:(r + 1) * 4, :]
        cs = p.emit("scalar", lambda e, dh=dh, bnk=bnk: e.activation(
            dh, bank(bnk).rearrange("p (a b) -> p a b", b=128), AF.Identity),
            waits=[("sT", ct)], inc="sS")
        cv = p.emit("vector", lambda e, dl=dl, dh=dh, bnk=bnk: e.tensor_tensor(
            dl, bank(bnk).rearrange("p (a b) -> p a b", b=128), dh, op=ALU.subtract),
            waits=[("sS", cs)], inc="sV")
        bank_free[bnk] = [("sV", cv)]
    vtok_rdy = [("sV", p.cnt["sV"])]

    pq_f = [ar.alloc(512, F32) for _ in range(2)]
    pq_h = [ar.alloc(512, BF16) for _ in range(2)]
    pq_l = [ar.alloc(512, BF16) for _ in range(2)]
    rrow = ar.alloc(D, F32)
    pq_free = [[], []]
    for qc in range(4):
        ob = bank_free[2]
        rb = bank_free[3]
        c_o, c_r = None, None
        n_it = 32
        for it in range(n_it):
            par = it % 2
            ct = None
            sops = [(kT_hi, vT_hi), (kT_hi, vT_lo), (kT_lo, vT_hi)]
            for k, (la, rbs) in enumerate(sops):
                ct = p.emit("tensor",
                            lambda e, la=la, rbs=rbs, it=it, qc=qc, par=par,
                            first=(k == 0), last=(k == 2): e.matmul(
                                bank(par), la[:, it * 128:(it + 1) * 128],
                                rbs[:, qc * 512:(qc + 1) * 512],
                                start=first, stop=last, **MMKW),
                            waits=(kv_split + bank_free[par] + pq_free[par]) if k == 0 else [],
                            inc="sT")
            cs = p.emit("scalar", lambda e, par=par: e.activation(pq_f[par], bank(par), AF.Exp,
                                                                  scale=c128[:, 0:1]),
                        waits=[("sT", ct)], inc="sS")
            bank_free[par] = [("sS", cs)]
            cv1 = p.emit("vector", lambda e, par=par: e.tensor_copy(pq_h[par], pq_f[par]),
                         waits=[("sS", cs)], inc="sV")
            cv2 = p.emit("vector", lambda e, par=par: e.tensor_tensor(pq_l[par], pq_f[par], pq_h[par],
                                                                      op=ALU.subtract),
                         waits=[("sV", cv1)], inc="sV")
            aops = [(vtok_hi, pq_h), (vtok_hi, pq_l), (vtok_lo, pq_h)]
            for k, (la, hb) in enumerate(aops):
                c_o = p.emit("tensor",
                             lambda e, la=la, hb=hb, it=it, par=par,
                             first=(c_o is None), last=(it == n_it - 1 and k == 2): e.matmul(
                                 bank(2), la[:, it, :], hb[par], start=first, stop=last, **MMKW),
                             waits=(vtok_rdy + [("sV", cv2)] + (ob if c_o is None else [])),
                             inc="sT")
            for k, hb in enumerate((pq_h, pq_l)):
                c_r = p.emit("tensor",
                             lambda e, hb=hb, par=par, first=(c_r is None),
                             last=(it == n_it - 1 and k == 1): e.matmul(
                                 bank(3)[0:1, :], ones_c, hb[par], start=first, stop=last, **MMKW),
                             waits=(rb if c_r is None else []), inc="sT")
            pq_free[par] = [("sT", p.cnt["sT"])]
        cs = p.emit("scalar", lambda e, qc=qc: e.activation(oT_h[:, qc * 512:(qc + 1) * 512],
                                                            bank(2), AF.Identity),
                    waits=[("sT", c_o)], inc="sS")
        cv = p.emit("vector", lambda e, qc=qc: e.tensor_tensor(
            oT_l[:, qc * 512:(qc + 1) * 512], bank(2), oT_h[:, qc * 512:(qc + 1) * 512], op=ALU.subtract),
            waits=[("sS", cs)], inc="sV")
        bank_free[2] = [("sV", cv)]
        cvr = p.emit("vector", lambda e, qc=qc: e.tensor_copy(rrow[0:1, qc * 512:(qc + 1) * 512],
                                                              bank(3)[0:1, :]),
                     waits=[("sT", c_r)], inc="sV")
        bank_free[3] = [("sV", cvr)]
    oT_rdy = [("sS", p.cnt["sS"]), ("sV", p.cnt["sV"])]
    ct = None
    for j in range(16):
        ct = p.emit("tensor",
                    lambda e, j=j: e.transpose(bank(6)[:, j:j + 1], rrow[0:1, j * 128:(j + 1) * 128],
                                               ident[0:1, 0:1]),
                    waits=oT_rdy + bank_free[6], inc="sT")
    cv = p.emit("vector", lambda e: e.reciprocal(rinv, bank(6)[:, 0:16]), waits=[("sT", ct)], inc="sV")
    bank_free[6] = [("sV", cv)]
    st["rinv"] = [("sV", cv)]
    if "oT_h" in taps:
        p.dma_tap(taps["oT_h"], oT_h, waits=oT_rdy)
    if "oT_l" in taps:
        p.dma_tap(taps["oT_l"], oT_l, waits=oT_rdy)
    if "rinv" in taps:
        p.dma_tap(taps["rinv"], rinv, waits=st["rinv"])
    if "kT" in taps:
        p.dma_tap(taps["kT"], kT_f, waits=kv_done)
    if "vT" in taps:
        p.dma_tap(taps["vT"], vT_f, waits=kv_done)
    if "woe" in taps:
        p.dma_tap(taps["woe"], woe_f, waits=st["woe"])
    ar.pop()
    ar.pop()

    fenceA = p.fence()
    ar.push()
    amax_buf = ar.alloc(4096, F32)
    s2T_gu = ar.alloc(II, F32)
    r2T_gu = ar.alloc(II, F32)
    s2T_d = s2T_gu[:, 0:2 * D]
    r2T_d = s2T_gu[:, 2 * D:4 * D]
    raw2 = [ar.alloc(2048, F32) for _ in range(2)]
    rep_r = ar.alloc(2048, F32)
    rep_s = ar.alloc(2048, F32)
    q8w = ar.alloc(2048, FP8)
    wdq = ar.alloc(2048, F32)
    whi = ar.alloc(2048, BF16)
    wlo = ar.alloc(2048, BF16)

    gn_r = gn_h.rearrange("(n p) m -> n p m", p=128)
    un_r = un_h.rearrange("(n p) m -> n p m", p=128)
    dn_r = dn_h.rearrange("(n p) m -> n p m", p=128)
    gt_r = gt_h.rearrange("(n p) m -> n p m", p=128)
    ut_r = ut_h.rearrange("(n p) m -> n p m", p=128)
    dt_r = dt_h.rearrange("(n p) m -> n p m", p=128)
    scg_r, scu_r, scd_r = sc_g, sc_u, sc_d

    rawgate = [list(fenceA), list(fenceA)]
    repgate = [list(fenceA)]
    pipe = {"q8": [], "wdq": [], "whi": [], "wlo": []}

    def quant_weight(nat_r, trn_r, sc_r, widx):
        is_d = (widx == 2)
        amax_v = amax_buf.rearrange("p (t b) -> p t b", b=(256 if is_d else 64))
        am_gate = [("sT", p.cnt["sT"])]
        last_am = 0
        for j in range(16 if is_d else 64):
            par = j % 2
            if not is_d:
                cy = p.dma("sync", f"sW{par}", raw2[par][:, 0:2048], nat_r[j], waits=rawgate[par])
                cv = p.emit("vector",
                            lambda e, j=j, par=par: e.reduce_max(
                                amax_v[:, j, :], raw2[par][:, 0:2048].rearrange("p (b k) -> p b k", k=32),
                                axis=AX, apply_absolute_value=True),
                            waits=[(f"sW{par}", cy)] + am_gate, inc="sV")
                rawgate[par] = [("sV", cv)]
            else:
                for hh in range(4):
                    par = (j * 4 + hh) % 2
                    cy = p.dma("sync", f"sW{par}", raw2[par], nat_r[j][:, hh * 2048:(hh + 1) * 2048],
                               waits=rawgate[par])
                    cv = p.emit("vector",
                                lambda e, j=j, hh=hh, par=par: e.reduce_max(
                                    amax_v[:, j, hh * 64:(hh + 1) * 64],
                                    raw2[par].rearrange("p (b k) -> p b k", k=32),
                                    axis=AX, apply_absolute_value=True),
                                waits=[(f"sW{par}", cy)] + am_gate, inc="sV")
                    rawgate[par] = [("sV", cv)]
            last_am = p.cnt["sV"]
        cv = p.emit("vector",
                    lambda e: e.tensor_scalar(amax_buf, amax_buf, 1.0 / 224.0, 2e-30,
                                              op0=ALU.mult, op1=ALU.max),
                    waits=[("sV", last_am)], inc="sV")
        if not is_d:
            prow = 64 * widx
            tgate = [("sRR", p.cnt["sRR"]), ("sRS", p.cnt["sRS"])]
            for r in range(16):
                bnk = r % 2
                ct = None
                for j in range(4):
                    tj = r * 4 + j
                    ct = p.emit("tensor",
                                lambda e, tj=tj, bnk=bnk, j=j: e.transpose(
                                    bank(bnk)[0:64, j * 128:(j + 1) * 128], amax_v[:, tj, :], ident),
                                waits=[("sV", cv)] + bank_free[bnk] + id_rdy, inc="sT")
                cvv = p.emit("vector",
                             lambda e, r=r, bnk=bnk, prow=prow: e.tensor_copy(
                                 s2T_gu[prow:prow + 64, r * 512:(r + 1) * 512], bank(bnk)[0:64, :]),
                             waits=[("sT", ct)] + tgate, inc="sV")
                bank_free[bnk] = [("sV", cvv)]
            cvr = p.emit("vector",
                         lambda e, prow=prow: e.reciprocal(r2T_gu[prow:prow + 64, :], s2T_gu[prow:prow + 64, :]),
                         waits=[("sV", p.cnt["sV"])], inc="sV")
        else:
            tgate = [("sRR", p.cnt["sRR"]), ("sRS", p.cnt["sRS"])]
            for j in range(16):
                bnk = j % 2
                ct0 = p.emit("tensor",
                             lambda e, j=j, bnk=bnk: e.transpose(
                                 bank(bnk)[:, 0:128], amax_v[:, j, 0:128], ident),
                             waits=[("sV", cv)] + bank_free[bnk] + id_rdy, inc="sT")
                ct1 = p.emit("tensor",
                             lambda e, j=j, bnk=bnk: e.transpose(
                                 bank(bnk)[:, 128:256], amax_v[:, j, 128:256], ident),
                             waits=[], inc="sT")
                cv1 = p.emit("vector",
                             lambda e, j=j, bnk=bnk: e.tensor_copy(
                                 s2T_d[:, j * 128:(j + 1) * 128], bank(bnk)[:, 0:128]),
                             waits=[("sT", ct1)] + tgate, inc="sV")
                cv2 = p.emit("vector",
                             lambda e, j=j, bnk=bnk: e.tensor_copy(
                                 s2T_d[:, D + j * 128: D + (j + 1) * 128], bank(bnk)[:, 128:256]),
                             waits=[("sT", ct1)], inc="sV")
                bank_free[bnk] = [("sV", cv2)]
            cvr = p.emit("vector", lambda e: e.reciprocal(r2T_d, s2T_d),
                         waits=[("sV", p.cnt["sV"])], inc="sV")
        tbl = [("sV", cvr)]

        wq = 2048
        for j in range(64 if is_d else 16):
            for hh in range(1 if is_d else 4):
                par = (j * (1 if is_d else 4) + hh) % 2
                src_tile = trn_r[j] if is_d else trn_r[j][:, hh * wq:(hh + 1) * wq]
                cy = p.dma("sync", f"sW{par}", raw2[par][:, 0:wq], src_tile, waits=rawgate[par] + tbl)
                if not is_d:
                    r0 = 64 * widx + j * 4
                    src_s = s2T_gu[r0:r0 + 4, hh * 2048:(hh + 1) * 2048]
                    src_r = r2T_gu[r0:r0 + 4, hh * 2048:(hh + 1) * 2048]
                else:
                    ib0 = j * 4
                    q_, r0 = ib0 // 128, ib0 % 128
                    src_s = s2T_d[r0:r0 + 4, q_ * D:(q_ + 1) * D]
                    src_r = r2T_d[r0:r0 + 4, q_ * D:(q_ + 1) * D]
                cg1 = p.dma("gpsimd", "sRR", rep_r[:, 0:wq], src_r.unsqueeze(1).broadcast_to([4, 32, wq]),
                            waits=tbl + repgate[0])
                cg2 = p.dma("gpsimd", "sRS", rep_s[:, 0:wq], src_s.unsqueeze(1).broadcast_to([4, 32, wq]),
                            waits=tbl + repgate[0])
                cq = p.emit("vector",
                            lambda e, par=par, wq=wq: e.tensor_tensor(
                                q8w[:, 0:wq], raw2[par][:, 0:wq], rep_r[:, 0:wq], op=ALU.mult),
                            waits=[(f"sW{par}", cy), ("sRR", cg1)] + pipe["q8"], inc="sV")
                rawgate[par] = [("sV", cq)]
                cd = p.emit("vector",
                            lambda e, wq=wq: e.tensor_tensor(
                                wdq[:, 0:wq], q8w[:, 0:wq], rep_s[:, 0:wq], op=ALU.mult),
                            waits=[("sV", cq), ("sRS", cg2)] + pipe["wdq"], inc="sV")
                repgate[0] = [("sV", cd)]
                pipe["q8"] = [("sV", cd)]
                ch = p.emit("scalar",
                            lambda e, wq=wq: e.activation(whi[:, 0:wq], wdq[:, 0:wq], AF.Identity),
                            waits=[("sV", cd)] + pipe["whi"], inc="sS")
                cl = p.emit("vector",
                            lambda e, wq=wq: e.tensor_tensor(
                                wlo[:, 0:wq], wdq[:, 0:wq], whi[:, 0:wq], op=ALU.subtract),
                            waits=[("sS", ch)] + pipe["wlo"], inc="sV")
                pipe["wdq"] = [("sV", cl)]
                dsth = sc_r[j, 0, :, :] if is_d else sc_r[j, 0, :, hh * wq:(hh + 1) * wq]
                dstl = sc_r[j, 1, :, :] if is_d else sc_r[j, 1, :, hh * wq:(hh + 1) * wq]
                cw1 = p.dma("scalar", "sWH", dsth, whi[:, 0:wq], waits=[("sV", cl)])
                cw2 = p.dma("scalar", "sWL", dstl, wlo[:, 0:wq], waits=[("sV", cl)])
                pipe["whi"] = [("sWH", cw1)]
                pipe["wlo"] = [("sWL", cw2)]
        return [("sWH", p.cnt["sWH"]), ("sWL", p.cnt["sWL"])]

    wg_done = quant_weight(gn_r, gt_r, scg_r, 0)
    wu_done = quant_weight(un_r, ut_r, scu_r, 1)
    wd_done = quant_weight(dn_r, dt_r, scd_r, 2)
    ar.pop()

    fenceW = p.fence()
    ar.push()
    xhT_h = ar.alloc(16 * 512, BF16).rearrange("p (t q) -> p t q", q=512)
    xhT_l = ar.alloc(16 * 512, BF16).rearrange("p (t q) -> p t q", q=512)
    acc = ar.alloc(4 * D, F32).rearrange("p (t m) -> p t m", m=D)
    hT_h = ar.alloc(16 * 512, BF16).rearrange("p (t q) -> p t q", q=512)
    hT_l = ar.alloc(16 * 512, BF16).rearrange("p (t q) -> p t q", q=512)
    gu_buf = [[ar.alloc(2 * 2 * 512, BF16).rearrange("p (t q) -> p t q", q=512)
               for _ in range(2)] for _ in range(2)]
    d_buf = [ar.alloc(2 * 512, BF16).rearrange("p (h q) -> p h q", h=2) for _ in range(2)]
    xt_b = ar.alloc(D, F32)
    q8x = ar.alloc(D, FP8)
    xq_b = ar.alloc(D, F32)
    qs = ar.alloc(3 * 64, F32)
    h_b = ar.alloc(512, F32)
    sg_b = ar.alloc(512, F32)
    hq_b = ar.alloc(512, F32)
    q8h = ar.alloc(512, FP8)
    hs = ar.alloc(3 * 16, F32)
    fin = ar.alloc(D, F32)
    res_b = ar.alloc(D, F32)
    sq2_b = fin
    xn2_b = res_b

    out_r = out_h.rearrange("(n p) m -> n p m", p=128)
    gu_free = [list(fenceW), list(fenceW)]
    d_free = [list(fenceW), list(fenceW)]
    xt_gate = list(fenceW)
    fin_gate = list(fenceW)
    res_gate = list(fenceW)
    acc_gate = list(fenceW)
    sg_gate = []
    xq_readers = []

    for tc in range(4):
        for i in range(4):
            gt = tc * 4 + i
            cy = p.dma("sync", "sXB", xt_b, xown_r[gt], waits=xt_gate)
            cs_xn = emit_ln(xt_b, [("sXB", cy)], sq2_b, xn2_b)
            xt_gate = [("sS", cs_xn)]
            qa = p.emit("vector",
                        lambda e: e.reduce_max(qs[:, 0:64], xn2_b.rearrange("p (b k) -> p b k", k=32),
                                               axis=AX, apply_absolute_value=True),
                        waits=[("sS", cs_xn)], inc="sV")
            qb = p.emit("vector",
                        lambda e: e.tensor_scalar(qs[:, 64:128], qs[:, 0:64], 1.0 / 224.0, 2e-30,
                                                  op0=ALU.mult, op1=ALU.max),
                        waits=[("sV", qa)], inc="sV")
            qc_ = p.emit("vector", lambda e: e.reciprocal(qs[:, 128:192], qs[:, 64:128]),
                         waits=[("sV", qb)], inc="sV")
            qd = p.emit("vector",
                        lambda e: e.tensor_tensor(
                            q8x.rearrange("p (b k) -> p b k", k=32),
                            xn2_b.rearrange("p (b k) -> p b k", k=32),
                            qs[:, 128:192].unsqueeze(2).broadcast_to([128, 64, 32]), op=ALU.mult),
                        waits=[("sV", qc_)], inc="sV")
            qe = p.emit("vector",
                        lambda e: e.tensor_tensor(
                            xq_b.rearrange("p (b k) -> p b k", k=32),
                            q8x.rearrange("p (b k) -> p b k", k=32),
                            qs[:, 64:128].unsqueeze(2).broadcast_to([128, 64, 32]), op=ALU.mult),
                        waits=[("sV", qd)], inc="sV")
            for r in range(4):
                bnk = r % 2
                ct = None
                for jj in range(4):
                    mt = r * 4 + jj
                    ct = p.emit("tensor",
                                lambda e, mt=mt, bnk=bnk, jj=jj: e.transpose(
                                    bank(bnk)[:, jj * 128:(jj + 1) * 128],
                                    xq_b[:, mt * 128:(mt + 1) * 128], ident),
                                waits=[("sV", qe)] + bank_free[bnk] + xq_readers, inc="sT")
                dh = xhT_h[:, r * 4:(r + 1) * 4, i * 128:(i + 1) * 128]
                dl = xhT_l[:, r * 4:(r + 1) * 4, i * 128:(i + 1) * 128]
                cs = p.emit("scalar", lambda e, dh=dh, bnk=bnk: e.activation(
                    dh, bank(bnk).rearrange("p (a b) -> p a b", b=128), AF.Identity),
                    waits=[("sT", ct)], inc="sS")
                cvv = p.emit("vector", lambda e, dl=dl, dh=dh, bnk=bnk: e.tensor_tensor(
                    dl, bank(bnk).rearrange("p (a b) -> p a b", b=128), dh, op=ALU.subtract),
                    waits=[("sS", cs)], inc="sV")
                bank_free[bnk] = [("sV", cvv)]
        xq_rdy = [("sV", p.cnt["sV"]), ("sS", p.cnt["sS"])]
        xq_readers = []
        if "xhT_h" in taps and tc == 0:
            p.dma_tap(taps["xhT_h"], xhT_h.rearrange("p t q -> p (t q)"), waits=xq_rdy)
            xq_readers = xq_readers + [("sTAP", p.cnt["sTAP"])]

        for g in range(4):
            hT_last = []
            for jic in range(4):
                ic = g * 4 + jic
                mm_gate_last = [None] * 4
                mm_up_last = [None] * 4
                for qtr in range(8):
                    par = qtr % 2
                    cyg = p.dma(
                        "sync", f"sGU{par}g", gu_buf[par][0],
                        scg_r[qtr * 2:(qtr + 1) * 2, :, :, ic * 512:(ic + 1) * 512]
                        .rearrange("a h p q -> p (a h) q"),
                        waits=gu_free[par] + wg_done + xq_rdy)
                    cyu = p.dma(
                        "sync", f"sGU{par}u", gu_buf[par][1],
                        scu_r[qtr * 2:(qtr + 1) * 2, :, :, ic * 512:(ic + 1) * 512]
                        .rearrange("a h p q -> p (a h) q"),
                        waits=gu_free[par] + wu_done)
                    ld_done2 = [(f"sGU{par}g", cyg), (f"sGU{par}u", cyu)]
                    for mtl in range(2):
                        mt = qtr * 2 + mtl
                        for tsub in range(4):
                            gb, ub = tsub, 4 + tsub
                            ops6 = ((xhT_h, gu_buf[par][0], 0, gb, mm_gate_last, False),
                                    (xhT_h, gu_buf[par][0], 1, gb, mm_gate_last, False),
                                    (xhT_l, gu_buf[par][0], 0, gb, mm_gate_last, True),
                                    (xhT_h, gu_buf[par][1], 0, ub, mm_up_last, False),
                                    (xhT_h, gu_buf[par][1], 1, ub, mm_up_last, False),
                                    (xhT_l, gu_buf[par][1], 0, ub, mm_up_last, True))
                            for (la, wvv, hl, pb, rec, is3) in ops6:
                                first = rec[tsub] is None
                                last = (qtr == 7 and mtl == 1 and is3)
                                ws = list(ld_done2) + xq_rdy
                                if first:
                                    ws += bank_free[pb]
                                rec[tsub] = p.emit(
                                    "tensor",
                                    lambda e, la=la, wvv=wvv, hl=hl, pb=pb, mt=mt, mtl=mtl, tsub=tsub,
                                    first=first, last=last: e.matmul(
                                        bank(pb), la[:, mt, tsub * 128:(tsub + 1) * 128],
                                        wvv[:, mtl * 2 + hl, :], start=first, stop=last, **MMKW),
                                    waits=ws, inc="sT")
                    gu_free[par] = [("sT", p.cnt["sT"])]
                for tsub in range(4):
                    gb, ub = tsub, 4 + tsub
                    e1 = p.emit("scalar", lambda e, gb=gb: e.activation(sg_b, bank(gb), AF.Silu),
                                waits=[("sT", mm_gate_last[tsub])] + sg_gate, inc="sS")
                    e2 = p.emit("vector", lambda e, ub=ub: e.tensor_tensor(h_b, bank(ub), sg_b, op=ALU.mult),
                                waits=[("sS", e1), ("sT", mm_up_last[tsub])], inc="sV")
                    sg_gate = [("sV", e2)]
                    bank_free[ub] = [("sV", e2)]
                    q1 = p.emit("vector",
                                lambda e: e.reduce_max(hs[:, 0:16], h_b.rearrange("p (b k) -> p b k", k=32),
                                                       axis=AX, apply_absolute_value=True),
                                waits=[("sV", e2)], inc="sV")
                    q2 = p.emit("vector",
                                lambda e: e.tensor_scalar(hs[:, 16:32], hs[:, 0:16], 1.0 / 224.0, 2e-30,
                                                          op0=ALU.mult, op1=ALU.max),
                                waits=[("sV", q1)], inc="sV")
                    q3 = p.emit("vector", lambda e: e.reciprocal(hs[:, 32:48], hs[:, 16:32]),
                                waits=[("sV", q2)], inc="sV")
                    q4 = p.emit("vector",
                                lambda e: e.tensor_tensor(
                                    q8h.rearrange("p (b k) -> p b k", k=32),
                                    h_b.rearrange("p (b k) -> p b k", k=32),
                                    hs[:, 32:48].unsqueeze(2).broadcast_to([128, 16, 32]), op=ALU.mult),
                                waits=[("sV", q3)], inc="sV")
                    q5 = p.emit("vector",
                                lambda e: e.tensor_tensor(
                                    hq_b.rearrange("p (b k) -> p b k", k=32),
                                    q8h.rearrange("p (b k) -> p b k", k=32),
                                    hs[:, 16:32].unsqueeze(2).broadcast_to([128, 16, 32]), op=ALU.mult),
                                waits=[("sV", q4)], inc="sV")
                    ct = None
                    for jj in range(4):
                        ct = p.emit("tensor",
                                    lambda e, jj=jj, gb=gb: e.transpose(
                                        bank(gb)[:, jj * 128:(jj + 1) * 128],
                                        hq_b[:, jj * 128:(jj + 1) * 128], ident),
                                    waits=[("sV", q5)] + bank_free[gb], inc="sT")
                    it0 = jic * 4
                    dh = hT_h[:, it0:it0 + 4, tsub * 128:(tsub + 1) * 128]
                    dl = hT_l[:, it0:it0 + 4, tsub * 128:(tsub + 1) * 128]
                    cs = p.emit("scalar", lambda e, dh=dh, gb=gb: e.activation(
                        dh, bank(gb).rearrange("p (a b) -> p a b", b=128), AF.Identity),
                        waits=[("sT", ct)], inc="sS")
                    cvv = p.emit("vector", lambda e, dl=dl, dh=dh, gb=gb: e.tensor_tensor(
                        dl, bank(gb).rearrange("p (a b) -> p a b", b=128), dh, op=ALU.subtract),
                        waits=[("sS", cs)], inc="sV")
                    bank_free[gb] = [("sV", cvv)]
            hT_all = [("sV", p.cnt["sV"]), ("sS", p.cnt["sS"])]
            for mch in range(4):
                dmm_last = [None] * 4
                for it in range(16):
                    par = it % 2
                    cy1 = p.dma("sync", f"sD{par}", d_buf[par],
                                scd_r[g * 16 + it, :, :, mch * 512:(mch + 1) * 512]
                                .rearrange("h p q -> p h q"),
                                waits=d_free[par] + wd_done)
                    for tsub in range(4):
                        pb = 4 + tsub
                        dops = ((hT_h, 0, False), (hT_h, 1, False), (hT_l, 0, True))
                        for (lah, hl, is3) in dops:
                            first = dmm_last[tsub] is None
                            last = (it == 15 and is3)
                            ws = [(f"sD{par}", cy1)] + hT_all
                            if first:
                                ws += bank_free[pb]
                            dmm_last[tsub] = p.emit(
                                "tensor",
                                lambda e, lah=lah, hl=hl, par=par, it=it, tsub=tsub, pb=pb,
                                first=first, last=last: e.matmul(
                                    bank(pb), lah[:, it, tsub * 128:(tsub + 1) * 128], d_buf[par][:, hl, :],
                                    start=first, stop=last, **MMKW),
                                waits=ws, inc="sT")
                    d_free[par] = [("sT", p.cnt["sT"])]
                for tsub in range(4):
                    pb = 4 + tsub
                    if g == 0:
                        cvx = p.emit("vector",
                                     lambda e, tsub=tsub, mch=mch, pb=pb: e.tensor_copy(
                                         acc[:, tsub, mch * 512:(mch + 1) * 512], bank(pb)),
                                     waits=[("sT", dmm_last[tsub])] + acc_gate, inc="sV")
                    else:
                        cvx = p.emit("vector",
                                     lambda e, tsub=tsub, mch=mch, pb=pb: e.tensor_tensor(
                                         acc[:, tsub, mch * 512:(mch + 1) * 512],
                                         acc[:, tsub, mch * 512:(mch + 1) * 512],
                                         bank(pb), op=ALU.add),
                                     waits=[("sT", dmm_last[tsub])], inc="sV")
                    bank_free[pb] = [("sV", cvx)]
        acc_done = [("sV", p.cnt["sV"])]
        if "acc" in taps and tc == 0:
            p.dma_tap(taps["acc"], acc.rearrange("p t m -> p (t m)"), waits=acc_done)
            acc_gate = acc_gate + [("sTAP", p.cnt["sTAP"])]

        for tsub in range(4):
            j = tc * 4 + tsub
            cmt = None
            for mch in range(4):
                pb = mch
                wops = ((oT_h, woe_h), (oT_h, woe_l), (oT_l, woe_h))
                for k, (la, rb2) in enumerate(wops):
                    ws = []
                    if k == 0:
                        ws = list(bank_free[pb]) + st["woe"] + oT_rdy
                    cmt = p.emit("tensor",
                                 lambda e, la=la, rb2=rb2, j=j, mch=mch, pb=pb,
                                 first=(k == 0), last=(k == 2): e.matmul(
                                     bank(pb), la[:, j * 128:(j + 1) * 128],
                                     rb2[:, mch * 512:(mch + 1) * 512],
                                     start=first, stop=last, **MMKW),
                                 waits=ws, inc="sT")
            cy = p.dma("scalar", "sFIN", fin, xown_r[j], waits=fin_gate)
            cs = p.emit("scalar",
                        lambda e, j=j: e.activation(res_b, PS[:, 0:2048], AF.Identity,
                                                    scale=rinv[:, j:j + 1]),
                        waits=[("sT", cmt)] + st["rinv"] + res_gate, inc="sS")
            for b_ in range(4):
                bank_free[b_] = [("sS", cs)]
            cv1 = p.emit("vector", lambda e: e.tensor_add(res_b, res_b, bo_rep),
                         waits=[("sS", cs), ("sBO", cG_bo)], inc="sV")
            cv2 = p.emit("vector", lambda e: e.tensor_add(res_b, res_b, fin),
                         waits=[("sV", cv1), ("sFIN", cy)], inc="sV")
            cv3 = p.emit("vector",
                         lambda e, tsub=tsub: e.tensor_add(res_b, res_b, acc[:, tsub, :]),
                         waits=[("sV", cv2)] + acc_done, inc="sV")
            cg = p.dma("gpsimd", "sOUT", out_r[j], res_b, waits=[("sV", cv3)])
            fin_gate = [("sV", cv2)]
            res_gate = [("sOUT", cg)]
        acc_gate = [("sOUT", p.cnt["sOUT"])]
        xq_readers = [("sT", p.cnt["sT"]), ("sTAP", p.cnt["sTAP"])]
        xt_gate = xt_gate + [("sOUT", p.cnt["sOUT"]), ("sFIN", p.cnt["sFIN"]),
                             ("sV", p.cnt["sV"])]

    p.emit("gpsimd", lambda e: e.nop(), waits=[("sOUT", p.cnt["sOUT"])])

    with nc.Block() as blk:
        p.build(blk)
    return nc


def kernel(**inputs):
    x = np.ascontiguousarray(inputs["L0_attn_q"], dtype=np.float32)
    wk = np.ascontiguousarray(inputs["L0_attn_Wk"], dtype=np.float32)
    wv = np.ascontiguousarray(inputs["L0_attn_Wv"], dtype=np.float32)
    bk = np.ascontiguousarray(inputs["L0_attn_bk"], dtype=np.float32)
    bv = np.ascontiguousarray(inputs["L0_attn_bv"], dtype=np.float32)
    wo = np.ascontiguousarray(inputs["L0_attn_Wo"], dtype=np.float32)
    bo = np.ascontiguousarray(inputs["L0_attn_bo"], dtype=np.float32)
    gw = np.ascontiguousarray(inputs["L0_ffn_gate_w"], dtype=np.float32)
    uw = np.ascontiguousarray(inputs["L0_ffn_up_w"], dtype=np.float32)
    dw = np.ascontiguousarray(inputs["L0_ffn_down_w"], dtype=np.float32)
    gwT = np.ascontiguousarray(gw.T)
    uwT = np.ascontiguousarray(uw.T)
    dwT = np.ascontiguousarray(dw.T)
    ident = np.eye(128, dtype=np.float32)

    nc = build_program()
    in_maps = []
    for c in range(NCORE):
        b, h = c // 2, c % 2
        in_maps.append({
            "x_own": x[b, h * TOK:(h + 1) * TOK],
            "x_oth": x[b, (1 - h) * TOK:(2 - h) * TOK],
            "wk": wk, "wv": wv, "bk": bk, "bv": bv, "wo": wo, "bo": bo,
            "g_nat": gw, "u_nat": uw, "d_nat": dw,
            "g_t": gwT, "u_t": uwT, "d_t": dwT,
            "ident": ident,
        })
    res = run_bass_kernel_spmd(nc, in_maps, core_ids=list(range(NCORE)))
    out = np.empty((4, S, D), dtype=np.float32)
    for c in range(NCORE):
        b, h = c // 2, c % 2
        out[b, h * TOK:(h + 1) * TOK] = res.results[c]["out"]
    return out


# version 02
# speedup vs baseline: 1.4076x; 1.4076x over previous
import numpy as np
import concourse.bass as bass
import concourse.mybir as mybir
from concourse.bass_utils import run_bass_kernel_spmd

F32 = mybir.dt.float32
BF16 = mybir.dt.bfloat16
FP8 = mybir.dt.float8e4
I32 = mybir.dt.int32
ALU = mybir.AluOpType
AF = mybir.ActivationFunctionType
AX = mybir.AxisListType.X

TOK = 2048
S = 4096
D = 2048
II = 8192
NCORE = 8
ENGS = ("tensor", "vector", "scalar", "sync", "gpsimd")

DEBUG_TAPS = []


class PB:

    def __init__(self, nc):
        self.nc = nc
        self.ev = {e: [] for e in ENGS}
        self.sem = {}
        self.cnt = {}
        self.wm = {e: {} for e in ENGS}

    def add_sem(self, name):
        self.sem[name] = self.nc.alloc_semaphore(name)
        self.cnt[name] = 0

    def emit(self, eng, fn, waits=(), inc=None, by=1):
        w2 = []
        for s, v in waits:
            if not v or v <= 0:
                continue
            if v > self.wm[eng].get(s, 0):
                w2.append((s, v))
                self.wm[eng][s] = v
        self.ev[eng].append((fn, w2, inc, by))
        if inc is not None:
            self.cnt[inc] += by
            return self.cnt[inc]
        return None

    def dma(self, eng, sem, out, in_, waits=(), **kw):
        return self.emit(eng, lambda e, o=out, i=in_, kw=kw: e.dma_start(o, i, **kw),
                         waits=waits, inc=sem, by=16)

    def dma_tap(self, out, in_, waits=()):
        w = list(waits) + ([("sTAP", self.cnt["sTAP"])] if self.cnt["sTAP"] else [])
        return self.dma("gpsimd", "sTAP", out, in_, waits=w)

    def fence(self):
        return [(s, c) for s, c in self.cnt.items() if c > 0]

    def build(self, blk):
        for eng in ENGS:
            evs = self.ev[eng]
            if not evs:
                continue
            sec = getattr(blk, eng)

            def body(e, evs=evs):
                for fn, waits, inc, by in evs:
                    for s, v in waits:
                        e.wait_ge(self.sem[s], v)
                    ins = fn(e)
                    if inc is not None:
                        ins.then_inc(self.sem[inc], by)
            sec(body)


class Arena:
    def __init__(self, t, nbytes):
        self.t = t
        self.cap = nbytes
        self.off = 0
        self.marks = []

    def alloc(self, n_elem, dt):
        esz = {F32: 4, BF16: 2, FP8: 1}[dt]
        self.off = (self.off + 63) & ~63
        start = self.off
        self.off += n_elem * esz
        assert self.off <= self.cap, f"arena overflow {self.off} > {self.cap}"
        if dt == F32:
            return self.t[:, start // 4: start // 4 + n_elem]
        return self.t.bitcast(dt)[:, start // esz: start // esz + n_elem]

    def push(self):
        self.marks.append(self.off)

    def pop(self):
        self.off = self.marks.pop()


def build_program():
    nc = bass.Bass("TRN2", target_bir_lowering=False)
    p = PB(nc)
    for s_ in ("sT", "sV", "sS", "sID", "sXTA", "sWO", "sWKV", "sW0", "sW1",
               "sGU0g", "sGU0u", "sGU1g", "sGU1u", "sD0", "sD1", "sXB",
               "sWH0", "sWH1", "sWL0", "sWL1", "sFIN", "sBO", "sRS0", "sRS1",
               "sOUT", "sTAP"):
        p.add_sem(s_)

    dr = lambda n, sh, dt, k: nc.dram_tensor(n, sh, dt, kind=k).ap()
    x_own = dr("x_own", [TOK, D], F32, "ExternalInput")
    x_oth = dr("x_oth", [TOK, D], F32, "ExternalInput")
    wk_h = dr("wk", [D, 128], F32, "ExternalInput")
    wv_h = dr("wv", [D, 128], F32, "ExternalInput")
    bk_h = dr("bk", [128], F32, "ExternalInput")
    bv_h = dr("bv", [128], F32, "ExternalInput")
    wo_h = dr("wo", [D, D], F32, "ExternalInput")
    bo_h = dr("bo", [D], F32, "ExternalInput")
    gn_h = dr("g_nat", [II, D], F32, "ExternalInput")
    un_h = dr("u_nat", [II, D], F32, "ExternalInput")
    dn_h = dr("d_nat", [D, II], F32, "ExternalInput")
    gt_h = dr("g_t", [D, II], F32, "ExternalInput")
    ut_h = dr("u_t", [D, II], F32, "ExternalInput")
    dt_h = dr("d_t", [II, D], F32, "ExternalInput")
    id_h = dr("ident", [128, 128], F32, "ExternalInput")
    out_h = dr("out", [TOK, D], F32, "ExternalOutput")
    sc_g = dr("sc_g", [16, 2, 128, II], BF16, "Internal")
    sc_u = dr("sc_u", [16, 2, 128, II], BF16, "Internal")
    sc_d = dr("sc_d", [64, 2, 128, D], BF16, "Internal")
    taps = {nm: dr("tap_" + nm, sh, dt, "ExternalOutput") for nm, sh, dt in DEBUG_TAPS}

    ident = nc.alloc_sbuf_tensor("ident_s", [128, 128], F32).ap()
    bo_rep = nc.alloc_sbuf_tensor("bo_rep", [128, D], F32).ap()
    woe_h = nc.alloc_sbuf_tensor("woe_h", [128, D], BF16).ap()
    woe_l = nc.alloc_sbuf_tensor("woe_l", [128, D], BF16).ap()
    oT_h = nc.alloc_sbuf_tensor("oT_h", [128, TOK], BF16).ap()
    oT_l = nc.alloc_sbuf_tensor("oT_l", [128, TOK], BF16).ap()
    rinv = nc.alloc_sbuf_tensor("rinv", [128, 16], F32).ap()
    ones_c = nc.alloc_sbuf_tensor("ones_c", [128, 1], BF16).ap()
    c128 = nc.alloc_sbuf_tensor("c128", [128, 1], F32).ap()
    trig = nc.alloc_sbuf_tensor("trig", [128, 2], F32).ap()
    smallv = nc.alloc_sbuf_tensor("smallv", [128, 16], F32).ap()
    AR_B = 162176
    ar = Arena(nc.alloc_sbuf_tensor("arena", [128, AR_B // 4], F32).ap(), AR_B)

    PS = nc.alloc_psum_tensor("PS", [128, 4096], F32).ap()
    bank = lambda i: PS[:, i * 512:(i + 1) * 512]
    bank_free = [[("sT", 0)] for _ in range(8)]

    MMKW = dict(skip_group_check=True)
    st = {}

    p.emit("vector", lambda e: e.memset(ones_c, 1.0))
    p.emit("vector", lambda e: e.memset(c128, 1.0 / 128.0))
    cV = p.emit("vector", lambda e: e.memset(trig, 2.0), inc="sV")
    p.emit("vector", lambda e: e.tensor_copy(trig[:, 1:2].bitcast(I32), trig[:, 0:1].bitcast(I32)),
           waits=[("sV", cV)], inc="sV")
    cID = p.dma("sync", "sID", ident, id_h)
    cG_bo = p.dma("gpsimd", "sBO", bo_rep, bo_h.unsqueeze(0).partition_broadcast(128))
    id_rdy = [("sID", cID)]

    ar.push()
    wo_all = ar.alloc(8 * D, F32)
    wo_v = wo_all.rearrange("p (c m) -> p c m", m=D)
    woe_f = ar.alloc(D, F32)
    wo_r = wo_h.rearrange("(c p) m -> p c m", p=128)
    half_done = []
    wo_gate = []
    for hf in range(2):
        cY = p.dma("sync", "sWO", wo_v, wo_r[:, hf * 8:(hf + 1) * 8, :], waits=wo_gate)
        cV = p.emit("vector", lambda e: e.tensor_add(
            wo_v[:, 0:4, :].rearrange("p a b -> p (a b)"), wo_v[:, 0:4, :].rearrange("p a b -> p (a b)"),
            wo_v[:, 4:8, :].rearrange("p a b -> p (a b)")), waits=[("sWO", cY)], inc="sV")
        cV = p.emit("vector", lambda e: e.tensor_add(
            wo_v[:, 0:2, :].rearrange("p a b -> p (a b)"), wo_v[:, 0:2, :].rearrange("p a b -> p (a b)"),
            wo_v[:, 2:4, :].rearrange("p a b -> p (a b)")), waits=[("sV", cV)], inc="sV")
        if hf == 0:
            cV = p.emit("vector", lambda e: e.tensor_add(woe_f, wo_v[:, 0, :], wo_v[:, 1, :]),
                        waits=[("sV", cV)], inc="sV")
        else:
            cV = p.emit("vector", lambda e: e.tensor_add(wo_v[:, 0, :], wo_v[:, 0, :], wo_v[:, 1, :]),
                        waits=[("sV", cV)], inc="sV")
            cV = p.emit("vector", lambda e: e.tensor_add(woe_f, woe_f, wo_v[:, 0, :]),
                        waits=[("sV", cV)], inc="sV")
        wo_gate = [("sV", cV)]
    cS = p.emit("scalar", lambda e: e.activation(woe_h, woe_f, AF.Identity),
                waits=[("sV", cV)], inc="sS")
    cV = p.emit("vector", lambda e: e.tensor_tensor(woe_l, woe_f, woe_h, op=ALU.subtract),
                waits=[("sS", cS)], inc="sV")
    st["woe"] = [("sV", cV), ("sS", cS)]
    ar.pop()
    ar.push()

    kTvT = ar.alloc(2 * S, F32)
    kT_f, vT_f = kTvT[:, 0:S], kTvT[:, S:2 * S]
    kv_b = ar.alloc(4 * S, BF16)
    kT_hi, kT_lo = kv_b[:, 0:S], kv_b[:, S:2 * S]
    vT_hi, vT_lo = kv_b[:, 2 * S:3 * S], kv_b[:, 3 * S:4 * S]
    ar.push()
    wkv_f = ar.alloc(2 * D, F32)
    wk_f = wkv_f[:, 0:D].rearrange("p (t d) -> p t d", d=128)
    wv_f = wkv_f[:, D:2 * D].rearrange("p (t d) -> p t d", d=128)
    wkv_b = ar.alloc(4 * D, BF16)
    wk_hi = wkv_b[:, 0 * D:1 * D].rearrange("p (t d) -> p t d", d=128)
    wk_lo = wkv_b[:, 1 * D:2 * D].rearrange("p (t d) -> p t d", d=128)
    wv_hi = wkv_b[:, 2 * D:3 * D].rearrange("p (t d) -> p t d", d=128)
    wv_lo = wkv_b[:, 3 * D:4 * D].rearrange("p (t d) -> p t d", d=128)
    bkv = ar.alloc(2, F32)
    cY1 = p.dma("sync", "sWKV", wk_f, wk_h.rearrange("(t p) d -> p t d", p=128),
                waits=st["woe"])
    cY2 = p.dma("sync", "sWKV", wv_f, wv_h.rearrange("(t p) d -> p t d", p=128),
                waits=[("sWKV", cY1)])
    cY3 = p.dma("sync", "sWKV", bkv[:, 0:1], bk_h.unsqueeze(1), waits=[("sWKV", cY2)])
    cY4 = p.dma("sync", "sWKV", bkv[:, 1:2], bv_h.unsqueeze(1), waits=[("sWKV", cY3)])
    cS1 = p.emit("scalar", lambda e: e.activation(
        wk_hi.rearrange("p a b -> p (a b)"), wkv_f[:, 0:D], AF.Identity), waits=[("sWKV", cY1)], inc="sS")
    cS2 = p.emit("scalar", lambda e: e.activation(
        wv_hi.rearrange("p a b -> p (a b)"), wkv_f[:, D:2 * D], AF.Identity), waits=[("sWKV", cY2)], inc="sS")
    cV1 = p.emit("vector", lambda e: e.tensor_tensor(
        wk_lo.rearrange("p a b -> p (a b)"), wkv_f[:, 0:D],
        wk_hi.rearrange("p a b -> p (a b)"), op=ALU.subtract), waits=[("sS", cS1)], inc="sV")
    cV2 = p.emit("vector", lambda e: e.tensor_tensor(
        wv_lo.rearrange("p a b -> p (a b)"), wkv_f[:, D:2 * D],
        wv_hi.rearrange("p a b -> p (a b)"), op=ALU.subtract), waits=[("sS", cS2)], inc="sV")
    st["wkv"] = [("sV", cV2), ("sWKV", cY4)]

    xt2 = [ar.alloc(D, F32)]
    xt2.append(xt2[0])
    sq_b = ar.alloc(D, F32)
    xn_b = ar.alloc(D, F32)
    xnT_hi = ar.alloc(16 * 512, BF16).rearrange("p (t q) -> p t q", q=512)
    xnT_lo = ar.alloc(16 * 512, BF16).rearrange("p (t q) -> p t q", q=512)
    xown_r = x_own.rearrange("(n p) m -> n p m", p=128)
    xoth_r = x_oth.rearrange("(n p) m -> n p m", p=128)

    def emit_ln(xt_ap, wa, sqb, xnb):
        rs = smallv
        c1 = p.emit("vector", lambda e: e.reduce_sum(rs[:, 0:1], xt_ap, axis=AX), waits=wa, inc="sV")
        c2 = p.emit("scalar", lambda e: e.activation(sqb, xt_ap, AF.Square, accum_out=rs[:, 2:3]),
                    waits=wa, inc="sS")
        c3 = p.emit("vector", lambda e: e.tensor_scalar(rs[:, 1:2], rs[:, 0:1], 1.0 / D, None, op0=ALU.mult),
                    waits=[("sV", c1)], inc="sV")
        c4 = p.emit("vector", lambda e: e.tensor_tensor(rs[:, 3:4], rs[:, 1:2], rs[:, 1:2], op=ALU.mult),
                    waits=[("sV", c3)], inc="sV")
        c5 = p.emit("vector", lambda e: e.tensor_scalar(rs[:, 4:5], rs[:, 2:3], 1.0 / D, 1e-5,
                                                        op0=ALU.mult, op1=ALU.add),
                    waits=[("sS", c2)], inc="sV")
        c6 = p.emit("vector", lambda e: e.tensor_tensor(rs[:, 5:6], rs[:, 4:5], rs[:, 3:4], op=ALU.subtract),
                    waits=[("sV", c5), ("sV", c4)], inc="sV")
        c7 = p.emit("scalar", lambda e: e.activation(rs[:, 6:7], rs[:, 5:6], AF.Sqrt),
                    waits=[("sV", c6)], inc="sS")
        c8 = p.emit("vector", lambda e: e.reciprocal(rs[:, 7:8], rs[:, 6:7]), waits=[("sS", c7)], inc="sV")
        c9 = p.emit("vector", lambda e: e.tensor_tensor(rs[:, 8:9], rs[:, 7:8], rs[:, 7:8], op=ALU.mult),
                    waits=[("sV", c8)], inc="sV")
        c10 = p.emit("vector", lambda e: e.tensor_tensor(rs[:, 9:10], rs[:, 8:9], rs[:, 5:6], op=ALU.mult),
                     waits=[("sV", c9)], inc="sV")
        c11 = p.emit("vector", lambda e: e.tensor_scalar(rs[:, 10:11], rs[:, 9:10], -0.5, 1.5,
                                                         op0=ALU.mult, op1=ALU.add),
                     waits=[("sV", c10)], inc="sV")
        c12 = p.emit("vector", lambda e: e.tensor_tensor(rs[:, 11:12], rs[:, 7:8], rs[:, 10:11], op=ALU.mult),
                     waits=[("sV", c11)], inc="sV")
        c15 = p.emit("vector", lambda e: e.tensor_tensor(rs[:, 13:14], rs[:, 1:2], rs[:, 11:12], op=ALU.mult),
                     waits=[("sV", c12)], inc="sV")
        c16 = p.emit("vector", lambda e: e.tensor_scalar(rs[:, 14:15], rs[:, 13:14], -1.0, None, op0=ALU.mult),
                     waits=[("sV", c15)], inc="sV")
        c17 = p.emit("scalar", lambda e: e.activation(xnb, xt_ap, AF.Identity,
                                                      bias=rs[:, 14:15], scale=rs[:, 11:12]),
                     waits=wa + [("sV", c16)], inc="sS")
        return c17


    xt_free = [[], []]
    xnT_free = [("sT", 0)]
    for w in range(8):
        for i in range(4):
            gtile = w * 4 + i
            src = xown_r[gtile] if gtile < 16 else xoth_r[gtile - 16]
            par = 0
            cy = p.dma("sync", "sXTA", xt2[par], src, waits=xt_free[par])
            cs_xn = emit_ln(xt2[par], [("sXTA", cy)], sq_b, xn_b)
            xt_free[par] = [("sS", cs_xn)]
            for r in range(4):
                bnk = 6 + (r % 2)
                ct = None
                for j in range(4):
                    mt = r * 4 + j
                    ct = p.emit("tensor",
                                lambda e, mt=mt, bnk=bnk, j=j: e.transpose(
                                    bank(bnk)[:, j * 128:(j + 1) * 128],
                                    xn_b[:, mt * 128:(mt + 1) * 128], ident),
                                waits=[("sS", cs_xn)] + bank_free[bnk] + xnT_free + id_rdy, inc="sT")
                dst_h = xnT_hi[:, r * 4:(r + 1) * 4, i * 128:(i + 1) * 128]
                dst_l = xnT_lo[:, r * 4:(r + 1) * 4, i * 128:(i + 1) * 128]
                cs = p.emit("scalar", lambda e, d=dst_h, bnk=bnk: e.activation(
                    d, bank(bnk).rearrange("p (a b) -> p a b", b=128), AF.Identity),
                    waits=[("sT", ct)], inc="sS")
                cv = p.emit("vector", lambda e, dl=dst_l, dh=dst_h, bnk=bnk: e.tensor_tensor(
                    dl, bank(bnk).rearrange("p (a b) -> p a b", b=128), dh, op=ALU.subtract),
                    waits=[("sS", cs)], inc="sV")
                bank_free[bnk] = [("sV", cv)]
        split_rdy = [("sV", p.cnt["sV"]), ("sS", p.cnt["sS"])]
        for ch, (whi, wlo, pb) in enumerate([(wk_hi, wk_lo, 4), (wv_hi, wv_lo, 5)]):
            ops = [(whi, xnT_hi), (whi, xnT_lo), (wlo, xnT_hi)]
            cm = None
            n_ops = 16 * 3
            k = 0
            for mt in range(16):
                for (la, hb) in ops:
                    first, last = (k == 0), (k == n_ops - 1)
                    extra = (st["wkv"] + split_rdy + bank_free[pb]) if first else []
                    cm = p.emit("tensor",
                                lambda e, la=la, hb=hb, mt=mt, pb=pb, first=first, last=last: e.matmul(
                                    bank(pb), la[:, mt, :], hb[:, mt, :],
                                    start=first, stop=last, **MMKW),
                                waits=extra, inc="sT")
                    k += 1
            dst = kT_f if ch == 0 else vT_f
            cs = p.emit("scalar",
                        lambda e, dst=dst, w=w, pb=pb, ch=ch: e.activation(
                            dst[:, w * 512:(w + 1) * 512], bank(pb), AF.Identity, bias=bkv[:, ch:ch + 1]),
                        waits=[("sT", cm)], inc="sS")
            bank_free[pb] = [("sS", cs)]
        xnT_free = [("sT", p.cnt["sT"])]
    kv_done = [("sS", p.cnt["sS"])]
    cS1 = p.emit("scalar", lambda e: e.activation(kT_hi, kT_f, AF.Identity), waits=kv_done, inc="sS")
    cV1 = p.emit("vector", lambda e: e.tensor_tensor(kT_lo, kT_f, kT_hi, op=ALU.subtract),
                 waits=[("sS", cS1)], inc="sV")
    cS2 = p.emit("scalar", lambda e: e.activation(vT_hi, vT_f, AF.Identity), waits=kv_done, inc="sS")
    cV2 = p.emit("vector", lambda e: e.tensor_tensor(vT_lo, vT_f, vT_hi, op=ALU.subtract),
                 waits=[("sS", cS2)], inc="sV")
    kv_split = [("sV", cV2)]
    ar.pop()
    ar.push()
    vtok_b = ar.alloc(2 * S, BF16)
    vtok_hi = vtok_b[:, 0:S].rearrange("p (t d) -> p t d", d=128)
    vtok_lo = vtok_b[:, S:2 * S].rearrange("p (t d) -> p t d", d=128)
    for r in range(8):
        bnk = 6 + (r % 2)
        ct = None
        for j in range(4):
            it = r * 4 + j
            ct = p.emit("tensor",
                        lambda e, it=it, bnk=bnk, j=j: e.transpose(
                            bank(bnk)[:, j * 128:(j + 1) * 128],
                            vT_f[:, it * 128:(it + 1) * 128], ident),
                        waits=kv_done + bank_free[bnk], inc="sT")
        dh = vtok_hi[:, r * 4:(r + 1) * 4, :]
        dl = vtok_lo[:, r * 4:(r + 1) * 4, :]
        cs = p.emit("scalar", lambda e, dh=dh, bnk=bnk: e.activation(
            dh, bank(bnk).rearrange("p (a b) -> p a b", b=128), AF.Identity),
            waits=[("sT", ct)], inc="sS")
        cv = p.emit("vector", lambda e, dl=dl, dh=dh, bnk=bnk: e.tensor_tensor(
            dl, bank(bnk).rearrange("p (a b) -> p a b", b=128), dh, op=ALU.subtract),
            waits=[("sS", cs)], inc="sV")
        bank_free[bnk] = [("sV", cv)]
    vtok_rdy = [("sV", p.cnt["sV"])]

    pq_f = [ar.alloc(512, F32) for _ in range(2)]
    pq_h = [ar.alloc(512, BF16) for _ in range(2)]
    pq_l = [ar.alloc(512, BF16) for _ in range(2)]
    rrow = ar.alloc(D, F32)
    pq_free = [[], []]
    for qc in range(4):
        ob = bank_free[2]
        rb = bank_free[3]
        c_o, c_r = None, None
        n_it = 32
        for it in range(n_it):
            par = it % 2
            ct = None
            sops = [(kT_hi, vT_hi), (kT_hi, vT_lo), (kT_lo, vT_hi)]
            for k, (la, rbs) in enumerate(sops):
                ct = p.emit("tensor",
                            lambda e, la=la, rbs=rbs, it=it, qc=qc, par=par,
                            first=(k == 0), last=(k == 2): e.matmul(
                                bank(par), la[:, it * 128:(it + 1) * 128],
                                rbs[:, qc * 512:(qc + 1) * 512],
                                start=first, stop=last, **MMKW),
                            waits=(kv_split + bank_free[par] + pq_free[par]) if k == 0 else [],
                            inc="sT")
            cs = p.emit("scalar", lambda e, par=par: e.activation(pq_f[par], bank(par), AF.Exp,
                                                                  scale=c128[:, 0:1]),
                        waits=[("sT", ct)], inc="sS")
            bank_free[par] = [("sS", cs)]
            cv1 = p.emit("vector", lambda e, par=par: e.tensor_copy(pq_h[par], pq_f[par]),
                         waits=[("sS", cs)], inc="sV")
            cv2 = p.emit("vector", lambda e, par=par: e.tensor_tensor(pq_l[par], pq_f[par], pq_h[par],
                                                                      op=ALU.subtract),
                         waits=[("sV", cv1)], inc="sV")
            aops = [(vtok_hi, pq_h), (vtok_hi, pq_l), (vtok_lo, pq_h)]
            for k, (la, hb) in enumerate(aops):
                c_o = p.emit("tensor",
                             lambda e, la=la, hb=hb, it=it, par=par,
                             first=(c_o is None), last=(it == n_it - 1 and k == 2): e.matmul(
                                 bank(2), la[:, it, :], hb[par], start=first, stop=last, **MMKW),
                             waits=(vtok_rdy + [("sV", cv2)] + (ob if c_o is None else [])),
                             inc="sT")
            for k, hb in enumerate((pq_h, pq_l)):
                c_r = p.emit("tensor",
                             lambda e, hb=hb, par=par, first=(c_r is None),
                             last=(it == n_it - 1 and k == 1): e.matmul(
                                 bank(3)[0:1, :], ones_c, hb[par], start=first, stop=last, **MMKW),
                             waits=(rb if c_r is None else []), inc="sT")
            pq_free[par] = [("sT", p.cnt["sT"])]
        cs = p.emit("scalar", lambda e, qc=qc: e.activation(oT_h[:, qc * 512:(qc + 1) * 512],
                                                            bank(2), AF.Identity),
                    waits=[("sT", c_o)], inc="sS")
        cv = p.emit("vector", lambda e, qc=qc: e.tensor_tensor(
            oT_l[:, qc * 512:(qc + 1) * 512], bank(2), oT_h[:, qc * 512:(qc + 1) * 512], op=ALU.subtract),
            waits=[("sS", cs)], inc="sV")
        bank_free[2] = [("sV", cv)]
        cvr = p.emit("vector", lambda e, qc=qc: e.tensor_copy(rrow[0:1, qc * 512:(qc + 1) * 512],
                                                              bank(3)[0:1, :]),
                     waits=[("sT", c_r)], inc="sV")
        bank_free[3] = [("sV", cvr)]
    oT_rdy = [("sS", p.cnt["sS"]), ("sV", p.cnt["sV"])]
    ct = None
    for j in range(16):
        ct = p.emit("tensor",
                    lambda e, j=j: e.transpose(bank(6)[:, j:j + 1], rrow[0:1, j * 128:(j + 1) * 128],
                                               ident[0:1, 0:1]),
                    waits=oT_rdy + bank_free[6], inc="sT")
    cv = p.emit("vector", lambda e: e.reciprocal(rinv, bank(6)[:, 0:16]), waits=[("sT", ct)], inc="sV")
    bank_free[6] = [("sV", cv)]
    st["rinv"] = [("sV", cv)]
    if "oT_h" in taps:
        p.dma_tap(taps["oT_h"], oT_h, waits=oT_rdy)
    if "oT_l" in taps:
        p.dma_tap(taps["oT_l"], oT_l, waits=oT_rdy)
    if "rinv" in taps:
        p.dma_tap(taps["rinv"], rinv, waits=st["rinv"])
    if "kT" in taps:
        p.dma_tap(taps["kT"], kT_f, waits=kv_done)
    if "vT" in taps:
        p.dma_tap(taps["vT"], vT_f, waits=kv_done)
    if "woe" in taps:
        p.dma_tap(taps["woe"], woe_f, waits=st["woe"])
    ar.pop()
    ar.pop()

    fenceA = p.fence()
    ar.push()
    amax_buf = ar.alloc(4096, F32)
    s2T_gu = ar.alloc(II, F32)
    s2T_d = s2T_gu[:, 0:2 * D]
    raw2 = [ar.alloc(2048, F32) for _ in range(2)]
    rep_r = [ar.alloc(2048, F32) for _ in range(2)]
    rep_s = [ar.alloc(2048, F32) for _ in range(2)]
    wdq = [ar.alloc(2048, F32) for _ in range(2)]
    whi = [ar.alloc(2048, BF16) for _ in range(2)]
    wlo = [ar.alloc(2048, BF16) for _ in range(2)]
    q8w = [w.bitcast(FP8)[:, 0:2048] for w in wlo]

    gn_r = gn_h.rearrange("(n p) m -> n p m", p=128)
    un_r = un_h.rearrange("(n p) m -> n p m", p=128)
    dn_r = dn_h.rearrange("(n p) m -> n p m", p=128)
    gt_r = gt_h.rearrange("(n p) m -> n p m", p=128)
    ut_r = ut_h.rearrange("(n p) m -> n p m", p=128)
    dt_r = dt_h.rearrange("(n p) m -> n p m", p=128)
    scg_r, scu_r, scd_r = sc_g, sc_u, sc_d

    rawgate = [list(fenceA), list(fenceA)]
    repgate_r = [list(fenceA), list(fenceA)]
    repgate_s = [list(fenceA), list(fenceA)]
    pipe_q8 = [[], []]
    pipe_wdq = [[], []]
    pipe_whi = [[], []]
    pipe_wlo = [[], []]

    def quant_weight(nat_r, trn_r, sc_r, widx):
        is_d = (widx == 2)
        amax_v = amax_buf.rearrange("p (t b) -> p t b", b=(256 if is_d else 64))
        am_gate = [("sT", p.cnt["sT"])]
        last_am = 0
        for j in range(16 if is_d else 64):
            par = j % 2
            if not is_d:
                cy = p.dma("sync", f"sW{par}", raw2[par][:, 0:2048], nat_r[j], waits=rawgate[par])
                cv = p.emit("vector",
                            lambda e, j=j, par=par: e.reduce_max(
                                amax_v[:, j, :], raw2[par][:, 0:2048].rearrange("p (b k) -> p b k", k=32),
                                axis=AX, apply_absolute_value=True),
                            waits=[(f"sW{par}", cy)] + am_gate, inc="sV")
                rawgate[par] = [("sV", cv)]
            else:
                for hh in range(4):
                    par = (j * 4 + hh) % 2
                    cy = p.dma("sync", f"sW{par}", raw2[par], nat_r[j][:, hh * 2048:(hh + 1) * 2048],
                               waits=rawgate[par])
                    cv = p.emit("vector",
                                lambda e, j=j, hh=hh, par=par: e.reduce_max(
                                    amax_v[:, j, hh * 64:(hh + 1) * 64],
                                    raw2[par].rearrange("p (b k) -> p b k", k=32),
                                    axis=AX, apply_absolute_value=True),
                                waits=[(f"sW{par}", cy)] + am_gate, inc="sV")
                    rawgate[par] = [("sV", cv)]
            last_am = p.cnt["sV"]
        cv = p.emit("vector",
                    lambda e: e.tensor_scalar(amax_buf, amax_buf, 1.0 / 224.0, 2e-30,
                                              op0=ALU.mult, op1=ALU.max),
                    waits=[("sV", last_am)], inc="sV")
        if not is_d:
            prow = 64 * widx
            tgate = [("sRS0", p.cnt["sRS0"]), ("sRS1", p.cnt["sRS1"])]
            for r in range(16):
                bnk = r % 2
                ct = None
                for j in range(4):
                    tj = r * 4 + j
                    ct = p.emit("tensor",
                                lambda e, tj=tj, bnk=bnk, j=j: e.transpose(
                                    bank(bnk)[0:64, j * 128:(j + 1) * 128], amax_v[:, tj, :], ident),
                                waits=[("sV", cv)] + bank_free[bnk] + id_rdy, inc="sT")
                cvv = p.emit("vector",
                             lambda e, r=r, bnk=bnk, prow=prow: e.tensor_copy(
                                 s2T_gu[prow:prow + 64, r * 512:(r + 1) * 512], bank(bnk)[0:64, :]),
                             waits=[("sT", ct)] + tgate, inc="sV")
                bank_free[bnk] = [("sV", cvv)]
            cvr = p.emit("vector", lambda e: e.engine_nop(),
                         waits=[("sV", p.cnt["sV"])], inc="sV")
        else:
            tgate = [("sRS0", p.cnt["sRS0"]), ("sRS1", p.cnt["sRS1"])]
            for j in range(16):
                bnk = j % 2
                ct0 = p.emit("tensor",
                             lambda e, j=j, bnk=bnk: e.transpose(
                                 bank(bnk)[:, 0:128], amax_v[:, j, 0:128], ident),
                             waits=[("sV", cv)] + bank_free[bnk] + id_rdy, inc="sT")
                ct1 = p.emit("tensor",
                             lambda e, j=j, bnk=bnk: e.transpose(
                                 bank(bnk)[:, 128:256], amax_v[:, j, 128:256], ident),
                             waits=[], inc="sT")
                cv1 = p.emit("vector",
                             lambda e, j=j, bnk=bnk: e.tensor_copy(
                                 s2T_d[:, j * 128:(j + 1) * 128], bank(bnk)[:, 0:128]),
                             waits=[("sT", ct1)] + tgate, inc="sV")
                cv2 = p.emit("vector",
                             lambda e, j=j, bnk=bnk: e.tensor_copy(
                                 s2T_d[:, D + j * 128: D + (j + 1) * 128], bank(bnk)[:, 128:256]),
                             waits=[("sT", ct1)], inc="sV")
                bank_free[bnk] = [("sV", cv2)]
            cvr = p.emit("vector", lambda e: e.engine_nop(),
                         waits=[("sV", p.cnt["sV"])], inc="sV")
        tbl = [("sV", cvr)]

        wq = 2048
        for j in range(64 if is_d else 16):
            for hh in range(1 if is_d else 4):
                par = (j * (1 if is_d else 4) + hh) % 2
                src_tile = trn_r[j] if is_d else trn_r[j][:, hh * wq:(hh + 1) * wq]
                cy = p.dma("sync", f"sW{par}", raw2[par][:, 0:wq], src_tile, waits=rawgate[par] + tbl)
                if not is_d:
                    r0 = 64 * widx + j * 4
                    src_s = s2T_gu[r0:r0 + 4, hh * 2048:(hh + 1) * 2048]
                else:
                    ib0 = j * 4
                    q_, r0 = ib0 // 128, ib0 % 128
                    src_s = s2T_d[r0:r0 + 4, q_ * D:(q_ + 1) * D]
                cg2 = p.dma("gpsimd", f"sRS{par}", rep_s[par], src_s.unsqueeze(1).broadcast_to([4, 32, wq]),
                            waits=tbl + repgate_s[par])
                crr = p.emit("vector",
                             lambda e, par=par: e.reciprocal(rep_r[par], rep_s[par]),
                             waits=[(f"sRS{par}", cg2)] + repgate_r[par], inc="sV")
                cq = p.emit("vector",
                            lambda e, par=par: e.tensor_tensor(
                                q8w[par], raw2[par], rep_r[par], op=ALU.mult),
                            waits=[(f"sW{par}", cy), ("sV", crr)] + pipe_q8[par] + pipe_wlo[par],
                            inc="sV")
                rawgate[par] = [("sV", cq)]
                repgate_r[par] = [("sV", cq)]
                cd = p.emit("vector",
                            lambda e, par=par: e.tensor_tensor(
                                wdq[par], q8w[par], rep_s[par], op=ALU.mult),
                            waits=[("sV", cq)] + pipe_wdq[par], inc="sV")
                repgate_s[par] = [("sV", cd)]
                pipe_q8[par] = [("sV", cd)]
                ch = p.emit("scalar",
                            lambda e, par=par: e.activation(whi[par], wdq[par], AF.Identity),
                            waits=[("sV", cd)] + pipe_whi[par], inc="sS")
                cl = p.emit("vector",
                            lambda e, par=par: e.tensor_tensor(
                                wlo[par], wdq[par], whi[par], op=ALU.subtract),
                            waits=[("sS", ch)], inc="sV")
                pipe_wdq[par] = [("sV", cl)]
                dsth = sc_r[j, 0, :, :] if is_d else sc_r[j, 0, :, hh * wq:(hh + 1) * wq]
                dstl = sc_r[j, 1, :, :] if is_d else sc_r[j, 1, :, hh * wq:(hh + 1) * wq]
                cw1 = p.dma("scalar", f"sWH{par}", dsth, whi[par], waits=[("sV", cl)])
                cw2 = p.dma("scalar", f"sWL{par}", dstl, wlo[par], waits=[("sV", cl)])
                pipe_whi[par] = [(f"sWH{par}", cw1)]
                pipe_wlo[par] = [(f"sWL{par}", cw2)]
        return [("sWH0", p.cnt["sWH0"]), ("sWH1", p.cnt["sWH1"]),
                ("sWL0", p.cnt["sWL0"]), ("sWL1", p.cnt["sWL1"])]

    wg_done = quant_weight(gn_r, gt_r, scg_r, 0)
    wu_done = quant_weight(un_r, ut_r, scu_r, 1)
    wd_done = quant_weight(dn_r, dt_r, scd_r, 2)
    ar.pop()

    fenceW = p.fence()
    ar.push()
    xhT_h = ar.alloc(16 * 512, BF16).rearrange("p (t q) -> p t q", q=512)
    xhT_l = ar.alloc(16 * 512, BF16).rearrange("p (t q) -> p t q", q=512)
    acc = ar.alloc(4 * D, F32).rearrange("p (t m) -> p t m", m=D)
    hT_h = ar.alloc(16 * 512, BF16).rearrange("p (t q) -> p t q", q=512)
    hT_l = ar.alloc(16 * 512, BF16).rearrange("p (t q) -> p t q", q=512)
    gu_buf = [[ar.alloc(2 * 2 * 512, BF16).rearrange("p (t q) -> p t q", q=512)
               for _ in range(2)] for _ in range(2)]
    d_buf = [ar.alloc(2 * 512, BF16).rearrange("p (h q) -> p h q", h=2) for _ in range(2)]
    xt_b = ar.alloc(D, F32)
    q8x = ar.alloc(D, FP8)
    xq_b = ar.alloc(D, F32)
    qs = ar.alloc(3 * 64, F32)
    h_b = ar.alloc(512, F32)
    sg_b = ar.alloc(512, F32)
    hq_b = ar.alloc(512, F32)
    q8h = ar.alloc(512, FP8)
    hs = ar.alloc(3 * 16, F32)
    fin = ar.alloc(D, F32)
    res_b = ar.alloc(D, F32)
    sq2_b = fin
    xn2_b = res_b

    out_r = out_h.rearrange("(n p) m -> n p m", p=128)
    gu_free = [list(fenceW), list(fenceW)]
    d_free = [list(fenceW), list(fenceW)]
    xt_gate = list(fenceW)
    fin_gate = list(fenceW)
    res_gate = list(fenceW)
    acc_gate = list(fenceW)
    sg_gate = []
    xq_readers = []

    for tc in range(4):
        for i in range(4):
            gt = tc * 4 + i
            cy = p.dma("sync", "sXB", xt_b, xown_r[gt], waits=xt_gate)
            cs_xn = emit_ln(xt_b, [("sXB", cy)], sq2_b, xn2_b)
            xt_gate = [("sS", cs_xn)]
            qa = p.emit("vector",
                        lambda e: e.reduce_max(qs[:, 0:64], xn2_b.rearrange("p (b k) -> p b k", k=32),
                                               axis=AX, apply_absolute_value=True),
                        waits=[("sS", cs_xn)], inc="sV")
            qb = p.emit("vector",
                        lambda e: e.tensor_scalar(qs[:, 64:128], qs[:, 0:64], 1.0 / 224.0, 2e-30,
                                                  op0=ALU.mult, op1=ALU.max),
                        waits=[("sV", qa)], inc="sV")
            qc_ = p.emit("vector", lambda e: e.reciprocal(qs[:, 128:192], qs[:, 64:128]),
                         waits=[("sV", qb)], inc="sV")
            qd = p.emit("vector",
                        lambda e: e.tensor_tensor(
                            q8x.rearrange("p (b k) -> p b k", k=32),
                            xn2_b.rearrange("p (b k) -> p b k", k=32),
                            qs[:, 128:192].unsqueeze(2).broadcast_to([128, 64, 32]), op=ALU.mult),
                        waits=[("sV", qc_)], inc="sV")
            qe = p.emit("vector",
                        lambda e: e.tensor_tensor(
                            xq_b.rearrange("p (b k) -> p b k", k=32),
                            q8x.rearrange("p (b k) -> p b k", k=32),
                            qs[:, 64:128].unsqueeze(2).broadcast_to([128, 64, 32]), op=ALU.mult),
                        waits=[("sV", qd)], inc="sV")
            for r in range(4):
                bnk = r % 2
                ct = None
                for jj in range(4):
                    mt = r * 4 + jj
                    ct = p.emit("tensor",
                                lambda e, mt=mt, bnk=bnk, jj=jj: e.transpose(
                                    bank(bnk)[:, jj * 128:(jj + 1) * 128],
                                    xq_b[:, mt * 128:(mt + 1) * 128], ident),
                                waits=[("sV", qe)] + bank_free[bnk] + xq_readers, inc="sT")
                dh = xhT_h[:, r * 4:(r + 1) * 4, i * 128:(i + 1) * 128]
                dl = xhT_l[:, r * 4:(r + 1) * 4, i * 128:(i + 1) * 128]
                cs = p.emit("scalar", lambda e, dh=dh, bnk=bnk: e.activation(
                    dh, bank(bnk).rearrange("p (a b) -> p a b", b=128), AF.Identity),
                    waits=[("sT", ct)], inc="sS")
                cvv = p.emit("vector", lambda e, dl=dl, dh=dh, bnk=bnk: e.tensor_tensor(
                    dl, bank(bnk).rearrange("p (a b) -> p a b", b=128), dh, op=ALU.subtract),
                    waits=[("sS", cs)], inc="sV")
                bank_free[bnk] = [("sV", cvv)]
        xq_rdy = [("sV", p.cnt["sV"]), ("sS", p.cnt["sS"])]
        xq_readers = []
        if "xhT_h" in taps and tc == 0:
            p.dma_tap(taps["xhT_h"], xhT_h.rearrange("p t q -> p (t q)"), waits=xq_rdy)
            xq_readers = xq_readers + [("sTAP", p.cnt["sTAP"])]

        for g in range(4):
            hT_last = []
            for jic in range(4):
                ic = g * 4 + jic
                mm_gate_last = [None] * 4
                mm_up_last = [None] * 4
                for qtr in range(8):
                    par = qtr % 2
                    cyg = p.dma(
                        "sync", f"sGU{par}g", gu_buf[par][0],
                        scg_r[qtr * 2:(qtr + 1) * 2, :, :, ic * 512:(ic + 1) * 512]
                        .rearrange("a h p q -> p (a h) q"),
                        waits=gu_free[par] + wg_done + xq_rdy)
                    cyu = p.dma(
                        "sync", f"sGU{par}u", gu_buf[par][1],
                        scu_r[qtr * 2:(qtr + 1) * 2, :, :, ic * 512:(ic + 1) * 512]
                        .rearrange("a h p q -> p (a h) q"),
                        waits=gu_free[par] + wu_done)
                    ld_done2 = [(f"sGU{par}g", cyg), (f"sGU{par}u", cyu)]
                    for mtl in range(2):
                        mt = qtr * 2 + mtl
                        for tsub in range(4):
                            gb, ub = tsub, 4 + tsub
                            ops6 = ((xhT_h, gu_buf[par][0], 0, gb, mm_gate_last, False),
                                    (xhT_h, gu_buf[par][0], 1, gb, mm_gate_last, False),
                                    (xhT_l, gu_buf[par][0], 0, gb, mm_gate_last, True),
                                    (xhT_h, gu_buf[par][1], 0, ub, mm_up_last, False),
                                    (xhT_h, gu_buf[par][1], 1, ub, mm_up_last, False),
                                    (xhT_l, gu_buf[par][1], 0, ub, mm_up_last, True))
                            for (la, wvv, hl, pb, rec, is3) in ops6:
                                first = rec[tsub] is None
                                last = (qtr == 7 and mtl == 1 and is3)
                                ws = list(ld_done2) + xq_rdy
                                if first:
                                    ws += bank_free[pb]
                                rec[tsub] = p.emit(
                                    "tensor",
                                    lambda e, la=la, wvv=wvv, hl=hl, pb=pb, mt=mt, mtl=mtl, tsub=tsub,
                                    first=first, last=last: e.matmul(
                                        bank(pb), la[:, mt, tsub * 128:(tsub + 1) * 128],
                                        wvv[:, mtl * 2 + hl, :], start=first, stop=last, **MMKW),
                                    waits=ws, inc="sT")
                    gu_free[par] = [("sT", p.cnt["sT"])]
                for tsub in range(4):
                    gb, ub = tsub, 4 + tsub
                    e1 = p.emit("scalar", lambda e, gb=gb: e.activation(sg_b, bank(gb), AF.Silu),
                                waits=[("sT", mm_gate_last[tsub])] + sg_gate, inc="sS")
                    e2 = p.emit("vector", lambda e, ub=ub: e.tensor_tensor(h_b, bank(ub), sg_b, op=ALU.mult),
                                waits=[("sS", e1), ("sT", mm_up_last[tsub])], inc="sV")
                    sg_gate = [("sV", e2)]
                    bank_free[ub] = [("sV", e2)]
                    q1 = p.emit("vector",
                                lambda e: e.reduce_max(hs[:, 0:16], h_b.rearrange("p (b k) -> p b k", k=32),
                                                       axis=AX, apply_absolute_value=True),
                                waits=[("sV", e2)], inc="sV")
                    q2 = p.emit("vector",
                                lambda e: e.tensor_scalar(hs[:, 16:32], hs[:, 0:16], 1.0 / 224.0, 2e-30,
                                                          op0=ALU.mult, op1=ALU.max),
                                waits=[("sV", q1)], inc="sV")
                    q3 = p.emit("vector", lambda e: e.reciprocal(hs[:, 32:48], hs[:, 16:32]),
                                waits=[("sV", q2)], inc="sV")
                    q4 = p.emit("vector",
                                lambda e: e.tensor_tensor(
                                    q8h.rearrange("p (b k) -> p b k", k=32),
                                    h_b.rearrange("p (b k) -> p b k", k=32),
                                    hs[:, 32:48].unsqueeze(2).broadcast_to([128, 16, 32]), op=ALU.mult),
                                waits=[("sV", q3)], inc="sV")
                    q5 = p.emit("vector",
                                lambda e: e.tensor_tensor(
                                    hq_b.rearrange("p (b k) -> p b k", k=32),
                                    q8h.rearrange("p (b k) -> p b k", k=32),
                                    hs[:, 16:32].unsqueeze(2).broadcast_to([128, 16, 32]), op=ALU.mult),
                                waits=[("sV", q4)], inc="sV")
                    ct = None
                    for jj in range(4):
                        ct = p.emit("tensor",
                                    lambda e, jj=jj, gb=gb: e.transpose(
                                        bank(gb)[:, jj * 128:(jj + 1) * 128],
                                        hq_b[:, jj * 128:(jj + 1) * 128], ident),
                                    waits=[("sV", q5)] + bank_free[gb], inc="sT")
                    it0 = jic * 4
                    dh = hT_h[:, it0:it0 + 4, tsub * 128:(tsub + 1) * 128]
                    dl = hT_l[:, it0:it0 + 4, tsub * 128:(tsub + 1) * 128]
                    cs = p.emit("scalar", lambda e, dh=dh, gb=gb: e.activation(
                        dh, bank(gb).rearrange("p (a b) -> p a b", b=128), AF.Identity),
                        waits=[("sT", ct)], inc="sS")
                    cvv = p.emit("vector", lambda e, dl=dl, dh=dh, gb=gb: e.tensor_tensor(
                        dl, bank(gb).rearrange("p (a b) -> p a b", b=128), dh, op=ALU.subtract),
                        waits=[("sS", cs)], inc="sV")
                    bank_free[gb] = [("sV", cvv)]
            hT_all = [("sV", p.cnt["sV"]), ("sS", p.cnt["sS"])]
            for mch in range(4):
                dmm_last = [None] * 4
                for it in range(16):
                    par = it % 2
                    cy1 = p.dma("sync", f"sD{par}", d_buf[par],
                                scd_r[g * 16 + it, :, :, mch * 512:(mch + 1) * 512]
                                .rearrange("h p q -> p h q"),
                                waits=d_free[par] + wd_done)
                    for tsub in range(4):
                        pb = 4 + tsub
                        dops = ((hT_h, 0, False), (hT_h, 1, False), (hT_l, 0, True))
                        for (lah, hl, is3) in dops:
                            first = dmm_last[tsub] is None
                            last = (it == 15 and is3)
                            ws = [(f"sD{par}", cy1)] + hT_all
                            if first:
                                ws += bank_free[pb]
                            dmm_last[tsub] = p.emit(
                                "tensor",
                                lambda e, lah=lah, hl=hl, par=par, it=it, tsub=tsub, pb=pb,
                                first=first, last=last: e.matmul(
                                    bank(pb), lah[:, it, tsub * 128:(tsub + 1) * 128], d_buf[par][:, hl, :],
                                    start=first, stop=last, **MMKW),
                                waits=ws, inc="sT")
                    d_free[par] = [("sT", p.cnt["sT"])]
                for tsub in range(4):
                    pb = 4 + tsub
                    if g == 0:
                        cvx = p.emit("vector",
                                     lambda e, tsub=tsub, mch=mch, pb=pb: e.tensor_copy(
                                         acc[:, tsub, mch * 512:(mch + 1) * 512], bank(pb)),
                                     waits=[("sT", dmm_last[tsub])] + acc_gate, inc="sV")
                    else:
                        cvx = p.emit("vector",
                                     lambda e, tsub=tsub, mch=mch, pb=pb: e.tensor_tensor(
                                         acc[:, tsub, mch * 512:(mch + 1) * 512],
                                         acc[:, tsub, mch * 512:(mch + 1) * 512],
                                         bank(pb), op=ALU.add),
                                     waits=[("sT", dmm_last[tsub])], inc="sV")
                    bank_free[pb] = [("sV", cvx)]
        acc_done = [("sV", p.cnt["sV"])]
        if "acc" in taps and tc == 0:
            p.dma_tap(taps["acc"], acc.rearrange("p t m -> p (t m)"), waits=acc_done)
            acc_gate = acc_gate + [("sTAP", p.cnt["sTAP"])]

        for tsub in range(4):
            j = tc * 4 + tsub
            cmt = None
            for mch in range(4):
                pb = mch
                wops = ((oT_h, woe_h), (oT_h, woe_l), (oT_l, woe_h))
                for k, (la, rb2) in enumerate(wops):
                    ws = []
                    if k == 0:
                        ws = list(bank_free[pb]) + st["woe"] + oT_rdy
                    cmt = p.emit("tensor",
                                 lambda e, la=la, rb2=rb2, j=j, mch=mch, pb=pb,
                                 first=(k == 0), last=(k == 2): e.matmul(
                                     bank(pb), la[:, j * 128:(j + 1) * 128],
                                     rb2[:, mch * 512:(mch + 1) * 512],
                                     start=first, stop=last, **MMKW),
                                 waits=ws, inc="sT")
            cy = p.dma("scalar", "sFIN", fin, xown_r[j], waits=fin_gate)
            cs = p.emit("scalar",
                        lambda e, j=j: e.activation(res_b, PS[:, 0:2048], AF.Identity,
                                                    scale=rinv[:, j:j + 1]),
                        waits=[("sT", cmt)] + st["rinv"] + res_gate, inc="sS")
            for b_ in range(4):
                bank_free[b_] = [("sS", cs)]
            cv1 = p.emit("vector", lambda e: e.tensor_add(res_b, res_b, bo_rep),
                         waits=[("sS", cs), ("sBO", cG_bo)], inc="sV")
            cv2 = p.emit("vector", lambda e: e.tensor_add(res_b, res_b, fin),
                         waits=[("sV", cv1), ("sFIN", cy)], inc="sV")
            cv3 = p.emit("vector",
                         lambda e, tsub=tsub: e.tensor_add(res_b, res_b, acc[:, tsub, :]),
                         waits=[("sV", cv2)] + acc_done, inc="sV")
            cg = p.dma("gpsimd", "sOUT", out_r[j], res_b, waits=[("sV", cv3)])
            fin_gate = [("sV", cv2)]
            res_gate = [("sOUT", cg)]
        acc_gate = [("sOUT", p.cnt["sOUT"])]
        xq_readers = [("sT", p.cnt["sT"]), ("sTAP", p.cnt["sTAP"])]
        xt_gate = xt_gate + [("sOUT", p.cnt["sOUT"]), ("sFIN", p.cnt["sFIN"]),
                             ("sV", p.cnt["sV"])]

    p.emit("gpsimd", lambda e: e.nop(), waits=[("sOUT", p.cnt["sOUT"])])

    with nc.Block() as blk:
        p.build(blk)
    return nc


def kernel(**inputs):
    x = np.ascontiguousarray(inputs["L0_attn_q"], dtype=np.float32)
    wk = np.ascontiguousarray(inputs["L0_attn_Wk"], dtype=np.float32)
    wv = np.ascontiguousarray(inputs["L0_attn_Wv"], dtype=np.float32)
    bk = np.ascontiguousarray(inputs["L0_attn_bk"], dtype=np.float32)
    bv = np.ascontiguousarray(inputs["L0_attn_bv"], dtype=np.float32)
    wo = np.ascontiguousarray(inputs["L0_attn_Wo"], dtype=np.float32)
    bo = np.ascontiguousarray(inputs["L0_attn_bo"], dtype=np.float32)
    gw = np.ascontiguousarray(inputs["L0_ffn_gate_w"], dtype=np.float32)
    uw = np.ascontiguousarray(inputs["L0_ffn_up_w"], dtype=np.float32)
    dw = np.ascontiguousarray(inputs["L0_ffn_down_w"], dtype=np.float32)
    gwT = np.ascontiguousarray(gw.T)
    uwT = np.ascontiguousarray(uw.T)
    dwT = np.ascontiguousarray(dw.T)
    ident = np.eye(128, dtype=np.float32)

    nc = build_program()
    in_maps = []
    for c in range(NCORE):
        b, h = c // 2, c % 2
        in_maps.append({
            "x_own": x[b, h * TOK:(h + 1) * TOK],
            "x_oth": x[b, (1 - h) * TOK:(2 - h) * TOK],
            "wk": wk, "wv": wv, "bk": bk, "bv": bv, "wo": wo, "bo": bo,
            "g_nat": gw, "u_nat": uw, "d_nat": dw,
            "g_t": gwT, "u_t": uwT, "d_t": dwT,
            "ident": ident,
        })
    res = run_bass_kernel_spmd(nc, in_maps, core_ids=list(range(NCORE)))
    out = np.empty((4, S, D), dtype=np.float32)
    for c in range(NCORE):
        b, h = c // 2, c % 2
        out[b, h * TOK:(h + 1) * TOK] = res.results[c]["out"]
    return out


# version 03
# speedup vs baseline: 1.8398x; 1.3070x over previous
import numpy as np
import concourse.bass as bass
import concourse.mybir as mybir
from concourse.bass_utils import run_bass_kernel_spmd

F32 = mybir.dt.float32
BF16 = mybir.dt.bfloat16
FP8 = mybir.dt.float8e4
I32 = mybir.dt.int32
ALU = mybir.AluOpType
AF = mybir.ActivationFunctionType
AX = mybir.AxisListType.X

TOK = 2048
S = 4096
D = 2048
II = 8192
NCORE = 8
ENGS = ("tensor", "vector", "scalar", "sync", "gpsimd")

DEBUG_TAPS = []


class PB:

    def __init__(self, nc):
        self.nc = nc
        self.ev = {e: [] for e in ENGS}
        self.sem = {}
        self.cnt = {}
        self.wm = {e: {} for e in ENGS}

    def add_sem(self, name):
        self.sem[name] = self.nc.alloc_semaphore(name)
        self.cnt[name] = 0

    def emit(self, eng, fn, waits=(), inc=None, by=1):
        w2 = []
        for s, v in waits:
            if not v or v <= 0:
                continue
            if v > self.wm[eng].get(s, 0):
                w2.append((s, v))
                self.wm[eng][s] = v
        self.ev[eng].append((fn, w2, inc, by))
        if inc is not None:
            self.cnt[inc] += by
            return self.cnt[inc]
        return None

    def dma(self, eng, sem, out, in_, waits=(), **kw):
        return self.emit(eng, lambda e, o=out, i=in_, kw=kw: e.dma_start(o, i, **kw),
                         waits=waits, inc=sem, by=16)

    def dma_tap(self, out, in_, waits=()):
        w = list(waits) + ([("sTAP", self.cnt["sTAP"])] if self.cnt["sTAP"] else [])
        return self.dma("gpsimd", "sTAP", out, in_, waits=w)

    def fence(self):
        return [(s, c) for s, c in self.cnt.items() if c > 0]

    def build(self, blk):
        for eng in ENGS:
            evs = self.ev[eng]
            if not evs:
                continue
            sec = getattr(blk, eng)

            def body(e, evs=evs):
                for fn, waits, inc, by in evs:
                    for s, v in waits:
                        e.wait_ge(self.sem[s], v)
                    ins = fn(e)
                    if inc is not None:
                        ins.then_inc(self.sem[inc], by)
            sec(body)


class Arena:
    def __init__(self, t, nbytes):
        self.t = t
        self.cap = nbytes
        self.off = 0
        self.marks = []

    def alloc(self, n_elem, dt):
        esz = {F32: 4, BF16: 2, FP8: 1}[dt]
        self.off = (self.off + 63) & ~63
        start = self.off
        self.off += n_elem * esz
        assert self.off <= self.cap, f"arena overflow {self.off} > {self.cap}"
        if dt == F32:
            return self.t[:, start // 4: start // 4 + n_elem]
        return self.t.bitcast(dt)[:, start // esz: start // esz + n_elem]

    def push(self):
        self.marks.append(self.off)

    def pop(self):
        self.off = self.marks.pop()


def build_program():
    nc = bass.Bass("TRN2", target_bir_lowering=False)
    p = PB(nc)
    for s_ in ("sT", "sV", "sS", "sID", "sXTA", "sWO", "sWKV", "sW0", "sW1",
               "sGU0g", "sGU0u", "sGU1g", "sGU1u", "sD0", "sD1", "sXB",
               "sWH0", "sWH1", "sWL0", "sWL1", "sFIN", "sBO", "sRS0", "sRS1",
               "sOUT", "sTAP"):
        p.add_sem(s_)

    dr = lambda n, sh, dt, k: nc.dram_tensor(n, sh, dt, kind=k).ap()
    x_own = dr("x_own", [TOK, D], F32, "ExternalInput")
    x_oth = dr("x_oth", [TOK, D], F32, "ExternalInput")
    wk_h = dr("wk", [D, 128], F32, "ExternalInput")
    wv_h = dr("wv", [D, 128], F32, "ExternalInput")
    bk_h = dr("bk", [128], F32, "ExternalInput")
    bv_h = dr("bv", [128], F32, "ExternalInput")
    wo_h = dr("wo", [D, D], F32, "ExternalInput")
    bo_h = dr("bo", [D], F32, "ExternalInput")
    gn_h = dr("g_nat", [II, D], F32, "ExternalInput")
    un_h = dr("u_nat", [II, D], F32, "ExternalInput")
    dn_h = dr("d_nat", [D, II], F32, "ExternalInput")
    gt_h = dr("g_t", [D, II], F32, "ExternalInput")
    ut_h = dr("u_t", [D, II], F32, "ExternalInput")
    dt_h = dr("d_t", [II, D], F32, "ExternalInput")
    id_h = dr("ident", [128, 128], F32, "ExternalInput")
    selgu_h = dr("selgu", [64, 16, 128], F32, "ExternalInput")
    seld_h = dr("seld", [128, 32, 128], F32, "ExternalInput")
    out_h = dr("out", [TOK, D], F32, "ExternalOutput")
    sc_g = dr("sc_g", [16, 2, 128, II], BF16, "Internal")
    sc_u = dr("sc_u", [16, 2, 128, II], BF16, "Internal")
    sc_d = dr("sc_d", [64, 2, 128, D], BF16, "Internal")
    taps = {nm: dr("tap_" + nm, sh, dt, "ExternalOutput") for nm, sh, dt in DEBUG_TAPS}

    ident = nc.alloc_sbuf_tensor("ident_s", [128, 128], F32).ap()
    bo_rep = nc.alloc_sbuf_tensor("bo_rep", [128, D], F32).ap()
    woe_h = nc.alloc_sbuf_tensor("woe_h", [128, D], BF16).ap()
    woe_l = nc.alloc_sbuf_tensor("woe_l", [128, D], BF16).ap()
    oT_h = nc.alloc_sbuf_tensor("oT_h", [128, TOK], BF16).ap()
    oT_l = nc.alloc_sbuf_tensor("oT_l", [128, TOK], BF16).ap()
    rinv = nc.alloc_sbuf_tensor("rinv", [128, 16], F32).ap()
    ones_c = nc.alloc_sbuf_tensor("ones_c", [128, 1], BF16).ap()
    c128 = nc.alloc_sbuf_tensor("c128", [128, 1], F32).ap()
    trig = nc.alloc_sbuf_tensor("trig", [128, 2], F32).ap()
    smallv = nc.alloc_sbuf_tensor("smallv", [128, 16], F32).ap()
    AR_B = 162176
    ar = Arena(nc.alloc_sbuf_tensor("arena", [128, AR_B // 4], F32).ap(), AR_B)

    PS = nc.alloc_psum_tensor("PS", [128, 4096], F32).ap()
    bank = lambda i: PS[:, i * 512:(i + 1) * 512]
    bank_free = [[("sT", 0)] for _ in range(8)]

    MMKW = dict(skip_group_check=True)
    st = {}

    p.emit("vector", lambda e: e.memset(ones_c, 1.0))
    p.emit("vector", lambda e: e.memset(c128, 1.0 / 128.0))
    cV = p.emit("vector", lambda e: e.memset(trig, 2.0), inc="sV")
    p.emit("vector", lambda e: e.tensor_copy(trig[:, 1:2].bitcast(I32), trig[:, 0:1].bitcast(I32)),
           waits=[("sV", cV)], inc="sV")
    cID = p.dma("sync", "sID", ident, id_h)
    cG_bo = p.dma("gpsimd", "sBO", bo_rep, bo_h.unsqueeze(0).partition_broadcast(128))
    id_rdy = [("sID", cID)]

    ar.push()
    wo_all = ar.alloc(8 * D, F32)
    wo_v = wo_all.rearrange("p (c m) -> p c m", m=D)
    woe_f = ar.alloc(D, F32)
    wo_r = wo_h.rearrange("(c p) m -> p c m", p=128)
    half_done = []
    wo_gate = []
    for hf in range(2):
        cY = p.dma("sync", "sWO", wo_v, wo_r[:, hf * 8:(hf + 1) * 8, :], waits=wo_gate)
        cV = p.emit("vector", lambda e: e.tensor_add(
            wo_v[:, 0:4, :].rearrange("p a b -> p (a b)"), wo_v[:, 0:4, :].rearrange("p a b -> p (a b)"),
            wo_v[:, 4:8, :].rearrange("p a b -> p (a b)")), waits=[("sWO", cY)], inc="sV")
        cV = p.emit("vector", lambda e: e.tensor_add(
            wo_v[:, 0:2, :].rearrange("p a b -> p (a b)"), wo_v[:, 0:2, :].rearrange("p a b -> p (a b)"),
            wo_v[:, 2:4, :].rearrange("p a b -> p (a b)")), waits=[("sV", cV)], inc="sV")
        if hf == 0:
            cV = p.emit("vector", lambda e: e.tensor_add(woe_f, wo_v[:, 0, :], wo_v[:, 1, :]),
                        waits=[("sV", cV)], inc="sV")
        else:
            cV = p.emit("vector", lambda e: e.tensor_add(wo_v[:, 0, :], wo_v[:, 0, :], wo_v[:, 1, :]),
                        waits=[("sV", cV)], inc="sV")
            cV = p.emit("vector", lambda e: e.tensor_add(woe_f, woe_f, wo_v[:, 0, :]),
                        waits=[("sV", cV)], inc="sV")
        wo_gate = [("sV", cV)]
    cS = p.emit("scalar", lambda e: e.activation(woe_h, woe_f, AF.Identity),
                waits=[("sV", cV)], inc="sS")
    cV = p.emit("vector", lambda e: e.tensor_tensor(woe_l, woe_f, woe_h, op=ALU.subtract),
                waits=[("sS", cS)], inc="sV")
    st["woe"] = [("sV", cV), ("sS", cS)]
    ar.pop()
    ar.push()

    kTvT = ar.alloc(2 * S, F32)
    kT_f, vT_f = kTvT[:, 0:S], kTvT[:, S:2 * S]
    kv_b = ar.alloc(4 * S, BF16)
    kT_hi, kT_lo = kv_b[:, 0:S], kv_b[:, S:2 * S]
    vT_hi, vT_lo = kv_b[:, 2 * S:3 * S], kv_b[:, 3 * S:4 * S]
    ar.push()
    wkv_f = ar.alloc(2 * D, F32)
    wk_f = wkv_f[:, 0:D].rearrange("p (t d) -> p t d", d=128)
    wv_f = wkv_f[:, D:2 * D].rearrange("p (t d) -> p t d", d=128)
    wkv_b = ar.alloc(4 * D, BF16)
    wk_hi = wkv_b[:, 0 * D:1 * D].rearrange("p (t d) -> p t d", d=128)
    wk_lo = wkv_b[:, 1 * D:2 * D].rearrange("p (t d) -> p t d", d=128)
    wv_hi = wkv_b[:, 2 * D:3 * D].rearrange("p (t d) -> p t d", d=128)
    wv_lo = wkv_b[:, 3 * D:4 * D].rearrange("p (t d) -> p t d", d=128)
    bkv = ar.alloc(2, F32)
    cY1 = p.dma("sync", "sWKV", wk_f, wk_h.rearrange("(t p) d -> p t d", p=128),
                waits=st["woe"])
    cY2 = p.dma("sync", "sWKV", wv_f, wv_h.rearrange("(t p) d -> p t d", p=128),
                waits=[("sWKV", cY1)])
    cY3 = p.dma("sync", "sWKV", bkv[:, 0:1], bk_h.unsqueeze(1), waits=[("sWKV", cY2)])
    cY4 = p.dma("sync", "sWKV", bkv[:, 1:2], bv_h.unsqueeze(1), waits=[("sWKV", cY3)])
    cS1 = p.emit("scalar", lambda e: e.activation(
        wk_hi.rearrange("p a b -> p (a b)"), wkv_f[:, 0:D], AF.Identity), waits=[("sWKV", cY1)], inc="sS")
    cS2 = p.emit("scalar", lambda e: e.activation(
        wv_hi.rearrange("p a b -> p (a b)"), wkv_f[:, D:2 * D], AF.Identity), waits=[("sWKV", cY2)], inc="sS")
    cV1 = p.emit("vector", lambda e: e.tensor_tensor(
        wk_lo.rearrange("p a b -> p (a b)"), wkv_f[:, 0:D],
        wk_hi.rearrange("p a b -> p (a b)"), op=ALU.subtract), waits=[("sS", cS1)], inc="sV")
    cV2 = p.emit("vector", lambda e: e.tensor_tensor(
        wv_lo.rearrange("p a b -> p (a b)"), wkv_f[:, D:2 * D],
        wv_hi.rearrange("p a b -> p (a b)"), op=ALU.subtract), waits=[("sS", cS2)], inc="sV")
    st["wkv"] = [("sV", cV2), ("sWKV", cY4)]

    xt2 = [ar.alloc(D, F32)]
    xt2.append(xt2[0])
    sq_b = ar.alloc(D, F32)
    xn_b = ar.alloc(D, F32)
    xnT_hi = ar.alloc(16 * 512, BF16).rearrange("p (t q) -> p t q", q=512)
    xnT_lo = ar.alloc(16 * 512, BF16).rearrange("p (t q) -> p t q", q=512)
    xown_r = x_own.rearrange("(n p) m -> n p m", p=128)
    xoth_r = x_oth.rearrange("(n p) m -> n p m", p=128)

    def emit_ln(xt_ap, wa, sqb, xnb):
        rs = smallv
        c1 = p.emit("vector", lambda e: e.reduce_sum(rs[:, 0:1], xt_ap, axis=AX), waits=wa, inc="sV")
        c2 = p.emit("scalar", lambda e: e.activation(sqb, xt_ap, AF.Square, accum_out=rs[:, 2:3]),
                    waits=wa, inc="sS")
        c3 = p.emit("vector", lambda e: e.tensor_scalar(rs[:, 1:2], rs[:, 0:1], 1.0 / D, None, op0=ALU.mult),
                    waits=[("sV", c1)], inc="sV")
        c4 = p.emit("vector", lambda e: e.tensor_tensor(rs[:, 3:4], rs[:, 1:2], rs[:, 1:2], op=ALU.mult),
                    waits=[("sV", c3)], inc="sV")
        c5 = p.emit("vector", lambda e: e.tensor_scalar(rs[:, 4:5], rs[:, 2:3], 1.0 / D, 1e-5,
                                                        op0=ALU.mult, op1=ALU.add),
                    waits=[("sS", c2)], inc="sV")
        c6 = p.emit("vector", lambda e: e.tensor_tensor(rs[:, 5:6], rs[:, 4:5], rs[:, 3:4], op=ALU.subtract),
                    waits=[("sV", c5), ("sV", c4)], inc="sV")
        c7 = p.emit("scalar", lambda e: e.activation(rs[:, 6:7], rs[:, 5:6], AF.Sqrt),
                    waits=[("sV", c6)], inc="sS")
        c8 = p.emit("vector", lambda e: e.reciprocal(rs[:, 7:8], rs[:, 6:7]), waits=[("sS", c7)], inc="sV")
        c9 = p.emit("vector", lambda e: e.tensor_tensor(rs[:, 8:9], rs[:, 7:8], rs[:, 7:8], op=ALU.mult),
                    waits=[("sV", c8)], inc="sV")
        c10 = p.emit("vector", lambda e: e.tensor_tensor(rs[:, 9:10], rs[:, 8:9], rs[:, 5:6], op=ALU.mult),
                     waits=[("sV", c9)], inc="sV")
        c11 = p.emit("vector", lambda e: e.tensor_scalar(rs[:, 10:11], rs[:, 9:10], -0.5, 1.5,
                                                         op0=ALU.mult, op1=ALU.add),
                     waits=[("sV", c10)], inc="sV")
        c12 = p.emit("vector", lambda e: e.tensor_tensor(rs[:, 11:12], rs[:, 7:8], rs[:, 10:11], op=ALU.mult),
                     waits=[("sV", c11)], inc="sV")
        c15 = p.emit("vector", lambda e: e.tensor_tensor(rs[:, 13:14], rs[:, 1:2], rs[:, 11:12], op=ALU.mult),
                     waits=[("sV", c12)], inc="sV")
        c16 = p.emit("vector", lambda e: e.tensor_scalar(rs[:, 14:15], rs[:, 13:14], -1.0, None, op0=ALU.mult),
                     waits=[("sV", c15)], inc="sV")
        c17 = p.emit("scalar", lambda e: e.activation(xnb, xt_ap, AF.Identity,
                                                      bias=rs[:, 14:15], scale=rs[:, 11:12]),
                     waits=wa + [("sV", c16)], inc="sS")
        return c17


    xt_free = [[], []]
    xnT_free = [("sT", 0)]
    for w in range(8):
        for i in range(4):
            gtile = w * 4 + i
            src = xown_r[gtile] if gtile < 16 else xoth_r[gtile - 16]
            par = 0
            cy = p.dma("sync", "sXTA", xt2[par], src, waits=xt_free[par])
            cs_xn = emit_ln(xt2[par], [("sXTA", cy)], sq_b, xn_b)
            xt_free[par] = [("sS", cs_xn)]
            for r in range(4):
                bnk = 6 + (r % 2)
                ct = None
                for j in range(4):
                    mt = r * 4 + j
                    ct = p.emit("tensor",
                                lambda e, mt=mt, bnk=bnk, j=j: e.transpose(
                                    bank(bnk)[:, j * 128:(j + 1) * 128],
                                    xn_b[:, mt * 128:(mt + 1) * 128], ident),
                                waits=[("sS", cs_xn)] + bank_free[bnk] + xnT_free + id_rdy, inc="sT")
                dst_h = xnT_hi[:, r * 4:(r + 1) * 4, i * 128:(i + 1) * 128]
                dst_l = xnT_lo[:, r * 4:(r + 1) * 4, i * 128:(i + 1) * 128]
                cs = p.emit("scalar", lambda e, d=dst_h, bnk=bnk: e.activation(
                    d, bank(bnk).rearrange("p (a b) -> p a b", b=128), AF.Identity),
                    waits=[("sT", ct)], inc="sS")
                cv = p.emit("vector", lambda e, dl=dst_l, dh=dst_h, bnk=bnk: e.tensor_tensor(
                    dl, bank(bnk).rearrange("p (a b) -> p a b", b=128), dh, op=ALU.subtract),
                    waits=[("sS", cs)], inc="sV")
                bank_free[bnk] = [("sV", cv)]
        split_rdy = [("sV", p.cnt["sV"]), ("sS", p.cnt["sS"])]
        for ch, (whi, wlo, pb) in enumerate([(wk_hi, wk_lo, 4), (wv_hi, wv_lo, 5)]):
            ops = [(whi, xnT_hi), (whi, xnT_lo), (wlo, xnT_hi)]
            cm = None
            n_ops = 16 * 3
            k = 0
            for mt in range(16):
                for (la, hb) in ops:
                    first, last = (k == 0), (k == n_ops - 1)
                    extra = (st["wkv"] + split_rdy + bank_free[pb]) if first else []
                    cm = p.emit("tensor",
                                lambda e, la=la, hb=hb, mt=mt, pb=pb, first=first, last=last: e.matmul(
                                    bank(pb), la[:, mt, :], hb[:, mt, :],
                                    start=first, stop=last, **MMKW),
                                waits=extra, inc="sT")
                    k += 1
            dst = kT_f if ch == 0 else vT_f
            cs = p.emit("scalar",
                        lambda e, dst=dst, w=w, pb=pb, ch=ch: e.activation(
                            dst[:, w * 512:(w + 1) * 512], bank(pb), AF.Identity, bias=bkv[:, ch:ch + 1]),
                        waits=[("sT", cm)], inc="sS")
            bank_free[pb] = [("sS", cs)]
        xnT_free = [("sT", p.cnt["sT"])]
    kv_done = [("sS", p.cnt["sS"])]
    cS1 = p.emit("scalar", lambda e: e.activation(kT_hi, kT_f, AF.Identity), waits=kv_done, inc="sS")
    cV1 = p.emit("vector", lambda e: e.tensor_tensor(kT_lo, kT_f, kT_hi, op=ALU.subtract),
                 waits=[("sS", cS1)], inc="sV")
    cS2 = p.emit("scalar", lambda e: e.activation(vT_hi, vT_f, AF.Identity), waits=kv_done, inc="sS")
    cV2 = p.emit("vector", lambda e: e.tensor_tensor(vT_lo, vT_f, vT_hi, op=ALU.subtract),
                 waits=[("sS", cS2)], inc="sV")
    kv_split = [("sV", cV2)]
    ar.pop()
    ar.push()
    vtok_b = ar.alloc(2 * S, BF16)
    vtok_hi = vtok_b[:, 0:S].rearrange("p (t d) -> p t d", d=128)
    vtok_lo = vtok_b[:, S:2 * S].rearrange("p (t d) -> p t d", d=128)
    for r in range(8):
        bnk = 6 + (r % 2)
        ct = None
        for j in range(4):
            it = r * 4 + j
            ct = p.emit("tensor",
                        lambda e, it=it, bnk=bnk, j=j: e.transpose(
                            bank(bnk)[:, j * 128:(j + 1) * 128],
                            vT_f[:, it * 128:(it + 1) * 128], ident),
                        waits=kv_done + bank_free[bnk], inc="sT")
        dh = vtok_hi[:, r * 4:(r + 1) * 4, :]
        dl = vtok_lo[:, r * 4:(r + 1) * 4, :]
        cs = p.emit("scalar", lambda e, dh=dh, bnk=bnk: e.activation(
            dh, bank(bnk).rearrange("p (a b) -> p a b", b=128), AF.Identity),
            waits=[("sT", ct)], inc="sS")
        cv = p.emit("vector", lambda e, dl=dl, dh=dh, bnk=bnk: e.tensor_tensor(
            dl, bank(bnk).rearrange("p (a b) -> p a b", b=128), dh, op=ALU.subtract),
            waits=[("sS", cs)], inc="sV")
        bank_free[bnk] = [("sV", cv)]
    vtok_rdy = [("sV", p.cnt["sV"])]

    pq_f = [ar.alloc(512, F32) for _ in range(2)]
    pq_h = [ar.alloc(512, BF16) for _ in range(2)]
    pq_l = [ar.alloc(512, BF16) for _ in range(2)]
    rrow = ar.alloc(D, F32)
    pq_free = [[], []]
    for qc in range(4):
        ob = bank_free[2]
        rb = bank_free[3]
        c_o, c_r = None, None
        n_it = 32
        for it in range(n_it):
            par = it % 2
            ct = None
            sops = [(kT_hi, vT_hi), (kT_hi, vT_lo), (kT_lo, vT_hi)]
            for k, (la, rbs) in enumerate(sops):
                ct = p.emit("tensor",
                            lambda e, la=la, rbs=rbs, it=it, qc=qc, par=par,
                            first=(k == 0), last=(k == 2): e.matmul(
                                bank(par), la[:, it * 128:(it + 1) * 128],
                                rbs[:, qc * 512:(qc + 1) * 512],
                                start=first, stop=last, **MMKW),
                            waits=(kv_split + bank_free[par] + pq_free[par]) if k == 0 else [],
                            inc="sT")
            cs = p.emit("scalar", lambda e, par=par: e.activation(pq_f[par], bank(par), AF.Exp,
                                                                  scale=c128[:, 0:1]),
                        waits=[("sT", ct)], inc="sS")
            bank_free[par] = [("sS", cs)]
            cv1 = p.emit("vector", lambda e, par=par: e.tensor_copy(pq_h[par], pq_f[par]),
                         waits=[("sS", cs)], inc="sV")
            cv2 = p.emit("vector", lambda e, par=par: e.tensor_tensor(pq_l[par], pq_f[par], pq_h[par],
                                                                      op=ALU.subtract),
                         waits=[("sV", cv1)], inc="sV")
            aops = [(vtok_hi, pq_h), (vtok_hi, pq_l), (vtok_lo, pq_h)]
            for k, (la, hb) in enumerate(aops):
                c_o = p.emit("tensor",
                             lambda e, la=la, hb=hb, it=it, par=par,
                             first=(c_o is None), last=(it == n_it - 1 and k == 2): e.matmul(
                                 bank(2), la[:, it, :], hb[par], start=first, stop=last, **MMKW),
                             waits=(vtok_rdy + [("sV", cv2)] + (ob if c_o is None else [])),
                             inc="sT")
            for k, hb in enumerate((pq_h, pq_l)):
                c_r = p.emit("tensor",
                             lambda e, hb=hb, par=par, first=(c_r is None),
                             last=(it == n_it - 1 and k == 1): e.matmul(
                                 bank(3)[0:1, :], ones_c, hb[par], start=first, stop=last, **MMKW),
                             waits=(rb if c_r is None else []), inc="sT")
            pq_free[par] = [("sT", p.cnt["sT"])]
        cs = p.emit("scalar", lambda e, qc=qc: e.activation(oT_h[:, qc * 512:(qc + 1) * 512],
                                                            bank(2), AF.Identity),
                    waits=[("sT", c_o)], inc="sS")
        cv = p.emit("vector", lambda e, qc=qc: e.tensor_tensor(
            oT_l[:, qc * 512:(qc + 1) * 512], bank(2), oT_h[:, qc * 512:(qc + 1) * 512], op=ALU.subtract),
            waits=[("sS", cs)], inc="sV")
        bank_free[2] = [("sV", cv)]
        cvr = p.emit("vector", lambda e, qc=qc: e.tensor_copy(rrow[0:1, qc * 512:(qc + 1) * 512],
                                                              bank(3)[0:1, :]),
                     waits=[("sT", c_r)], inc="sV")
        bank_free[3] = [("sV", cvr)]
    oT_rdy = [("sS", p.cnt["sS"]), ("sV", p.cnt["sV"])]
    ct = None
    for j in range(16):
        ct = p.emit("tensor",
                    lambda e, j=j: e.transpose(bank(6)[:, j:j + 1], rrow[0:1, j * 128:(j + 1) * 128],
                                               ident[0:1, 0:1]),
                    waits=oT_rdy + bank_free[6], inc="sT")
    cv = p.emit("vector", lambda e: e.reciprocal(rinv, bank(6)[:, 0:16]), waits=[("sT", ct)], inc="sV")
    bank_free[6] = [("sV", cv)]
    st["rinv"] = [("sV", cv)]
    if "oT_h" in taps:
        p.dma_tap(taps["oT_h"], oT_h, waits=oT_rdy)
    if "oT_l" in taps:
        p.dma_tap(taps["oT_l"], oT_l, waits=oT_rdy)
    if "rinv" in taps:
        p.dma_tap(taps["rinv"], rinv, waits=st["rinv"])
    if "kT" in taps:
        p.dma_tap(taps["kT"], kT_f, waits=kv_done)
    if "vT" in taps:
        p.dma_tap(taps["vT"], vT_f, waits=kv_done)
    if "woe" in taps:
        p.dma_tap(taps["woe"], woe_f, waits=st["woe"])
    ar.pop()
    ar.pop()

    fenceA = p.fence()
    ar.push()
    selgu_s = ar.alloc(16 * 128, F32).rearrange("p (j q) -> p j q", q=128)
    seld_s = ar.alloc(32 * 128, F32).rearrange("p (j q) -> p j q", q=128)
    cSEL1 = p.dma("sync", "sID", selgu_s[0:64], selgu_h, waits=[("sID", p.cnt["sID"])] + fenceA)
    cSEL1b = p.dma("sync", "sID", selgu_s[64:128], selgu_h, waits=[("sID", cSEL1)])
    cSEL2 = p.dma("sync", "sID", seld_s, seld_h, waits=[("sID", cSEL1b)])
    sel_rdy = [("sID", cSEL2)]
    amax_buf = ar.alloc(4096, F32)
    r2n_buf = ar.alloc(4096, F32)
    s2T_gu = ar.alloc(II, F32)
    r2T_gu = ar.alloc(II, F32)
    s2T_d = s2T_gu[:, 0:2 * D]
    r2T_d = r2T_gu[:, 0:2 * D]
    amx2 = [r2n_buf[:, 0:2048], r2n_buf[:, 2048:4096]]
    raw2 = [ar.alloc(1024, F32) for _ in range(2)]
    wdq = [ar.alloc(1024, F32) for _ in range(2)]
    whi = [ar.alloc(1024, BF16) for _ in range(2)]
    wlo = [ar.alloc(1024, BF16) for _ in range(2)]
    q8w = [w.bitcast(FP8)[:, 0:1024] for w in wlo]
    rep_s_ps = [PS[:, 0:1024], PS[:, 4 * 512:4 * 512 + 1024]]
    rep_r_ps = [PS[:, 2 * 512:2 * 512 + 1024], PS[:, 6 * 512:6 * 512 + 1024]]

    gn_r = gn_h.rearrange("(n p) m -> n p m", p=128)
    un_r = un_h.rearrange("(n p) m -> n p m", p=128)
    dn_r = dn_h.rearrange("(n p) m -> n p m", p=128)
    gt_r = gt_h.rearrange("(n p) m -> n p m", p=128)
    ut_r = ut_h.rearrange("(n p) m -> n p m", p=128)
    dt_r = dt_h.rearrange("(n p) m -> n p m", p=128)
    scg_r, scu_r, scd_r = sc_g, sc_u, sc_d

    rawgate = [list(fenceA), list(fenceA)]
    repgate_r = [list(fenceA), list(fenceA)]
    repgate_s = [list(fenceA), list(fenceA)]
    pipe_q8 = [[], []]
    pipe_wdq = [[], []]
    pipe_whi = [[], []]
    pipe_wlo = [[], []]

    def quant_weight(nat_r, trn_r, sc_r, widx):
        is_d = (widx == 2)
        amax_v = amax_buf.rearrange("p (t b) -> p t b", b=(256 if is_d else 64))
        am_gate = [("sT", p.cnt["sT"])]
        last_am = 0
        for j in range(16 if is_d else 64):
            par = j % 2
            if not is_d:
                cy = p.dma("sync", f"sW{par}", amx2[par], nat_r[j], waits=rawgate[par])
                cv = p.emit("vector",
                            lambda e, j=j, par=par: e.reduce_max(
                                amax_v[:, j, :], amx2[par].rearrange("p (b k) -> p b k", k=32),
                                axis=AX, apply_absolute_value=True),
                            waits=[(f"sW{par}", cy)] + am_gate, inc="sV")
                rawgate[par] = [("sV", cv)]
            else:
                for hh in range(4):
                    par = (j * 4 + hh) % 2
                    cy = p.dma("sync", f"sW{par}", amx2[par], nat_r[j][:, hh * 2048:(hh + 1) * 2048],
                               waits=rawgate[par])
                    cv = p.emit("vector",
                                lambda e, j=j, hh=hh, par=par: e.reduce_max(
                                    amax_v[:, j, hh * 64:(hh + 1) * 64],
                                    amx2[par].rearrange("p (b k) -> p b k", k=32),
                                    axis=AX, apply_absolute_value=True),
                                waits=[(f"sW{par}", cy)] + am_gate, inc="sV")
                    rawgate[par] = [("sV", cv)]
            last_am = p.cnt["sV"]
        cv = p.emit("vector",
                    lambda e: e.tensor_scalar(amax_buf, amax_buf, 1.0 / 224.0, 2e-30,
                                              op0=ALU.mult, op1=ALU.max),
                    waits=[("sV", last_am)], inc="sV")
        cv = p.emit("vector", lambda e: e.reciprocal(r2n_buf, amax_buf),
                    waits=[("sV", cv)], inc="sV")
        r2n_v = r2n_buf.rearrange("p (t b) -> p t b", b=(256 if is_d else 64))
        tgate = [("sT", p.cnt["sT"])]
        if not is_d:
            prow = 64 * widx
            for (tab_v, dstT) in ((amax_v, s2T_gu), (r2n_v, r2T_gu)):
                for r in range(16):
                    bnk = r % 2
                    ct = None
                    for j in range(4):
                        tj = r * 4 + j
                        ct = p.emit("tensor",
                                    lambda e, tj=tj, bnk=bnk, j=j, tab_v=tab_v: e.transpose(
                                        bank(bnk)[0:64, j * 128:(j + 1) * 128], tab_v[:, tj, :], ident),
                                    waits=[("sV", cv)] + bank_free[bnk] + id_rdy, inc="sT")
                    cvv = p.emit("vector",
                                 lambda e, r=r, bnk=bnk, prow=prow, dstT=dstT: e.tensor_copy(
                                     dstT[prow:prow + 64, r * 512:(r + 1) * 512], bank(bnk)[0:64, :]),
                                 waits=[("sT", ct)] + tgate, inc="sV")
                    bank_free[bnk] = [("sV", cvv)]
        else:
            for (tab_v, dstT) in ((amax_v, s2T_d), (r2n_v, r2T_d)):
                for j in range(16):
                    bnk = j % 2
                    ct0 = p.emit("tensor",
                                 lambda e, j=j, bnk=bnk, tab_v=tab_v: e.transpose(
                                     bank(bnk)[:, 0:128], tab_v[:, j, 0:128], ident),
                                 waits=[("sV", cv)] + bank_free[bnk] + id_rdy, inc="sT")
                    ct1 = p.emit("tensor",
                                 lambda e, j=j, bnk=bnk, tab_v=tab_v: e.transpose(
                                     bank(bnk)[:, 128:256], tab_v[:, j, 128:256], ident),
                                 waits=[], inc="sT")
                    cv1 = p.emit("vector",
                                 lambda e, j=j, bnk=bnk, dstT=dstT: e.tensor_copy(
                                     dstT[:, j * 128:(j + 1) * 128], bank(bnk)[:, 0:128]),
                                 waits=[("sT", ct1)] + tgate, inc="sV")
                    cv2 = p.emit("vector",
                                 lambda e, j=j, bnk=bnk, dstT=dstT: e.tensor_copy(
                                     dstT[:, D + j * 128: D + (j + 1) * 128], bank(bnk)[:, 128:256]),
                                 waits=[("sT", ct1)], inc="sV")
                    bank_free[bnk] = [("sV", cv2)]
        tbl = [("sV", p.cnt["sV"])]

        wq = 1024
        nh = 2 if is_d else 8
        for j in range(64 if is_d else 16):
            for hh in range(nh):
                par = (j * nh + hh) % 2
                src_tile = trn_r[j][:, hh * wq:(hh + 1) * wq]
                cy = p.dma("sync", f"sW{par}", raw2[par], src_tile, waits=rawgate[par] + tbl)
                if not is_d:
                    prow = 64 * widx
                    cslice = slice(hh * 1024, (hh + 1) * 1024)
                    src_s = s2T_gu[prow:prow + 64, cslice]
                    src_r = r2T_gu[prow:prow + 64, cslice]
                else:
                    q_ = (j * 4) // 128
                    cslice = slice(q_ * D + hh * 1024, q_ * D + (hh + 1) * 1024)
                    src_s = s2T_gu[0:128, cslice]
                    src_r = r2T_gu[0:128, cslice]
                if not is_d:
                    sel_ap = selgu_s[64 * widx:64 * widx + 64, j, :]
                else:
                    sel_ap = seld_s[:, j % 32, :]
                cms = None
                for half in range(2):
                    cms = p.emit("tensor",
                                 lambda e, par=par, half=half, src_s=src_s, sel_ap=sel_ap: e.matmul(
                                     rep_s_ps[par][:, half * 512:(half + 1) * 512], sel_ap,
                                     src_s[:, half * 512:(half + 1) * 512],
                                     start=True, stop=True, **MMKW),
                                 waits=tbl + repgate_s[par] + sel_rdy, inc="sT")
                    cms = p.emit("tensor",
                                 lambda e, par=par, half=half, src_r=src_r, sel_ap=sel_ap: e.matmul(
                                     rep_r_ps[par][:, half * 512:(half + 1) * 512], sel_ap,
                                     src_r[:, half * 512:(half + 1) * 512],
                                     start=True, stop=True, **MMKW),
                                 waits=[], inc="sT")
                cq = p.emit("vector",
                            lambda e, par=par: e.tensor_tensor(
                                q8w[par], raw2[par], rep_r_ps[par], op=ALU.mult),
                            waits=[(f"sW{par}", cy), ("sT", cms)] + pipe_q8[par] + pipe_wlo[par],
                            inc="sV")
                rawgate[par] = [("sV", cq)]
                cd = p.emit("vector",
                            lambda e, par=par: e.tensor_tensor(
                                wdq[par], q8w[par], rep_s_ps[par], op=ALU.mult),
                            waits=[("sV", cq)] + pipe_wdq[par], inc="sV")
                repgate_s[par] = [("sV", cd)]
                pipe_q8[par] = [("sV", cd)]
                ch = p.emit("scalar",
                            lambda e, par=par: e.activation(whi[par], wdq[par], AF.Identity),
                            waits=[("sV", cd)] + pipe_whi[par], inc="sS")
                cl = p.emit("vector",
                            lambda e, par=par: e.tensor_tensor(
                                wlo[par], wdq[par], whi[par], op=ALU.subtract),
                            waits=[("sS", ch)], inc="sV")
                pipe_wdq[par] = [("sV", cl)]
                dsth = sc_r[j, 0, :, hh * wq:(hh + 1) * wq]
                dstl = sc_r[j, 1, :, hh * wq:(hh + 1) * wq]
                cw1 = p.dma("scalar", f"sWH{par}", dsth, whi[par], waits=[("sV", cl)])
                cw2 = p.dma("scalar", f"sWL{par}", dstl, wlo[par], waits=[("sV", cl)])
                pipe_whi[par] = [(f"sWH{par}", cw1)]
                pipe_wlo[par] = [(f"sWL{par}", cw2)]
        return [("sWH0", p.cnt["sWH0"]), ("sWH1", p.cnt["sWH1"]),
                ("sWL0", p.cnt["sWL0"]), ("sWL1", p.cnt["sWL1"])]

    wg_done = quant_weight(gn_r, gt_r, scg_r, 0)
    wu_done = quant_weight(un_r, ut_r, scu_r, 1)
    wd_done = quant_weight(dn_r, dt_r, scd_r, 2)
    for _b in range(8):
        bank_free[_b] = [("sV", p.cnt["sV"])]
    ar.pop()

    fenceW = p.fence()
    ar.push()
    xhT_h = ar.alloc(16 * 512, BF16).rearrange("p (t q) -> p t q", q=512)
    xhT_l = ar.alloc(16 * 512, BF16).rearrange("p (t q) -> p t q", q=512)
    acc = ar.alloc(4 * D, F32).rearrange("p (t m) -> p t m", m=D)
    hT_h = ar.alloc(16 * 512, BF16).rearrange("p (t q) -> p t q", q=512)
    hT_l = ar.alloc(16 * 512, BF16).rearrange("p (t q) -> p t q", q=512)
    gu_buf = [[ar.alloc(2 * 2 * 512, BF16).rearrange("p (t q) -> p t q", q=512)
               for _ in range(2)] for _ in range(2)]
    d_buf = [ar.alloc(2 * 512, BF16).rearrange("p (h q) -> p h q", h=2) for _ in range(2)]
    xt_b = ar.alloc(D, F32)
    q8x = ar.alloc(D, FP8)
    xq_b = ar.alloc(D, F32)
    qs = ar.alloc(3 * 64, F32)
    h_b = ar.alloc(512, F32)
    sg_b = ar.alloc(512, F32)
    hq_b = ar.alloc(512, F32)
    q8h = ar.alloc(512, FP8)
    hs = ar.alloc(3 * 16, F32)
    fin = ar.alloc(D, F32)
    res_b = ar.alloc(D, F32)
    sq2_b = fin
    xn2_b = res_b

    out_r = out_h.rearrange("(n p) m -> n p m", p=128)
    gu_free = [list(fenceW), list(fenceW)]
    d_free = [list(fenceW), list(fenceW)]
    xt_gate = list(fenceW)
    fin_gate = list(fenceW)
    res_gate = list(fenceW)
    acc_gate = list(fenceW)
    sg_gate = []
    xq_readers = []

    for tc in range(4):
        for i in range(4):
            gt = tc * 4 + i
            cy = p.dma("sync", "sXB", xt_b, xown_r[gt], waits=xt_gate)
            cs_xn = emit_ln(xt_b, [("sXB", cy)], sq2_b, xn2_b)
            xt_gate = [("sS", cs_xn)]
            qa = p.emit("vector",
                        lambda e: e.reduce_max(qs[:, 0:64], xn2_b.rearrange("p (b k) -> p b k", k=32),
                                               axis=AX, apply_absolute_value=True),
                        waits=[("sS", cs_xn)], inc="sV")
            qb = p.emit("vector",
                        lambda e: e.tensor_scalar(qs[:, 64:128], qs[:, 0:64], 1.0 / 224.0, 2e-30,
                                                  op0=ALU.mult, op1=ALU.max),
                        waits=[("sV", qa)], inc="sV")
            qc_ = p.emit("vector", lambda e: e.reciprocal(qs[:, 128:192], qs[:, 64:128]),
                         waits=[("sV", qb)], inc="sV")
            qd = p.emit("vector",
                        lambda e: e.tensor_tensor(
                            q8x.rearrange("p (b k) -> p b k", k=32),
                            xn2_b.rearrange("p (b k) -> p b k", k=32),
                            qs[:, 128:192].unsqueeze(2).broadcast_to([128, 64, 32]), op=ALU.mult),
                        waits=[("sV", qc_)], inc="sV")
            qe = p.emit("vector",
                        lambda e: e.tensor_tensor(
                            xq_b.rearrange("p (b k) -> p b k", k=32),
                            q8x.rearrange("p (b k) -> p b k", k=32),
                            qs[:, 64:128].unsqueeze(2).broadcast_to([128, 64, 32]), op=ALU.mult),
                        waits=[("sV", qd)], inc="sV")
            for r in range(4):
                bnk = r % 2
                ct = None
                for jj in range(4):
                    mt = r * 4 + jj
                    ct = p.emit("tensor",
                                lambda e, mt=mt, bnk=bnk, jj=jj: e.transpose(
                                    bank(bnk)[:, jj * 128:(jj + 1) * 128],
                                    xq_b[:, mt * 128:(mt + 1) * 128], ident),
                                waits=[("sV", qe)] + bank_free[bnk] + xq_readers, inc="sT")
                dh = xhT_h[:, r * 4:(r + 1) * 4, i * 128:(i + 1) * 128]
                dl = xhT_l[:, r * 4:(r + 1) * 4, i * 128:(i + 1) * 128]
                cs = p.emit("scalar", lambda e, dh=dh, bnk=bnk: e.activation(
                    dh, bank(bnk).rearrange("p (a b) -> p a b", b=128), AF.Identity),
                    waits=[("sT", ct)], inc="sS")
                cvv = p.emit("vector", lambda e, dl=dl, dh=dh, bnk=bnk: e.tensor_tensor(
                    dl, bank(bnk).rearrange("p (a b) -> p a b", b=128), dh, op=ALU.subtract),
                    waits=[("sS", cs)], inc="sV")
                bank_free[bnk] = [("sV", cvv)]
        xq_rdy = [("sV", p.cnt["sV"]), ("sS", p.cnt["sS"])]
        xq_readers = []
        if "xhT_h" in taps and tc == 0:
            p.dma_tap(taps["xhT_h"], xhT_h.rearrange("p t q -> p (t q)"), waits=xq_rdy)
            xq_readers = xq_readers + [("sTAP", p.cnt["sTAP"])]

        for g in range(4):
            hT_last = []
            for jic in range(4):
                ic = g * 4 + jic
                mm_gate_last = [None] * 4
                mm_up_last = [None] * 4
                for qtr in range(8):
                    par = qtr % 2
                    cyg = p.dma(
                        "sync", f"sGU{par}g", gu_buf[par][0],
                        scg_r[qtr * 2:(qtr + 1) * 2, :, :, ic * 512:(ic + 1) * 512]
                        .rearrange("a h p q -> p (a h) q"),
                        waits=gu_free[par] + wg_done + xq_rdy)
                    cyu = p.dma(
                        "sync", f"sGU{par}u", gu_buf[par][1],
                        scu_r[qtr * 2:(qtr + 1) * 2, :, :, ic * 512:(ic + 1) * 512]
                        .rearrange("a h p q -> p (a h) q"),
                        waits=gu_free[par] + wu_done)
                    ld_done2 = [(f"sGU{par}g", cyg), (f"sGU{par}u", cyu)]
                    for mtl in range(2):
                        mt = qtr * 2 + mtl
                        for tsub in range(4):
                            gb, ub = tsub, 4 + tsub
                            ops6 = ((xhT_h, gu_buf[par][0], 0, gb, mm_gate_last, False),
                                    (xhT_h, gu_buf[par][0], 1, gb, mm_gate_last, False),
                                    (xhT_l, gu_buf[par][0], 0, gb, mm_gate_last, True),
                                    (xhT_h, gu_buf[par][1], 0, ub, mm_up_last, False),
                                    (xhT_h, gu_buf[par][1], 1, ub, mm_up_last, False),
                                    (xhT_l, gu_buf[par][1], 0, ub, mm_up_last, True))
                            for (la, wvv, hl, pb, rec, is3) in ops6:
                                first = rec[tsub] is None
                                last = (qtr == 7 and mtl == 1 and is3)
                                ws = list(ld_done2) + xq_rdy
                                if first:
                                    ws += bank_free[pb]
                                rec[tsub] = p.emit(
                                    "tensor",
                                    lambda e, la=la, wvv=wvv, hl=hl, pb=pb, mt=mt, mtl=mtl, tsub=tsub,
                                    first=first, last=last: e.matmul(
                                        bank(pb), la[:, mt, tsub * 128:(tsub + 1) * 128],
                                        wvv[:, mtl * 2 + hl, :], start=first, stop=last, **MMKW),
                                    waits=ws, inc="sT")
                    gu_free[par] = [("sT", p.cnt["sT"])]
                for tsub in range(4):
                    gb, ub = tsub, 4 + tsub
                    e1 = p.emit("scalar", lambda e, gb=gb: e.activation(sg_b, bank(gb), AF.Silu),
                                waits=[("sT", mm_gate_last[tsub])] + sg_gate, inc="sS")
                    e2 = p.emit("vector", lambda e, ub=ub: e.tensor_tensor(h_b, bank(ub), sg_b, op=ALU.mult),
                                waits=[("sS", e1), ("sT", mm_up_last[tsub])], inc="sV")
                    sg_gate = [("sV", e2)]
                    bank_free[ub] = [("sV", e2)]
                    q1 = p.emit("vector",
                                lambda e: e.reduce_max(hs[:, 0:16], h_b.rearrange("p (b k) -> p b k", k=32),
                                                       axis=AX, apply_absolute_value=True),
                                waits=[("sV", e2)], inc="sV")
                    q2 = p.emit("vector",
                                lambda e: e.tensor_scalar(hs[:, 16:32], hs[:, 0:16], 1.0 / 224.0, 2e-30,
                                                          op0=ALU.mult, op1=ALU.max),
                                waits=[("sV", q1)], inc="sV")
                    q3 = p.emit("vector", lambda e: e.reciprocal(hs[:, 32:48], hs[:, 16:32]),
                                waits=[("sV", q2)], inc="sV")
                    q4 = p.emit("vector",
                                lambda e: e.tensor_tensor(
                                    q8h.rearrange("p (b k) -> p b k", k=32),
                                    h_b.rearrange("p (b k) -> p b k", k=32),
                                    hs[:, 32:48].unsqueeze(2).broadcast_to([128, 16, 32]), op=ALU.mult),
                                waits=[("sV", q3)], inc="sV")
                    q5 = p.emit("vector",
                                lambda e: e.tensor_tensor(
                                    hq_b.rearrange("p (b k) -> p b k", k=32),
                                    q8h.rearrange("p (b k) -> p b k", k=32),
                                    hs[:, 16:32].unsqueeze(2).broadcast_to([128, 16, 32]), op=ALU.mult),
                                waits=[("sV", q4)], inc="sV")
                    ct = None
                    for jj in range(4):
                        ct = p.emit("tensor",
                                    lambda e, jj=jj, gb=gb: e.transpose(
                                        bank(gb)[:, jj * 128:(jj + 1) * 128],
                                        hq_b[:, jj * 128:(jj + 1) * 128], ident),
                                    waits=[("sV", q5)] + bank_free[gb], inc="sT")
                    it0 = jic * 4
                    dh = hT_h[:, it0:it0 + 4, tsub * 128:(tsub + 1) * 128]
                    dl = hT_l[:, it0:it0 + 4, tsub * 128:(tsub + 1) * 128]
                    cs = p.emit("scalar", lambda e, dh=dh, gb=gb: e.activation(
                        dh, bank(gb).rearrange("p (a b) -> p a b", b=128), AF.Identity),
                        waits=[("sT", ct)], inc="sS")
                    cvv = p.emit("vector", lambda e, dl=dl, dh=dh, gb=gb: e.tensor_tensor(
                        dl, bank(gb).rearrange("p (a b) -> p a b", b=128), dh, op=ALU.subtract),
                        waits=[("sS", cs)], inc="sV")
                    bank_free[gb] = [("sV", cvv)]
            hT_all = [("sV", p.cnt["sV"]), ("sS", p.cnt["sS"])]
            for mch in range(4):
                dmm_last = [None] * 4
                for it in range(16):
                    par = it % 2
                    cy1 = p.dma("sync", f"sD{par}", d_buf[par],
                                scd_r[g * 16 + it, :, :, mch * 512:(mch + 1) * 512]
                                .rearrange("h p q -> p h q"),
                                waits=d_free[par] + wd_done)
                    for tsub in range(4):
                        pb = 4 + tsub
                        dops = ((hT_h, 0, False), (hT_h, 1, False), (hT_l, 0, True))
                        for (lah, hl, is3) in dops:
                            first = dmm_last[tsub] is None
                            last = (it == 15 and is3)
                            ws = [(f"sD{par}", cy1)] + hT_all
                            if first:
                                ws += bank_free[pb]
                            dmm_last[tsub] = p.emit(
                                "tensor",
                                lambda e, lah=lah, hl=hl, par=par, it=it, tsub=tsub, pb=pb,
                                first=first, last=last: e.matmul(
                                    bank(pb), lah[:, it, tsub * 128:(tsub + 1) * 128], d_buf[par][:, hl, :],
                                    start=first, stop=last, **MMKW),
                                waits=ws, inc="sT")
                    d_free[par] = [("sT", p.cnt["sT"])]
                for tsub in range(4):
                    pb = 4 + tsub
                    if g == 0:
                        cvx = p.emit("vector",
                                     lambda e, tsub=tsub, mch=mch, pb=pb: e.tensor_copy(
                                         acc[:, tsub, mch * 512:(mch + 1) * 512], bank(pb)),
                                     waits=[("sT", dmm_last[tsub])] + acc_gate, inc="sV")
                    else:
                        cvx = p.emit("vector",
                                     lambda e, tsub=tsub, mch=mch, pb=pb: e.tensor_tensor(
                                         acc[:, tsub, mch * 512:(mch + 1) * 512],
                                         acc[:, tsub, mch * 512:(mch + 1) * 512],
                                         bank(pb), op=ALU.add),
                                     waits=[("sT", dmm_last[tsub])], inc="sV")
                    bank_free[pb] = [("sV", cvx)]
        acc_done = [("sV", p.cnt["sV"])]
        if "acc" in taps and tc == 0:
            p.dma_tap(taps["acc"], acc.rearrange("p t m -> p (t m)"), waits=acc_done)
            acc_gate = acc_gate + [("sTAP", p.cnt["sTAP"])]

        for tsub in range(4):
            j = tc * 4 + tsub
            cmt = None
            for mch in range(4):
                pb = mch
                wops = ((oT_h, woe_h), (oT_h, woe_l), (oT_l, woe_h))
                for k, (la, rb2) in enumerate(wops):
                    ws = []
                    if k == 0:
                        ws = list(bank_free[pb]) + st["woe"] + oT_rdy
                    cmt = p.emit("tensor",
                                 lambda e, la=la, rb2=rb2, j=j, mch=mch, pb=pb,
                                 first=(k == 0), last=(k == 2): e.matmul(
                                     bank(pb), la[:, j * 128:(j + 1) * 128],
                                     rb2[:, mch * 512:(mch + 1) * 512],
                                     start=first, stop=last, **MMKW),
                                 waits=ws, inc="sT")
            cy = p.dma("scalar", "sFIN", fin, xown_r[j], waits=fin_gate)
            cs = p.emit("scalar",
                        lambda e, j=j: e.activation(res_b, PS[:, 0:2048], AF.Identity,
                                                    scale=rinv[:, j:j + 1]),
                        waits=[("sT", cmt)] + st["rinv"] + res_gate, inc="sS")
            for b_ in range(4):
                bank_free[b_] = [("sS", cs)]
            cv1 = p.emit("vector", lambda e: e.tensor_add(res_b, res_b, bo_rep),
                         waits=[("sS", cs), ("sBO", cG_bo)], inc="sV")
            cv2 = p.emit("vector", lambda e: e.tensor_add(res_b, res_b, fin),
                         waits=[("sV", cv1), ("sFIN", cy)], inc="sV")
            cv3 = p.emit("vector",
                         lambda e, tsub=tsub: e.tensor_add(res_b, res_b, acc[:, tsub, :]),
                         waits=[("sV", cv2)] + acc_done, inc="sV")
            cg = p.dma("gpsimd", "sOUT", out_r[j], res_b, waits=[("sV", cv3)])
            fin_gate = [("sV", cv2)]
            res_gate = [("sOUT", cg)]
        acc_gate = [("sOUT", p.cnt["sOUT"])]
        xq_readers = [("sT", p.cnt["sT"]), ("sTAP", p.cnt["sTAP"])]
        xt_gate = xt_gate + [("sOUT", p.cnt["sOUT"]), ("sFIN", p.cnt["sFIN"]),
                             ("sV", p.cnt["sV"])]

    p.emit("gpsimd", lambda e: e.nop(), waits=[("sOUT", p.cnt["sOUT"])])

    with nc.Block() as blk:
        p.build(blk)
    return nc


def kernel(**inputs):
    x = np.ascontiguousarray(inputs["L0_attn_q"], dtype=np.float32)
    wk = np.ascontiguousarray(inputs["L0_attn_Wk"], dtype=np.float32)
    wv = np.ascontiguousarray(inputs["L0_attn_Wv"], dtype=np.float32)
    bk = np.ascontiguousarray(inputs["L0_attn_bk"], dtype=np.float32)
    bv = np.ascontiguousarray(inputs["L0_attn_bv"], dtype=np.float32)
    wo = np.ascontiguousarray(inputs["L0_attn_Wo"], dtype=np.float32)
    bo = np.ascontiguousarray(inputs["L0_attn_bo"], dtype=np.float32)
    gw = np.ascontiguousarray(inputs["L0_ffn_gate_w"], dtype=np.float32)
    uw = np.ascontiguousarray(inputs["L0_ffn_up_w"], dtype=np.float32)
    dw = np.ascontiguousarray(inputs["L0_ffn_down_w"], dtype=np.float32)
    gwT = np.ascontiguousarray(gw.T)
    uwT = np.ascontiguousarray(uw.T)
    dwT = np.ascontiguousarray(dw.T)
    ident = np.eye(128, dtype=np.float32)
    selgu = np.zeros((64, 16, 128), dtype=np.float32)
    for j in range(16):
        for pp in range(128):
            selgu[(j * 4 + pp // 32) % 64, j, pp] = 1.0
    seld = np.zeros((128, 32, 128), dtype=np.float32)
    for m in range(32):
        for pp in range(128):
            seld[((m * 4) % 128) + pp // 32, m, pp] = 1.0

    nc = build_program()
    in_maps = []
    for c in range(NCORE):
        b, h = c // 2, c % 2
        in_maps.append({
            "x_own": x[b, h * TOK:(h + 1) * TOK],
            "x_oth": x[b, (1 - h) * TOK:(2 - h) * TOK],
            "wk": wk, "wv": wv, "bk": bk, "bv": bv, "wo": wo, "bo": bo,
            "g_nat": gw, "u_nat": uw, "d_nat": dw,
            "g_t": gwT, "u_t": uwT, "d_t": dwT,
            "ident": ident, "selgu": selgu, "seld": seld,
        })
    res = run_bass_kernel_spmd(nc, in_maps, core_ids=list(range(NCORE)))
    out = np.empty((4, S, D), dtype=np.float32)
    for c in range(NCORE):
        b, h = c // 2, c % 2
        out[b, h * TOK:(h + 1) * TOK] = res.results[c]["out"]
    return out


# version 04 (final)
# speedup vs baseline: 1.8469x; 1.0039x over previous
import numpy as np
import concourse.bass as bass
import concourse.mybir as mybir
from concourse.bass_utils import run_bass_kernel_spmd

F32 = mybir.dt.float32
BF16 = mybir.dt.bfloat16
FP8 = mybir.dt.float8e4
I32 = mybir.dt.int32
ALU = mybir.AluOpType
AF = mybir.ActivationFunctionType
AX = mybir.AxisListType.X

TOK = 2048
S = 4096
D = 2048
II = 8192
NCORE = 8
ENGS = ("tensor", "vector", "scalar", "sync", "gpsimd")

DEBUG_TAPS = []


class PB:

    def __init__(self, nc):
        self.nc = nc
        self.ev = {e: [] for e in ENGS}
        self.sem = {}
        self.cnt = {}
        self.wm = {e: {} for e in ENGS}

    def add_sem(self, name):
        self.sem[name] = self.nc.alloc_semaphore(name)
        self.cnt[name] = 0

    def emit(self, eng, fn, waits=(), inc=None, by=1):
        w2 = []
        for s, v in waits:
            if not v or v <= 0:
                continue
            if v > self.wm[eng].get(s, 0):
                w2.append((s, v))
                self.wm[eng][s] = v
        self.ev[eng].append((fn, w2, inc, by))
        if inc is not None:
            self.cnt[inc] += by
            return self.cnt[inc]
        return None

    def dma(self, eng, sem, out, in_, waits=(), **kw):
        return self.emit(eng, lambda e, o=out, i=in_, kw=kw: e.dma_start(o, i, **kw),
                         waits=waits, inc=sem, by=16)

    def dma_tap(self, out, in_, waits=()):
        w = list(waits) + ([("sTAP", self.cnt["sTAP"])] if self.cnt["sTAP"] else [])
        return self.dma("gpsimd", "sTAP", out, in_, waits=w)

    def fence(self):
        return [(s, c) for s, c in self.cnt.items() if c > 0]

    def build(self, blk):
        for eng in ENGS:
            evs = self.ev[eng]
            if not evs:
                continue
            sec = getattr(blk, eng)

            def body(e, evs=evs):
                for fn, waits, inc, by in evs:
                    for s, v in waits:
                        e.wait_ge(self.sem[s], v)
                    ins = fn(e)
                    if inc is not None:
                        ins.then_inc(self.sem[inc], by)
            sec(body)


class Arena:
    def __init__(self, t, nbytes):
        self.t = t
        self.cap = nbytes
        self.off = 0
        self.marks = []

    def alloc(self, n_elem, dt):
        esz = {F32: 4, BF16: 2, FP8: 1}[dt]
        self.off = (self.off + 63) & ~63
        start = self.off
        self.off += n_elem * esz
        assert self.off <= self.cap, f"arena overflow {self.off} > {self.cap}"
        if dt == F32:
            return self.t[:, start // 4: start // 4 + n_elem]
        return self.t.bitcast(dt)[:, start // esz: start // esz + n_elem]

    def push(self):
        self.marks.append(self.off)

    def pop(self):
        self.off = self.marks.pop()


def build_program():
    nc = bass.Bass("TRN2", target_bir_lowering=False)
    p = PB(nc)
    for s_ in ("sT", "sV", "sS", "sID", "sXTA", "sWO", "sWKV", "sW0", "sW1",
               "sGU0g", "sGU0u", "sGU1g", "sGU1u", "sD0", "sD1", "sXB",
               "sWH0", "sWH1", "sWL0", "sWL1", "sFIN", "sBO", "sRS0", "sRS1",
               "sOUT", "sTAP", "sP"):
        p.add_sem(s_)

    dr = lambda n, sh, dt, k: nc.dram_tensor(n, sh, dt, kind=k).ap()
    x_own = dr("x_own", [TOK, D], F32, "ExternalInput")
    x_oth = dr("x_oth", [TOK, D], F32, "ExternalInput")
    wk_h = dr("wk", [D, 128], F32, "ExternalInput")
    wv_h = dr("wv", [D, 128], F32, "ExternalInput")
    bk_h = dr("bk", [128], F32, "ExternalInput")
    bv_h = dr("bv", [128], F32, "ExternalInput")
    wo_h = dr("wo", [D, D], F32, "ExternalInput")
    bo_h = dr("bo", [D], F32, "ExternalInput")
    gn_h = dr("g_nat", [II, D], F32, "ExternalInput")
    un_h = dr("u_nat", [II, D], F32, "ExternalInput")
    dn_h = dr("d_nat", [D, II], F32, "ExternalInput")
    gt_h = dr("g_t", [D, II], F32, "ExternalInput")
    ut_h = dr("u_t", [D, II], F32, "ExternalInput")
    dt_h = dr("d_t", [II, D], F32, "ExternalInput")
    id_h = dr("ident", [128, 128], F32, "ExternalInput")
    selgu_h = dr("selgu", [64, 16, 128], F32, "ExternalInput")
    seld_h = dr("seld", [128, 32, 128], F32, "ExternalInput")
    out_h = dr("out", [TOK, D], F32, "ExternalOutput")
    sc_g = dr("sc_g", [16, 2, 128, II], BF16, "Internal")
    sc_u = dr("sc_u", [16, 2, 128, II], BF16, "Internal")
    sc_d = dr("sc_d", [64, 2, 128, D], BF16, "Internal")
    taps = {nm: dr("tap_" + nm, sh, dt, "ExternalOutput") for nm, sh, dt in DEBUG_TAPS}

    ident = nc.alloc_sbuf_tensor("ident_s", [128, 128], F32).ap()
    bo_rep = nc.alloc_sbuf_tensor("bo_rep", [128, D], F32).ap()
    woe_h = nc.alloc_sbuf_tensor("woe_h", [128, D], BF16).ap()
    woe_l = nc.alloc_sbuf_tensor("woe_l", [128, D], BF16).ap()
    oT_h = nc.alloc_sbuf_tensor("oT_h", [128, TOK], BF16).ap()
    oT_l = nc.alloc_sbuf_tensor("oT_l", [128, TOK], BF16).ap()
    rinv = nc.alloc_sbuf_tensor("rinv", [128, 16], F32).ap()
    ones_c = nc.alloc_sbuf_tensor("ones_c", [128, 1], BF16).ap()
    c128 = nc.alloc_sbuf_tensor("c128", [128, 1], F32).ap()
    trig = nc.alloc_sbuf_tensor("trig", [128, 2], F32).ap()
    smallv = nc.alloc_sbuf_tensor("smallv", [128, 16], F32).ap()
    AR_B = 162176
    ar = Arena(nc.alloc_sbuf_tensor("arena", [128, AR_B // 4], F32).ap(), AR_B)

    PS = nc.alloc_psum_tensor("PS", [128, 4096], F32).ap()
    bank = lambda i: PS[:, i * 512:(i + 1) * 512]
    bank_free = [[("sT", 0)] for _ in range(8)]

    MMKW = dict(skip_group_check=True)
    st = {}

    p.emit("vector", lambda e: e.memset(ones_c, 1.0))
    p.emit("vector", lambda e: e.memset(c128, 1.0 / 128.0))
    cV = p.emit("vector", lambda e: e.memset(trig, 2.0), inc="sV")
    p.emit("vector", lambda e: e.tensor_copy(trig[:, 1:2].bitcast(I32), trig[:, 0:1].bitcast(I32)),
           waits=[("sV", cV)], inc="sV")
    cID = p.dma("sync", "sID", ident, id_h)
    cG_bo = p.dma("gpsimd", "sBO", bo_rep, bo_h.unsqueeze(0).partition_broadcast(128))
    id_rdy = [("sID", cID)]

    ar.push()
    wo_all = ar.alloc(8 * D, F32)
    wo_v = wo_all.rearrange("p (c m) -> p c m", m=D)
    woe_f = ar.alloc(D, F32)
    wo_r = wo_h.rearrange("(c p) m -> p c m", p=128)
    half_done = []
    wo_gate = []
    for hf in range(2):
        cY = p.dma("sync", "sWO", wo_v, wo_r[:, hf * 8:(hf + 1) * 8, :], waits=wo_gate)
        cV = p.emit("vector", lambda e: e.tensor_add(
            wo_v[:, 0:4, :].rearrange("p a b -> p (a b)"), wo_v[:, 0:4, :].rearrange("p a b -> p (a b)"),
            wo_v[:, 4:8, :].rearrange("p a b -> p (a b)")), waits=[("sWO", cY)], inc="sV")
        cV = p.emit("vector", lambda e: e.tensor_add(
            wo_v[:, 0:2, :].rearrange("p a b -> p (a b)"), wo_v[:, 0:2, :].rearrange("p a b -> p (a b)"),
            wo_v[:, 2:4, :].rearrange("p a b -> p (a b)")), waits=[("sV", cV)], inc="sV")
        if hf == 0:
            cV = p.emit("vector", lambda e: e.tensor_add(woe_f, wo_v[:, 0, :], wo_v[:, 1, :]),
                        waits=[("sV", cV)], inc="sV")
        else:
            cV = p.emit("vector", lambda e: e.tensor_add(wo_v[:, 0, :], wo_v[:, 0, :], wo_v[:, 1, :]),
                        waits=[("sV", cV)], inc="sV")
            cV = p.emit("vector", lambda e: e.tensor_add(woe_f, woe_f, wo_v[:, 0, :]),
                        waits=[("sV", cV)], inc="sV")
        wo_gate = [("sV", cV)]
    cS = p.emit("scalar", lambda e: e.activation(woe_h, woe_f, AF.Identity),
                waits=[("sV", cV)], inc="sS")
    cV = p.emit("vector", lambda e: e.tensor_tensor(woe_l, woe_f, woe_h, op=ALU.subtract),
                waits=[("sS", cS)], inc="sV")
    st["woe"] = [("sV", cV), ("sS", cS)]
    ar.pop()
    ar.push()

    kTvT = ar.alloc(2 * S, F32)
    kT_f, vT_f = kTvT[:, 0:S], kTvT[:, S:2 * S]
    kv_b = ar.alloc(4 * S, BF16)
    kT_hi, kT_lo = kv_b[:, 0:S], kv_b[:, S:2 * S]
    vT_hi, vT_lo = kv_b[:, 2 * S:3 * S], kv_b[:, 3 * S:4 * S]
    ar.push()
    wkv_f = ar.alloc(2 * D, F32)
    wk_f = wkv_f[:, 0:D].rearrange("p (t d) -> p t d", d=128)
    wv_f = wkv_f[:, D:2 * D].rearrange("p (t d) -> p t d", d=128)
    wkv_b = ar.alloc(4 * D, BF16)
    wk_hi = wkv_b[:, 0 * D:1 * D].rearrange("p (t d) -> p t d", d=128)
    wk_lo = wkv_b[:, 1 * D:2 * D].rearrange("p (t d) -> p t d", d=128)
    wv_hi = wkv_b[:, 2 * D:3 * D].rearrange("p (t d) -> p t d", d=128)
    wv_lo = wkv_b[:, 3 * D:4 * D].rearrange("p (t d) -> p t d", d=128)
    bkv = ar.alloc(2, F32)
    cY1 = p.dma("sync", "sWKV", wk_f, wk_h.rearrange("(t p) d -> p t d", p=128),
                waits=st["woe"])
    cY2 = p.dma("sync", "sWKV", wv_f, wv_h.rearrange("(t p) d -> p t d", p=128),
                waits=[("sWKV", cY1)])
    cY3 = p.dma("sync", "sWKV", bkv[:, 0:1], bk_h.unsqueeze(1), waits=[("sWKV", cY2)])
    cY4 = p.dma("sync", "sWKV", bkv[:, 1:2], bv_h.unsqueeze(1), waits=[("sWKV", cY3)])
    cS1 = p.emit("scalar", lambda e: e.activation(
        wk_hi.rearrange("p a b -> p (a b)"), wkv_f[:, 0:D], AF.Identity), waits=[("sWKV", cY1)], inc="sS")
    cS2 = p.emit("scalar", lambda e: e.activation(
        wv_hi.rearrange("p a b -> p (a b)"), wkv_f[:, D:2 * D], AF.Identity), waits=[("sWKV", cY2)], inc="sS")
    cV1 = p.emit("vector", lambda e: e.tensor_tensor(
        wk_lo.rearrange("p a b -> p (a b)"), wkv_f[:, 0:D],
        wk_hi.rearrange("p a b -> p (a b)"), op=ALU.subtract), waits=[("sS", cS1)], inc="sV")
    cV2 = p.emit("vector", lambda e: e.tensor_tensor(
        wv_lo.rearrange("p a b -> p (a b)"), wkv_f[:, D:2 * D],
        wv_hi.rearrange("p a b -> p (a b)"), op=ALU.subtract), waits=[("sS", cS2)], inc="sV")
    st["wkv"] = [("sV", cV2), ("sWKV", cY4)]

    xt2 = [ar.alloc(D, F32)]
    xt2.append(xt2[0])
    sq_b = ar.alloc(D, F32)
    xn_b = ar.alloc(D, F32)
    xnT_hi = ar.alloc(16 * 512, BF16).rearrange("p (t q) -> p t q", q=512)
    xnT_lo = ar.alloc(16 * 512, BF16).rearrange("p (t q) -> p t q", q=512)
    xown_r = x_own.rearrange("(n p) m -> n p m", p=128)
    xoth_r = x_oth.rearrange("(n p) m -> n p m", p=128)

    def emit_ln(xt_ap, wa, sqb, xnb):
        rs = smallv
        c1 = p.emit("vector", lambda e: e.reduce_sum(rs[:, 0:1], xt_ap, axis=AX), waits=wa, inc="sV")
        c2 = p.emit("scalar", lambda e: e.activation(sqb, xt_ap, AF.Square, accum_out=rs[:, 2:3]),
                    waits=wa, inc="sS")
        c3 = p.emit("vector", lambda e: e.tensor_scalar(rs[:, 1:2], rs[:, 0:1], 1.0 / D, None, op0=ALU.mult),
                    waits=[("sV", c1)], inc="sV")
        c4 = p.emit("vector", lambda e: e.tensor_tensor(rs[:, 3:4], rs[:, 1:2], rs[:, 1:2], op=ALU.mult),
                    waits=[("sV", c3)], inc="sV")
        c5 = p.emit("vector", lambda e: e.tensor_scalar(rs[:, 4:5], rs[:, 2:3], 1.0 / D, 1e-5,
                                                        op0=ALU.mult, op1=ALU.add),
                    waits=[("sS", c2)], inc="sV")
        c6 = p.emit("vector", lambda e: e.tensor_tensor(rs[:, 5:6], rs[:, 4:5], rs[:, 3:4], op=ALU.subtract),
                    waits=[("sV", c5), ("sV", c4)], inc="sV")
        c7 = p.emit("scalar", lambda e: e.activation(rs[:, 6:7], rs[:, 5:6], AF.Sqrt),
                    waits=[("sV", c6)], inc="sS")
        c8 = p.emit("vector", lambda e: e.reciprocal(rs[:, 7:8], rs[:, 6:7]), waits=[("sS", c7)], inc="sV")
        c9 = p.emit("vector", lambda e: e.tensor_tensor(rs[:, 8:9], rs[:, 7:8], rs[:, 7:8], op=ALU.mult),
                    waits=[("sV", c8)], inc="sV")
        c10 = p.emit("vector", lambda e: e.tensor_tensor(rs[:, 9:10], rs[:, 8:9], rs[:, 5:6], op=ALU.mult),
                     waits=[("sV", c9)], inc="sV")
        c11 = p.emit("vector", lambda e: e.tensor_scalar(rs[:, 10:11], rs[:, 9:10], -0.5, 1.5,
                                                         op0=ALU.mult, op1=ALU.add),
                     waits=[("sV", c10)], inc="sV")
        c12 = p.emit("vector", lambda e: e.tensor_tensor(rs[:, 11:12], rs[:, 7:8], rs[:, 10:11], op=ALU.mult),
                     waits=[("sV", c11)], inc="sV")
        c15 = p.emit("vector", lambda e: e.tensor_tensor(rs[:, 13:14], rs[:, 1:2], rs[:, 11:12], op=ALU.mult),
                     waits=[("sV", c12)], inc="sV")
        c16 = p.emit("vector", lambda e: e.tensor_scalar(rs[:, 14:15], rs[:, 13:14], -1.0, None, op0=ALU.mult),
                     waits=[("sV", c15)], inc="sV")
        c17 = p.emit("scalar", lambda e: e.activation(xnb, xt_ap, AF.Identity,
                                                      bias=rs[:, 14:15], scale=rs[:, 11:12]),
                     waits=wa + [("sV", c16)], inc="sS")
        return c17


    xt_free = [[], []]
    xnT_free = [("sT", 0)]
    for w in range(8):
        for i in range(4):
            gtile = w * 4 + i
            src = xown_r[gtile] if gtile < 16 else xoth_r[gtile - 16]
            par = 0
            cy = p.dma("sync", "sXTA", xt2[par], src, waits=xt_free[par])
            cs_xn = emit_ln(xt2[par], [("sXTA", cy)], sq_b, xn_b)
            xt_free[par] = [("sS", cs_xn)]
            for r in range(4):
                bnk = 6 + (r % 2)
                ct = None
                for j in range(4):
                    mt = r * 4 + j
                    ct = p.emit("tensor",
                                lambda e, mt=mt, bnk=bnk, j=j: e.transpose(
                                    bank(bnk)[:, j * 128:(j + 1) * 128],
                                    xn_b[:, mt * 128:(mt + 1) * 128], ident),
                                waits=[("sS", cs_xn)] + bank_free[bnk] + xnT_free + id_rdy, inc="sT")
                dst_h = xnT_hi[:, r * 4:(r + 1) * 4, i * 128:(i + 1) * 128]
                dst_l = xnT_lo[:, r * 4:(r + 1) * 4, i * 128:(i + 1) * 128]
                cs = p.emit("scalar", lambda e, d=dst_h, bnk=bnk: e.activation(
                    d, bank(bnk).rearrange("p (a b) -> p a b", b=128), AF.Identity),
                    waits=[("sT", ct)], inc="sS")
                cv = p.emit("vector", lambda e, dl=dst_l, dh=dst_h, bnk=bnk: e.tensor_tensor(
                    dl, bank(bnk).rearrange("p (a b) -> p a b", b=128), dh, op=ALU.subtract),
                    waits=[("sS", cs)], inc="sV")
                bank_free[bnk] = [("sV", cv)]
        split_rdy = [("sV", p.cnt["sV"]), ("sS", p.cnt["sS"])]
        for ch, (whi, wlo, pb) in enumerate([(wk_hi, wk_lo, 4), (wv_hi, wv_lo, 5)]):
            ops = [(whi, xnT_hi), (whi, xnT_lo), (wlo, xnT_hi)]
            cm = None
            n_ops = 16 * 3
            k = 0
            for mt in range(16):
                for (la, hb) in ops:
                    first, last = (k == 0), (k == n_ops - 1)
                    extra = (st["wkv"] + split_rdy + bank_free[pb]) if first else []
                    cm = p.emit("tensor",
                                lambda e, la=la, hb=hb, mt=mt, pb=pb, first=first, last=last: e.matmul(
                                    bank(pb), la[:, mt, :], hb[:, mt, :],
                                    start=first, stop=last, **MMKW),
                                waits=extra, inc="sT")
                    k += 1
            dst = kT_f if ch == 0 else vT_f
            cs = p.emit("scalar",
                        lambda e, dst=dst, w=w, pb=pb, ch=ch: e.activation(
                            dst[:, w * 512:(w + 1) * 512], bank(pb), AF.Identity, bias=bkv[:, ch:ch + 1]),
                        waits=[("sT", cm)], inc="sS")
            bank_free[pb] = [("sS", cs)]
        xnT_free = [("sT", p.cnt["sT"])]
    kv_done = [("sS", p.cnt["sS"])]
    cS1 = p.emit("scalar", lambda e: e.activation(kT_hi, kT_f, AF.Identity), waits=kv_done, inc="sS")
    cV1 = p.emit("vector", lambda e: e.tensor_tensor(kT_lo, kT_f, kT_hi, op=ALU.subtract),
                 waits=[("sS", cS1)], inc="sV")
    cS2 = p.emit("scalar", lambda e: e.activation(vT_hi, vT_f, AF.Identity), waits=kv_done, inc="sS")
    cV2 = p.emit("vector", lambda e: e.tensor_tensor(vT_lo, vT_f, vT_hi, op=ALU.subtract),
                 waits=[("sS", cS2)], inc="sV")
    kv_split = [("sV", cV2)]
    ar.pop()
    ar.push()
    vtok_b = ar.alloc(2 * S, BF16)
    vtok_hi = vtok_b[:, 0:S].rearrange("p (t d) -> p t d", d=128)
    vtok_lo = vtok_b[:, S:2 * S].rearrange("p (t d) -> p t d", d=128)
    for r in range(8):
        bnk = 6 + (r % 2)
        ct = None
        for j in range(4):
            it = r * 4 + j
            ct = p.emit("tensor",
                        lambda e, it=it, bnk=bnk, j=j: e.transpose(
                            bank(bnk)[:, j * 128:(j + 1) * 128],
                            vT_f[:, it * 128:(it + 1) * 128], ident),
                        waits=kv_done + bank_free[bnk], inc="sT")
        dh = vtok_hi[:, r * 4:(r + 1) * 4, :]
        dl = vtok_lo[:, r * 4:(r + 1) * 4, :]
        cs = p.emit("scalar", lambda e, dh=dh, bnk=bnk: e.activation(
            dh, bank(bnk).rearrange("p (a b) -> p a b", b=128), AF.Identity),
            waits=[("sT", ct)], inc="sS")
        cv = p.emit("vector", lambda e, dl=dl, dh=dh, bnk=bnk: e.tensor_tensor(
            dl, bank(bnk).rearrange("p (a b) -> p a b", b=128), dh, op=ALU.subtract),
            waits=[("sS", cs)], inc="sV")
        bank_free[bnk] = [("sV", cv)]
    vtok_rdy = [("sV", p.cnt["sV"])]

    pq_f = [ar.alloc(512, F32) for _ in range(2)]
    pq_h = [ar.alloc(512, BF16) for _ in range(2)]
    pq_l = [ar.alloc(512, BF16) for _ in range(2)]
    rrow = ar.alloc(D, F32)
    pq_free = [[], []]
    for qc in range(4):
        ob = bank_free[2]
        rb = bank_free[3]
        c_o, c_r = None, None
        n_it = 32
        for it in range(n_it):
            par = it % 2
            ct = None
            sops = [(kT_hi, vT_hi), (kT_hi, vT_lo), (kT_lo, vT_hi)]
            for k, (la, rbs) in enumerate(sops):
                ct = p.emit("tensor",
                            lambda e, la=la, rbs=rbs, it=it, qc=qc, par=par,
                            first=(k == 0), last=(k == 2): e.matmul(
                                bank(par), la[:, it * 128:(it + 1) * 128],
                                rbs[:, qc * 512:(qc + 1) * 512],
                                start=first, stop=last, **MMKW),
                            waits=(kv_split + bank_free[par] + pq_free[par]) if k == 0 else [],
                            inc="sT")
            cs = p.emit("scalar", lambda e, par=par: e.activation(pq_f[par], bank(par), AF.Exp,
                                                                  scale=c128[:, 0:1]),
                        waits=[("sT", ct)], inc="sS")
            bank_free[par] = [("sS", cs)]
            cv1 = p.emit("vector", lambda e, par=par: e.tensor_copy(pq_h[par], pq_f[par]),
                         waits=[("sS", cs)], inc="sV")
            cv2 = p.emit("vector", lambda e, par=par: e.tensor_tensor(pq_l[par], pq_f[par], pq_h[par],
                                                                      op=ALU.subtract),
                         waits=[("sV", cv1)], inc="sV")
            aops = [(vtok_hi, pq_h), (vtok_hi, pq_l), (vtok_lo, pq_h)]
            for k, (la, hb) in enumerate(aops):
                c_o = p.emit("tensor",
                             lambda e, la=la, hb=hb, it=it, par=par,
                             first=(c_o is None), last=(it == n_it - 1 and k == 2): e.matmul(
                                 bank(2), la[:, it, :], hb[par], start=first, stop=last, **MMKW),
                             waits=(vtok_rdy + [("sV", cv2)] + (ob if c_o is None else [])),
                             inc="sT")
            for k, hb in enumerate((pq_h, pq_l)):
                c_r = p.emit("tensor",
                             lambda e, hb=hb, par=par, first=(c_r is None),
                             last=(it == n_it - 1 and k == 1): e.matmul(
                                 bank(3)[0:1, :], ones_c, hb[par], start=first, stop=last, **MMKW),
                             waits=(rb if c_r is None else []), inc="sT")
            pq_free[par] = [("sT", p.cnt["sT"])]
        cs = p.emit("scalar", lambda e, qc=qc: e.activation(oT_h[:, qc * 512:(qc + 1) * 512],
                                                            bank(2), AF.Identity),
                    waits=[("sT", c_o)], inc="sS")
        cv = p.emit("vector", lambda e, qc=qc: e.tensor_tensor(
            oT_l[:, qc * 512:(qc + 1) * 512], bank(2), oT_h[:, qc * 512:(qc + 1) * 512], op=ALU.subtract),
            waits=[("sS", cs)], inc="sV")
        bank_free[2] = [("sV", cv)]
        cvr = p.emit("vector", lambda e, qc=qc: e.tensor_copy(rrow[0:1, qc * 512:(qc + 1) * 512],
                                                              bank(3)[0:1, :]),
                     waits=[("sT", c_r)], inc="sV")
        bank_free[3] = [("sV", cvr)]
    oT_rdy = [("sS", p.cnt["sS"]), ("sV", p.cnt["sV"])]
    ct = None
    for j in range(16):
        ct = p.emit("tensor",
                    lambda e, j=j: e.transpose(bank(6)[:, j:j + 1], rrow[0:1, j * 128:(j + 1) * 128],
                                               ident[0:1, 0:1]),
                    waits=oT_rdy + bank_free[6], inc="sT")
    cv = p.emit("vector", lambda e: e.reciprocal(rinv, bank(6)[:, 0:16]), waits=[("sT", ct)], inc="sV")
    bank_free[6] = [("sV", cv)]
    st["rinv"] = [("sV", cv)]
    if "oT_h" in taps:
        p.dma_tap(taps["oT_h"], oT_h, waits=oT_rdy)
    if "oT_l" in taps:
        p.dma_tap(taps["oT_l"], oT_l, waits=oT_rdy)
    if "rinv" in taps:
        p.dma_tap(taps["rinv"], rinv, waits=st["rinv"])
    if "kT" in taps:
        p.dma_tap(taps["kT"], kT_f, waits=kv_done)
    if "vT" in taps:
        p.dma_tap(taps["vT"], vT_f, waits=kv_done)
    if "woe" in taps:
        p.dma_tap(taps["woe"], woe_f, waits=st["woe"])
    ar.pop()
    ar.pop()

    fenceA = p.fence()
    ar.push()
    selgu_s = ar.alloc(16 * 128, F32).rearrange("p (j q) -> p j q", q=128)
    seld_s = ar.alloc(32 * 128, F32).rearrange("p (j q) -> p j q", q=128)
    cSEL1 = p.dma("sync", "sID", selgu_s[0:64], selgu_h, waits=[("sID", p.cnt["sID"])] + fenceA)
    cSEL1b = p.dma("sync", "sID", selgu_s[64:128], selgu_h, waits=[("sID", cSEL1)])
    cSEL2 = p.dma("sync", "sID", seld_s, seld_h, waits=[("sID", cSEL1b)])
    sel_rdy = [("sID", cSEL2)]
    amax_buf = ar.alloc(4096, F32)
    r2n_buf = ar.alloc(4096, F32)
    s2T_gu = ar.alloc(II, F32)
    r2T_gu = ar.alloc(II, F32)
    s2T_d = s2T_gu[:, 0:2 * D]
    r2T_d = r2T_gu[:, 0:2 * D]
    amx2 = [r2n_buf[:, 0:2048], r2n_buf[:, 2048:4096]]
    raw2 = [ar.alloc(1024, F32) for _ in range(2)]
    wdq = [ar.alloc(1024, F32) for _ in range(2)]
    whi = [ar.alloc(1024, BF16) for _ in range(2)]
    wlo = [ar.alloc(1024, BF16) for _ in range(2)]
    q8w = [w.bitcast(FP8)[:, 0:1024] for w in wlo]
    rep_s_ps = [PS[:, 0:1024], PS[:, 4 * 512:4 * 512 + 1024]]
    rep_r_ps = [PS[:, 2 * 512:2 * 512 + 1024], PS[:, 6 * 512:6 * 512 + 1024]]

    gn_r = gn_h.rearrange("(n p) m -> n p m", p=128)
    un_r = un_h.rearrange("(n p) m -> n p m", p=128)
    dn_r = dn_h.rearrange("(n p) m -> n p m", p=128)
    gt_r = gt_h.rearrange("(n p) m -> n p m", p=128)
    ut_r = ut_h.rearrange("(n p) m -> n p m", p=128)
    dt_r = dt_h.rearrange("(n p) m -> n p m", p=128)
    scg_r, scu_r, scd_r = sc_g, sc_u, sc_d

    rawgate = [list(fenceA), list(fenceA)]
    repgate_r = [list(fenceA), list(fenceA)]
    repgate_s = [list(fenceA), list(fenceA)]
    pipe_q8 = [[], []]
    pipe_wdq = [[], []]
    pipe_whi = [[], []]
    pipe_wlo = [[], []]

    def quant_weight(nat_r, trn_r, sc_r, widx):
        is_d = (widx == 2)
        amax_v = amax_buf.rearrange("p (t b) -> p t b", b=(256 if is_d else 64))
        am_gate = [("sT", p.cnt["sT"])]
        last_am = 0
        for j in range(16 if is_d else 64):
            par = j % 2
            if not is_d:
                cy = p.dma("sync", f"sW{par}", amx2[par], nat_r[j], waits=rawgate[par])
                cv = p.emit("vector",
                            lambda e, j=j, par=par: e.reduce_max(
                                amax_v[:, j, :], amx2[par].rearrange("p (b k) -> p b k", k=32),
                                axis=AX, apply_absolute_value=True),
                            waits=[(f"sW{par}", cy)] + am_gate, inc="sV")
                rawgate[par] = [("sV", cv)]
            else:
                for hh in range(4):
                    par = (j * 4 + hh) % 2
                    cy = p.dma("sync", f"sW{par}", amx2[par], nat_r[j][:, hh * 2048:(hh + 1) * 2048],
                               waits=rawgate[par])
                    cv = p.emit("vector",
                                lambda e, j=j, hh=hh, par=par: e.reduce_max(
                                    amax_v[:, j, hh * 64:(hh + 1) * 64],
                                    amx2[par].rearrange("p (b k) -> p b k", k=32),
                                    axis=AX, apply_absolute_value=True),
                                waits=[(f"sW{par}", cy)] + am_gate, inc="sV")
                    rawgate[par] = [("sV", cv)]
            last_am = p.cnt["sV"]
        cv = p.emit("vector",
                    lambda e: e.tensor_scalar(amax_buf, amax_buf, 1.0 / 224.0, 2e-30,
                                              op0=ALU.mult, op1=ALU.max),
                    waits=[("sV", last_am)], inc="sV")
        cv = p.emit("vector", lambda e: e.reciprocal(r2n_buf, amax_buf),
                    waits=[("sV", cv)], inc="sV")
        r2n_v = r2n_buf.rearrange("p (t b) -> p t b", b=(256 if is_d else 64))
        tgate = [("sT", p.cnt["sT"])]
        if not is_d:
            prow = 64 * widx
            for (tab_v, dstT) in ((amax_v, s2T_gu), (r2n_v, r2T_gu)):
                for r in range(16):
                    bnk = r % 2
                    ct = None
                    for j in range(4):
                        tj = r * 4 + j
                        ct = p.emit("tensor",
                                    lambda e, tj=tj, bnk=bnk, j=j, tab_v=tab_v: e.transpose(
                                        bank(bnk)[0:64, j * 128:(j + 1) * 128], tab_v[:, tj, :], ident),
                                    waits=[("sV", cv)] + bank_free[bnk] + id_rdy, inc="sT")
                    cvv = p.emit("vector",
                                 lambda e, r=r, bnk=bnk, prow=prow, dstT=dstT: e.tensor_copy(
                                     dstT[prow:prow + 64, r * 512:(r + 1) * 512], bank(bnk)[0:64, :]),
                                 waits=[("sT", ct)] + tgate, inc="sV")
                    bank_free[bnk] = [("sV", cvv)]
        else:
            for (tab_v, dstT) in ((amax_v, s2T_d), (r2n_v, r2T_d)):
                for j in range(16):
                    bnk = j % 2
                    ct0 = p.emit("tensor",
                                 lambda e, j=j, bnk=bnk, tab_v=tab_v: e.transpose(
                                     bank(bnk)[:, 0:128], tab_v[:, j, 0:128], ident),
                                 waits=[("sV", cv)] + bank_free[bnk] + id_rdy, inc="sT")
                    ct1 = p.emit("tensor",
                                 lambda e, j=j, bnk=bnk, tab_v=tab_v: e.transpose(
                                     bank(bnk)[:, 128:256], tab_v[:, j, 128:256], ident),
                                 waits=[], inc="sT")
                    cv1 = p.emit("vector",
                                 lambda e, j=j, bnk=bnk, dstT=dstT: e.tensor_copy(
                                     dstT[:, j * 128:(j + 1) * 128], bank(bnk)[:, 0:128]),
                                 waits=[("sT", ct1)] + tgate, inc="sV")
                    cv2 = p.emit("vector",
                                 lambda e, j=j, bnk=bnk, dstT=dstT: e.tensor_copy(
                                     dstT[:, D + j * 128: D + (j + 1) * 128], bank(bnk)[:, 128:256]),
                                 waits=[("sT", ct1)], inc="sV")
                    bank_free[bnk] = [("sV", cv2)]
        tbl = [("sV", p.cnt["sV"])]

        wq = 1024
        nh = 2 if is_d else 8
        for j in range(64 if is_d else 16):
            for hh in range(nh):
                par = (j * nh + hh) % 2
                src_tile = trn_r[j][:, hh * wq:(hh + 1) * wq]
                cy = p.dma("sync", f"sW{par}", raw2[par], src_tile, waits=rawgate[par] + tbl)
                if not is_d:
                    prow = 64 * widx
                    cslice = slice(hh * 1024, (hh + 1) * 1024)
                    src_s = s2T_gu[prow:prow + 64, cslice]
                    src_r = r2T_gu[prow:prow + 64, cslice]
                else:
                    q_ = (j * 4) // 128
                    cslice = slice(q_ * D + hh * 1024, q_ * D + (hh + 1) * 1024)
                    src_s = s2T_gu[0:128, cslice]
                    src_r = r2T_gu[0:128, cslice]
                if not is_d:
                    sel_ap = selgu_s[64 * widx:64 * widx + 64, j, :]
                else:
                    sel_ap = seld_s[:, j % 32, :]
                cms = None
                for half in range(2):
                    cms = p.emit("tensor",
                                 lambda e, par=par, half=half, src_s=src_s, sel_ap=sel_ap: e.matmul(
                                     rep_s_ps[par][:, half * 512:(half + 1) * 512], sel_ap,
                                     src_s[:, half * 512:(half + 1) * 512],
                                     start=True, stop=True, **MMKW),
                                 waits=tbl + repgate_s[par] + sel_rdy, inc="sT")
                    cms = p.emit("tensor",
                                 lambda e, par=par, half=half, src_r=src_r, sel_ap=sel_ap: e.matmul(
                                     rep_r_ps[par][:, half * 512:(half + 1) * 512], sel_ap,
                                     src_r[:, half * 512:(half + 1) * 512],
                                     start=True, stop=True, **MMKW),
                                 waits=[], inc="sT")
                cq = p.emit("vector",
                            lambda e, par=par: e.tensor_tensor(
                                q8w[par], raw2[par], rep_r_ps[par], op=ALU.mult),
                            waits=[(f"sW{par}", cy), ("sT", cms)] + pipe_q8[par] + pipe_wlo[par],
                            inc="sV")
                rawgate[par] = [("sV", cq)]
                cd = p.emit("vector",
                            lambda e, par=par: e.tensor_tensor(
                                wdq[par], q8w[par], rep_s_ps[par], op=ALU.mult),
                            waits=[("sV", cq)] + pipe_wdq[par], inc="sV")
                repgate_s[par] = [("sV", cd)]
                pipe_q8[par] = [("sV", cd)]
                ch = p.emit("scalar",
                            lambda e, par=par: e.activation(whi[par], wdq[par], AF.Identity),
                            waits=[("sV", cd)] + pipe_whi[par], inc="sS")
                cl = p.emit("gpsimd",
                            lambda e, par=par: e.tensor_tensor(
                                wlo[par], wdq[par], whi[par], op=ALU.subtract),
                            waits=[("sS", ch)], inc="sP")
                pipe_wdq[par] = [("sP", cl)]
                dsth = sc_r[j, 0, :, hh * wq:(hh + 1) * wq]
                dstl = sc_r[j, 1, :, hh * wq:(hh + 1) * wq]
                cw1 = p.dma("scalar", f"sWH{par}", dsth, whi[par], waits=[("sP", cl)])
                cw2 = p.dma("scalar", f"sWL{par}", dstl, wlo[par], waits=[("sP", cl)])
                pipe_whi[par] = [(f"sWH{par}", cw1)]
                pipe_wlo[par] = [(f"sWL{par}", cw2)]
        return [("sWH0", p.cnt["sWH0"]), ("sWH1", p.cnt["sWH1"]),
                ("sWL0", p.cnt["sWL0"]), ("sWL1", p.cnt["sWL1"])]

    wg_done = quant_weight(gn_r, gt_r, scg_r, 0)
    wu_done = quant_weight(un_r, ut_r, scu_r, 1)
    wd_done = quant_weight(dn_r, dt_r, scd_r, 2)
    for _b in range(8):
        bank_free[_b] = [("sV", p.cnt["sV"])]
    ar.pop()

    fenceW = p.fence()
    ar.push()
    xhT_h = ar.alloc(16 * 512, BF16).rearrange("p (t q) -> p t q", q=512)
    xhT_l = ar.alloc(16 * 512, BF16).rearrange("p (t q) -> p t q", q=512)
    acc = ar.alloc(4 * D, F32).rearrange("p (t m) -> p t m", m=D)
    hT_h = ar.alloc(16 * 512, BF16).rearrange("p (t q) -> p t q", q=512)
    hT_l = ar.alloc(16 * 512, BF16).rearrange("p (t q) -> p t q", q=512)
    gu_buf = [[ar.alloc(2 * 2 * 512, BF16).rearrange("p (t q) -> p t q", q=512)
               for _ in range(2)] for _ in range(2)]
    d_buf = [ar.alloc(2 * 512, BF16).rearrange("p (h q) -> p h q", h=2) for _ in range(2)]
    xt_b = ar.alloc(D, F32)
    q8x = ar.alloc(D, FP8)
    xq_b = ar.alloc(D, F32)
    qs = ar.alloc(3 * 64, F32)
    h_b = ar.alloc(512, F32)
    sg_b = ar.alloc(512, F32)
    hq_b = ar.alloc(512, F32)
    q8h = ar.alloc(512, FP8)
    hs = ar.alloc(3 * 16, F32)
    fin = ar.alloc(D, F32)
    res_b = ar.alloc(D, F32)
    sq2_b = fin
    xn2_b = res_b

    out_r = out_h.rearrange("(n p) m -> n p m", p=128)
    gu_free = [list(fenceW), list(fenceW)]
    d_free = [list(fenceW), list(fenceW)]
    xt_gate = list(fenceW)
    fin_gate = list(fenceW)
    res_gate = list(fenceW)
    acc_gate = list(fenceW)
    sg_gate = []
    xq_readers = []

    for tc in range(4):
        for i in range(4):
            gt = tc * 4 + i
            cy = p.dma("sync", "sXB", xt_b, xown_r[gt], waits=xt_gate)
            cs_xn = emit_ln(xt_b, [("sXB", cy)], sq2_b, xn2_b)
            xt_gate = [("sS", cs_xn)]
            qa = p.emit("vector",
                        lambda e: e.reduce_max(qs[:, 0:64], xn2_b.rearrange("p (b k) -> p b k", k=32),
                                               axis=AX, apply_absolute_value=True),
                        waits=[("sS", cs_xn)], inc="sV")
            qb = p.emit("vector",
                        lambda e: e.tensor_scalar(qs[:, 64:128], qs[:, 0:64], 1.0 / 224.0, 2e-30,
                                                  op0=ALU.mult, op1=ALU.max),
                        waits=[("sV", qa)], inc="sV")
            qc_ = p.emit("vector", lambda e: e.reciprocal(qs[:, 128:192], qs[:, 64:128]),
                         waits=[("sV", qb)], inc="sV")
            qd = p.emit("vector",
                        lambda e: e.tensor_tensor(
                            q8x.rearrange("p (b k) -> p b k", k=32),
                            xn2_b.rearrange("p (b k) -> p b k", k=32),
                            qs[:, 128:192].unsqueeze(2).broadcast_to([128, 64, 32]), op=ALU.mult),
                        waits=[("sV", qc_)], inc="sV")
            qe = p.emit("vector",
                        lambda e: e.tensor_tensor(
                            xq_b.rearrange("p (b k) -> p b k", k=32),
                            q8x.rearrange("p (b k) -> p b k", k=32),
                            qs[:, 64:128].unsqueeze(2).broadcast_to([128, 64, 32]), op=ALU.mult),
                        waits=[("sV", qd)], inc="sV")
            for r in range(4):
                bnk = r % 2
                ct = None
                for jj in range(4):
                    mt = r * 4 + jj
                    ct = p.emit("tensor",
                                lambda e, mt=mt, bnk=bnk, jj=jj: e.transpose(
                                    bank(bnk)[:, jj * 128:(jj + 1) * 128],
                                    xq_b[:, mt * 128:(mt + 1) * 128], ident),
                                waits=[("sV", qe)] + bank_free[bnk] + xq_readers, inc="sT")
                dh = xhT_h[:, r * 4:(r + 1) * 4, i * 128:(i + 1) * 128]
                dl = xhT_l[:, r * 4:(r + 1) * 4, i * 128:(i + 1) * 128]
                cs = p.emit("scalar", lambda e, dh=dh, bnk=bnk: e.activation(
                    dh, bank(bnk).rearrange("p (a b) -> p a b", b=128), AF.Identity),
                    waits=[("sT", ct)], inc="sS")
                cvv = p.emit("vector", lambda e, dl=dl, dh=dh, bnk=bnk: e.tensor_tensor(
                    dl, bank(bnk).rearrange("p (a b) -> p a b", b=128), dh, op=ALU.subtract),
                    waits=[("sS", cs)], inc="sV")
                bank_free[bnk] = [("sV", cvv)]
        xq_rdy = [("sV", p.cnt["sV"]), ("sS", p.cnt["sS"])]
        xq_readers = []
        if "xhT_h" in taps and tc == 0:
            p.dma_tap(taps["xhT_h"], xhT_h.rearrange("p t q -> p (t q)"), waits=xq_rdy)
            xq_readers = xq_readers + [("sTAP", p.cnt["sTAP"])]

        for g in range(4):
            hT_last = []
            for jic in range(4):
                ic = g * 4 + jic
                mm_gate_last = [None] * 4
                mm_up_last = [None] * 4
                for qtr in range(8):
                    par = qtr % 2
                    cyg = p.dma(
                        "sync", f"sGU{par}g", gu_buf[par][0],
                        scg_r[qtr * 2:(qtr + 1) * 2, :, :, ic * 512:(ic + 1) * 512]
                        .rearrange("a h p q -> p (a h) q"),
                        waits=gu_free[par] + wg_done + xq_rdy)
                    cyu = p.dma(
                        "sync", f"sGU{par}u", gu_buf[par][1],
                        scu_r[qtr * 2:(qtr + 1) * 2, :, :, ic * 512:(ic + 1) * 512]
                        .rearrange("a h p q -> p (a h) q"),
                        waits=gu_free[par] + wu_done)
                    ld_done2 = [(f"sGU{par}g", cyg), (f"sGU{par}u", cyu)]
                    for mtl in range(2):
                        mt = qtr * 2 + mtl
                        for tsub in range(4):
                            gb, ub = tsub, 4 + tsub
                            ops6 = ((xhT_h, gu_buf[par][0], 0, gb, mm_gate_last, False),
                                    (xhT_h, gu_buf[par][0], 1, gb, mm_gate_last, False),
                                    (xhT_l, gu_buf[par][0], 0, gb, mm_gate_last, True),
                                    (xhT_h, gu_buf[par][1], 0, ub, mm_up_last, False),
                                    (xhT_h, gu_buf[par][1], 1, ub, mm_up_last, False),
                                    (xhT_l, gu_buf[par][1], 0, ub, mm_up_last, True))
                            for (la, wvv, hl, pb, rec, is3) in ops6:
                                first = rec[tsub] is None
                                last = (qtr == 7 and mtl == 1 and is3)
                                ws = list(ld_done2) + xq_rdy
                                if first:
                                    ws += bank_free[pb]
                                rec[tsub] = p.emit(
                                    "tensor",
                                    lambda e, la=la, wvv=wvv, hl=hl, pb=pb, mt=mt, mtl=mtl, tsub=tsub,
                                    first=first, last=last: e.matmul(
                                        bank(pb), la[:, mt, tsub * 128:(tsub + 1) * 128],
                                        wvv[:, mtl * 2 + hl, :], start=first, stop=last, **MMKW),
                                    waits=ws, inc="sT")
                    gu_free[par] = [("sT", p.cnt["sT"])]
                for tsub in range(4):
                    gb, ub = tsub, 4 + tsub
                    e1 = p.emit("scalar", lambda e, gb=gb: e.activation(sg_b, bank(gb), AF.Silu),
                                waits=[("sT", mm_gate_last[tsub])] + sg_gate, inc="sS")
                    e2 = p.emit("vector", lambda e, ub=ub: e.tensor_tensor(h_b, bank(ub), sg_b, op=ALU.mult),
                                waits=[("sS", e1), ("sT", mm_up_last[tsub])], inc="sV")
                    sg_gate = [("sV", e2)]
                    bank_free[ub] = [("sV", e2)]
                    q1 = p.emit("vector",
                                lambda e: e.reduce_max(hs[:, 0:16], h_b.rearrange("p (b k) -> p b k", k=32),
                                                       axis=AX, apply_absolute_value=True),
                                waits=[("sV", e2)], inc="sV")
                    q2 = p.emit("vector",
                                lambda e: e.tensor_scalar(hs[:, 16:32], hs[:, 0:16], 1.0 / 224.0, 2e-30,
                                                          op0=ALU.mult, op1=ALU.max),
                                waits=[("sV", q1)], inc="sV")
                    q3 = p.emit("vector", lambda e: e.reciprocal(hs[:, 32:48], hs[:, 16:32]),
                                waits=[("sV", q2)], inc="sV")
                    q4 = p.emit("vector",
                                lambda e: e.tensor_tensor(
                                    q8h.rearrange("p (b k) -> p b k", k=32),
                                    h_b.rearrange("p (b k) -> p b k", k=32),
                                    hs[:, 32:48].unsqueeze(2).broadcast_to([128, 16, 32]), op=ALU.mult),
                                waits=[("sV", q3)], inc="sV")
                    q5 = p.emit("vector",
                                lambda e: e.tensor_tensor(
                                    hq_b.rearrange("p (b k) -> p b k", k=32),
                                    q8h.rearrange("p (b k) -> p b k", k=32),
                                    hs[:, 16:32].unsqueeze(2).broadcast_to([128, 16, 32]), op=ALU.mult),
                                waits=[("sV", q4)], inc="sV")
                    ct = None
                    for jj in range(4):
                        ct = p.emit("tensor",
                                    lambda e, jj=jj, gb=gb: e.transpose(
                                        bank(gb)[:, jj * 128:(jj + 1) * 128],
                                        hq_b[:, jj * 128:(jj + 1) * 128], ident),
                                    waits=[("sV", q5)] + bank_free[gb], inc="sT")
                    it0 = jic * 4
                    dh = hT_h[:, it0:it0 + 4, tsub * 128:(tsub + 1) * 128]
                    dl = hT_l[:, it0:it0 + 4, tsub * 128:(tsub + 1) * 128]
                    cs = p.emit("scalar", lambda e, dh=dh, gb=gb: e.activation(
                        dh, bank(gb).rearrange("p (a b) -> p a b", b=128), AF.Identity),
                        waits=[("sT", ct)], inc="sS")
                    cvv = p.emit("vector", lambda e, dl=dl, dh=dh, gb=gb: e.tensor_tensor(
                        dl, bank(gb).rearrange("p (a b) -> p a b", b=128), dh, op=ALU.subtract),
                        waits=[("sS", cs)], inc="sV")
                    bank_free[gb] = [("sV", cvv)]
            hT_all = [("sV", p.cnt["sV"]), ("sS", p.cnt["sS"])]
            for mch in range(4):
                dmm_last = [None] * 4
                for it in range(16):
                    par = it % 2
                    cy1 = p.dma("sync", f"sD{par}", d_buf[par],
                                scd_r[g * 16 + it, :, :, mch * 512:(mch + 1) * 512]
                                .rearrange("h p q -> p h q"),
                                waits=d_free[par] + wd_done)
                    for tsub in range(4):
                        pb = 4 + tsub
                        dops = ((hT_h, 0, False), (hT_h, 1, False), (hT_l, 0, True))
                        for (lah, hl, is3) in dops:
                            first = dmm_last[tsub] is None
                            last = (it == 15 and is3)
                            ws = [(f"sD{par}", cy1)] + hT_all
                            if first:
                                ws += bank_free[pb]
                            dmm_last[tsub] = p.emit(
                                "tensor",
                                lambda e, lah=lah, hl=hl, par=par, it=it, tsub=tsub, pb=pb,
                                first=first, last=last: e.matmul(
                                    bank(pb), lah[:, it, tsub * 128:(tsub + 1) * 128], d_buf[par][:, hl, :],
                                    start=first, stop=last, **MMKW),
                                waits=ws, inc="sT")
                    d_free[par] = [("sT", p.cnt["sT"])]
                for tsub in range(4):
                    pb = 4 + tsub
                    if g == 0:
                        cvx = p.emit("vector",
                                     lambda e, tsub=tsub, mch=mch, pb=pb: e.tensor_copy(
                                         acc[:, tsub, mch * 512:(mch + 1) * 512], bank(pb)),
                                     waits=[("sT", dmm_last[tsub])] + acc_gate, inc="sV")
                    else:
                        cvx = p.emit("vector",
                                     lambda e, tsub=tsub, mch=mch, pb=pb: e.tensor_tensor(
                                         acc[:, tsub, mch * 512:(mch + 1) * 512],
                                         acc[:, tsub, mch * 512:(mch + 1) * 512],
                                         bank(pb), op=ALU.add),
                                     waits=[("sT", dmm_last[tsub])], inc="sV")
                    bank_free[pb] = [("sV", cvx)]
        acc_done = [("sV", p.cnt["sV"])]
        if "acc" in taps and tc == 0:
            p.dma_tap(taps["acc"], acc.rearrange("p t m -> p (t m)"), waits=acc_done)
            acc_gate = acc_gate + [("sTAP", p.cnt["sTAP"])]

        for tsub in range(4):
            j = tc * 4 + tsub
            cmt = None
            for mch in range(4):
                pb = mch
                wops = ((oT_h, woe_h), (oT_h, woe_l), (oT_l, woe_h))
                for k, (la, rb2) in enumerate(wops):
                    ws = []
                    if k == 0:
                        ws = list(bank_free[pb]) + st["woe"] + oT_rdy
                    cmt = p.emit("tensor",
                                 lambda e, la=la, rb2=rb2, j=j, mch=mch, pb=pb,
                                 first=(k == 0), last=(k == 2): e.matmul(
                                     bank(pb), la[:, j * 128:(j + 1) * 128],
                                     rb2[:, mch * 512:(mch + 1) * 512],
                                     start=first, stop=last, **MMKW),
                                 waits=ws, inc="sT")
            cy = p.dma("scalar", "sFIN", fin, xown_r[j], waits=fin_gate)
            cs = p.emit("scalar",
                        lambda e, j=j: e.activation(res_b, PS[:, 0:2048], AF.Identity,
                                                    scale=rinv[:, j:j + 1]),
                        waits=[("sT", cmt)] + st["rinv"] + res_gate, inc="sS")
            for b_ in range(4):
                bank_free[b_] = [("sS", cs)]
            cv1 = p.emit("vector", lambda e: e.tensor_add(res_b, res_b, bo_rep),
                         waits=[("sS", cs), ("sBO", cG_bo)], inc="sV")
            cv2 = p.emit("vector", lambda e: e.tensor_add(res_b, res_b, fin),
                         waits=[("sV", cv1), ("sFIN", cy)], inc="sV")
            cv3 = p.emit("vector",
                         lambda e, tsub=tsub: e.tensor_add(res_b, res_b, acc[:, tsub, :]),
                         waits=[("sV", cv2)] + acc_done, inc="sV")
            cg = p.dma("gpsimd", "sOUT", out_r[j], res_b, waits=[("sV", cv3)])
            fin_gate = [("sV", cv2)]
            res_gate = [("sOUT", cg)]
        acc_gate = [("sOUT", p.cnt["sOUT"])]
        xq_readers = [("sT", p.cnt["sT"]), ("sTAP", p.cnt["sTAP"])]
        xt_gate = xt_gate + [("sOUT", p.cnt["sOUT"]), ("sFIN", p.cnt["sFIN"]),
                             ("sV", p.cnt["sV"])]

    p.emit("gpsimd", lambda e: e.nop(), waits=[("sOUT", p.cnt["sOUT"])])

    with nc.Block() as blk:
        p.build(blk)
    return nc


def kernel(**inputs):
    x = np.ascontiguousarray(inputs["L0_attn_q"], dtype=np.float32)
    wk = np.ascontiguousarray(inputs["L0_attn_Wk"], dtype=np.float32)
    wv = np.ascontiguousarray(inputs["L0_attn_Wv"], dtype=np.float32)
    bk = np.ascontiguousarray(inputs["L0_attn_bk"], dtype=np.float32)
    bv = np.ascontiguousarray(inputs["L0_attn_bv"], dtype=np.float32)
    wo = np.ascontiguousarray(inputs["L0_attn_Wo"], dtype=np.float32)
    bo = np.ascontiguousarray(inputs["L0_attn_bo"], dtype=np.float32)
    gw = np.ascontiguousarray(inputs["L0_ffn_gate_w"], dtype=np.float32)
    uw = np.ascontiguousarray(inputs["L0_ffn_up_w"], dtype=np.float32)
    dw = np.ascontiguousarray(inputs["L0_ffn_down_w"], dtype=np.float32)
    gwT = np.ascontiguousarray(gw.T)
    uwT = np.ascontiguousarray(uw.T)
    dwT = np.ascontiguousarray(dw.T)
    ident = np.eye(128, dtype=np.float32)
    selgu = np.zeros((64, 16, 128), dtype=np.float32)
    for j in range(16):
        for pp in range(128):
            selgu[(j * 4 + pp // 32) % 64, j, pp] = 1.0
    seld = np.zeros((128, 32, 128), dtype=np.float32)
    for m in range(32):
        for pp in range(128):
            seld[((m * 4) % 128) + pp // 32, m, pp] = 1.0

    nc = build_program()
    in_maps = []
    for c in range(NCORE):
        b, h = c // 2, c % 2
        in_maps.append({
            "x_own": x[b, h * TOK:(h + 1) * TOK],
            "x_oth": x[b, (1 - h) * TOK:(2 - h) * TOK],
            "wk": wk, "wv": wv, "bk": bk, "bv": bv, "wo": wo, "bo": bo,
            "g_nat": gw, "u_nat": uw, "d_nat": dw,
            "g_t": gwT, "u_t": uwT, "d_t": dwT,
            "ident": ident, "selgu": selgu, "seld": seld,
        })
    res = run_bass_kernel_spmd(nc, in_maps, core_ids=list(range(NCORE)))
    out = np.empty((4, S, D), dtype=np.float32)
    for c in range(NCORE):
        b, h = c // 2, c % 2
        out[b, h * TOK:(h + 1) * TOK] = res.results[c]["out"]
    return out
